# version 13
# baseline (speedup 1.0000x reference)
import numpy as np
import ml_dtypes
import concourse.bass as bass
import concourse.mybir as mybir
from concourse.bass_utils import run_bass_kernel_spmd
from contextlib import ExitStack

F32 = mybir.dt.float32
BF16 = mybir.dt.bfloat16
U32 = mybir.dt.uint32
AF = mybir.ActivationFunctionType
ALU = mybir.AluOpType
AX = mybir.AxisListType.X

B, L, DM, DI, NS, RK = 4, 4096, 1024, 2048, 16, 64
TC = 256
NCH = L // TC
HALF = L // 2
T2C = 256
N2CH = HALF // T2C
SEG = TC + 1
EPS = 1e-5
FF = 4 * DM
DBG = False


class SemCtr:
    def __init__(self, nc, es, name):
        self.h = es.enter_context(nc.semaphore(name))
        self.n = 0

    def inc(self, instr, k=1):
        instr.then_inc(self.h, k)
        self.n += k
        return self.n


def build():
    nc = bass.Bass("TRN2", target_bir_lowering=False)
    ap = lambda t: t.ap()

    hid_in = ap(nc.dram_tensor("hid", [L, DM], F32, kind="ExternalInput"))
    hid2_in = ap(nc.dram_tensor("hid2", [HALF, DM], F32, kind="ExternalInput"))
    inw_in = ap(nc.dram_tensor("inw", [2 * DI, DM], F32, kind="ExternalInput"))
    convw_in = ap(nc.dram_tensor("convw", [DI, 4], F32, kind="ExternalInput"))
    convb_in = ap(nc.dram_tensor("convb", [DI], F32, kind="ExternalInput"))
    xw_in = ap(nc.dram_tensor("xw", [96, DI], F32, kind="ExternalInput"))
    dtpw_in = ap(nc.dram_tensor("dtpw", [DI, RK], F32, kind="ExternalInput"))
    dtpb_in = ap(nc.dram_tensor("dtpb", [DI], F32, kind="ExternalInput"))
    alog_in = ap(nc.dram_tensor("alog", [DI, NS], F32, kind="ExternalInput"))
    dvec_in = ap(nc.dram_tensor("dvec", [DI], F32, kind="ExternalInput"))
    moutw_in = ap(nc.dram_tensor("moutw", [DM, DI], F32, kind="ExternalInput"))
    normw_in = ap(nc.dram_tensor("normw", [DM], F32, kind="ExternalInput"))
    normb_in = ap(nc.dram_tensor("normb", [DM], F32, kind="ExternalInput"))
    trackw_in = ap(nc.dram_tensor("trackw", [1, 2], F32, kind="ExternalInput"))
    projw_in = ap(nc.dram_tensor("projw", [DM, DM], F32, kind="ExternalInput"))
    projb_in = ap(nc.dram_tensor("projb", [DM], F32, kind="ExternalInput"))
    w1_in = ap(nc.dram_tensor("w1", [FF, DM], F32, kind="ExternalInput"))
    b1_in = ap(nc.dram_tensor("b1", [FF], F32, kind="ExternalInput"))
    w2_in = ap(nc.dram_tensor("w2", [DM, FF], F32, kind="ExternalInput"))
    b2_in = ap(nc.dram_tensor("b2", [DM], F32, kind="ExternalInput"))
    eyeb_in = ap(nc.dram_tensor("eyeb", [128, 128], BF16, kind="ExternalInput"))
    eyef_in = ap(nc.dram_tensor("eyef", [128, 128], F32, kind="ExternalInput"))
    part_in = ap(nc.dram_tensor("part", [1, 2], U32, kind="ExternalInput"))
    selm_in = ap(nc.dram_tensor("selm", [32, 4096], BF16, kind="ExternalInput"))

    out_d = ap(nc.dram_tensor("out", [HALF, DM], F32, kind="ExternalOutput"))
    if DBG:
        dbg_shx = ap(nc.dram_tensor("dbg_shx", [DM, L], F32, kind="ExternalOutput"))
        dbg_hnfm = ap(nc.dram_tensor("dbg_hnfm", [128, 8 * TC], BF16, kind="ExternalOutput"))
        dbg_u = ap(nc.dram_tensor("dbg_u", [128, 16 * TC], BF16, kind="ExternalOutput"))
        dbg_sz = ap(nc.dram_tensor("dbg_sz", [128, 16 * TC], BF16, kind="ExternalOutput"))
        dbg_dtin = ap(nc.dram_tensor("dbg_dtin", [64, TC], BF16, kind="ExternalOutput"))
        dbg_bbc = ap(nc.dram_tensor("dbg_bbc", [128, 16 * TC], BF16, kind="ExternalOutput"))
        dbg_cbc = ap(nc.dram_tensor("dbg_cbc", [128, 16 * TC], BF16, kind="ExternalOutput"))
        dbg_a = ap(nc.dram_tensor("dbg_a", [128, NS * SEG], F32, kind="ExternalOutput"))
        dbg_h = ap(nc.dram_tensor("dbg_h", [128, NS * SEG], BF16, kind="ExternalOutput"))
        dbg_y = ap(nc.dram_tensor("dbg_y", [128, TC], F32, kind="ExternalOutput"))
        dbg_dtg = ap(nc.dram_tensor("dbg_dtg", [128, TC], F32, kind="ExternalOutput"))
        dbg_mo = ap(nc.dram_tensor("dbg_mo", [128, 8 * TC], F32, kind="ExternalOutput"))
        dbg_hid = ap(nc.dram_tensor("dbg_hid", [128, 2 * 1024], F32, kind="ExternalOutput"))
        dbg_hnbf = ap(nc.dram_tensor("dbg_hnbf", [128, 1024], BF16, kind="ExternalOutput"))
        dbg_nwb = ap(nc.dram_tensor("dbg_nwb", [128, 1024], F32, kind="ExternalOutput"))
        dbg_nbb = ap(nc.dram_tensor("dbg_nbb", [128, 1024], F32, kind="ExternalOutput"))
        dbg_inwt = ap(nc.dram_tensor("dbg_inwt", [128, 8 * 4096], BF16, kind="ExternalOutput"))
        dbg_xbf = ap(nc.dram_tensor("dbg_xbf", [128, 16 * (TC + 3)], BF16, kind="ExternalOutput"))
        dbg_lns = ap(nc.dram_tensor("dbg_lns", [128, 8], F32, kind="ExternalOutput"))
        dbg_lnt = ap(nc.dram_tensor("dbg_lnt", [128, 1024], F32, kind="ExternalOutput"))
        dbg_comb = ap(nc.dram_tensor("dbg_comb", [128, 8 * T2C], F32, kind="ExternalOutput"))
        dbg_attn = ap(nc.dram_tensor("dbg_attn", [128, 8 * T2C], F32, kind="ExternalOutput"))
        dbg_x1 = ap(nc.dram_tensor("dbg_x1", [128, 8 * T2C], F32, kind="ExternalOutput"))
        dbg_x2b = ap(nc.dram_tensor("dbg_x2b", [128, 8 * T2C], BF16, kind="ExternalOutput"))
        dbg_h1 = ap(nc.dram_tensor("dbg_h1", [128, 32 * T2C], BF16, kind="ExternalOutput"))
        dbg_ffn = ap(nc.dram_tensor("dbg_ffn", [128, 8 * T2C], F32, kind="ExternalOutput"))
        dbg_h2fm = ap(nc.dram_tensor("dbg_h2fm", [128, 8 * T2C], F32, kind="ExternalOutput"))
        dbg_ps = ap(nc.dram_tensor("dbg_ps", [128, 1024], F32, kind="ExternalOutput"))
        dbg_pjt = ap(nc.dram_tensor("dbg_pjt", [128, 2048], BF16, kind="ExternalOutput"))

    inw_bf = ap(nc.dram_tensor("inw_bf", [2 * DI, DM], BF16, kind="Internal"))
    inwT_d = ap(nc.dram_tensor("inwT_d", [DM, 2 * DI], BF16, kind="Internal"))
    mout_bf = ap(nc.dram_tensor("mout_bf", [DM, DI], BF16, kind="Internal"))
    proj_bf = ap(nc.dram_tensor("proj_bf", [DM, DM], BF16, kind="Internal"))
    w1_bf = ap(nc.dram_tensor("w1_bf", [FF, DM], BF16, kind="Internal"))
    w2_bf = ap(nc.dram_tensor("w2_bf", [DM, FF], BF16, kind="Internal"))

    shx = ap(nc.dram_tensor("shx", [2, DM, L], F32, kind="Internal",
                            addr_space="Shared"))
    barin_d = ap(nc.dram_tensor("barin_d", [1, 2], F32, kind="Internal"))
    barout_d = ap(nc.dram_tensor("barout_d", [1, 2], F32, kind="Internal"))

    es = ExitStack()
    SB = lambda name, shape, dt: es.enter_context(nc.sbuf_tensor(name, shape, dt))

    A_t = SB("A_t", [128, NS * NS], F32)
    D_t = SB("D_t", [128, 16], F32)
    convw_t = SB("convw_t", [128, 16, 4], F32)
    convb_t = SB("convb_t", [128, 16], F32)
    dtpb_t = SB("dtpb_t", [128, 16], F32)
    nw_b = SB("nw_b", [128, DM], F32)
    nb_b = SB("nb_b", [128, DM], F32)
    cone = SB("cone", [128, 1], F32)
    onesb = SB("onesb", [1, 128], BF16)
    zrow = SB("zrow", [1, 512], BF16)
    onesf = SB("onesf", [1, 128], F32)
    onescol = SB("onescol", [128, 1], F32)
    eyeb_t = SB("eyeb_t", [128, 128], BF16)
    eyef_t = SB("eyef_t", [128, 128], F32)
    xwT = SB("xwT", [128, 16, 96], BF16)
    dtpwT = SB("dtpwT", [65, 16, 128], BF16)
    states = SB("states", [128, 16, 16], F32)
    wsc = SB("wsc", [128, 1], F32)
    tr48 = SB("tr48", [48, 128], F32)
    selm_t = SB("selm_t", [32, 4096], BF16)
    tagt = SB("tagt", [1, 2], U32)
    ftile = SB("ftile", [1, 2], U32)
    part_t = SB("part_t", [1, 2], U32)
    dmy1 = SB("dmy1", [1, 2], U32)
    dmy2 = SB("dmy2", [1, 2], U32)
    tw_t = SB("tw_t", [1, 2], F32)
    sm_t = SB("sm_t", [1, 2], F32)
    sm_s = SB("sm_s", [1, 1], F32)
    wsel = SB("wsel", [1, 1], F32)
    tmpdtu = SB("tmpdtu", [128, TC], BF16)
    yg = SB("yg", [128, TC], F32)
    udg = SB("udg", [128, TC], F32)
    dt_in = SB("dt_in", [65, TC], BF16)
    bctok = SB("bctok", [128, 64], BF16)
    btt = SB("btt", [32, 256], BF16)
    lnstat = SB("lnstat", [128, 8], F32)

    PBC = es.enter_context(nc.psum_tensor("PBC", [128, 1024], F32))
    PM = [es.enter_context(nc.psum_tensor(f"PM{i}", [128, 512], F32))
          for i in range(4)]
    P5 = es.enter_context(nc.psum_tensor("P5", [128, 512], F32))
    PTR = es.enter_context(nc.psum_tensor("PTR", [128, 256], BF16))

    def pmslot(sl):
        return PM[sl // 2][:, (sl % 2) * 256:(sl % 2) * 256 + 256]

    sLD = SemCtr(nc, es, "sLD")
    sLa = SemCtr(nc, es, "sLa")
    sLb = SemCtr(nc, es, "sLb")
    sTa = SemCtr(nc, es, "sTa")
    sTb = SemCtr(nc, es, "sTb")
    sTc = SemCtr(nc, es, "sTc")
    sTd = SemCtr(nc, es, "sTd")
    sTW = SemCtr(nc, es, "sTW")
    sLB = SemCtr(nc, es, "sLB")
    sH2 = SemCtr(nc, es, "sH2")
    sGI = SemCtr(nc, es, "sGI")
    sGT = SemCtr(nc, es, "sGT")
    sGM = SemCtr(nc, es, "sGM")
    sGX = SemCtr(nc, es, "sGX")
    sGC = SemCtr(nc, es, "sGC")
    sGP = SemCtr(nc, es, "sGP")
    sGW1 = SemCtr(nc, es, "sGW1")
    sGW2 = SemCtr(nc, es, "sGW2")
    sDT = SemCtr(nc, es, "sDT")
    sBG = SemCtr(nc, es, "sBG")
    sTR = SemCtr(nc, es, "sTR")
    sG = SemCtr(nc, es, "sG")
    sPE = SemCtr(nc, es, "sPE")
    sV = SemCtr(nc, es, "sV")
    sSc = SemCtr(nc, es, "sSc")
    sPoll = SemCtr(nc, es, "sPoll")
    sST = SemCtr(nc, es, "sST")
    sBAR = SemCtr(nc, es, "sBAR")
    sVE = SemCtr(nc, es, "sVE")
    sSE = SemCtr(nc, es, "sSE")
    sPL = SemCtr(nc, es, "sPL")
    sPG = SemCtr(nc, es, "sPG")

    def vedge(v, instr):
        instr.then_inc(sVE.h, 1)
        sVE.n += 1
        v.wait_ge(sVE.h, sVE.n)

    def sedge(s, instr):
        instr.then_inc(sSE.h, 1)
        sSE.n += 1
        s.wait_ge(sSE.h, sSE.n)

    def pedge(p, instr):
        instr.then_inc(sPG.h, 1)
        sPG.n += 1
        p.wait_ge(sPG.h, sPG.n)

    blk = es.enter_context(nc.Block())

    def SEC(name, body):
        getattr(blk, name)(body)

    def _(sync):
        for dst, src in [
            (part_t[:], part_in),
            (tagt[:], hid_in[0:1, 0:2].bitcast(U32)),
            (eyeb_t[:], eyeb_in),
            (eyef_t[:], eyef_in),
            (A_t[:].rearrange("p (g n) -> p g n", g=16),
             alog_in.rearrange("(g p) n -> p g n", p=128)),
            (convw_t[:], convw_in.rearrange("(g p) j -> p g j", p=128)),
            (tr48[0:16, :], convb_in.rearrange("(g p) -> g p", p=128)),
            (tr48[16:32, :], dtpb_in.rearrange("(g p) -> g p", p=128)),
            (tr48[32:48, :], dvec_in.rearrange("(g p) -> g p", p=128)),
            (tw_t[:], trackw_in),
            (selm_t[:], selm_in),
        ]:
            sLD.inc(sync.dma_start(out=dst, in_=src), 16)
    SEC("sync", _)
    ld0 = sLD.n

    def _(v):
        v.wait_ge(sLD.h, ld0)
        v.memset(cone[:], 1.0)
        v.memset(onesb[:], 1.0)
        v.memset(zrow[:], 0.0)
        v.memset(onesf[:], 1.0)
        v.memset(onescol[:], 1.0)
        v.memset(states[:], 0.0)
        sV.inc(v.memset(ftile[:], 0))
    SEC("vector", _)

    def _(s):
        s.wait_ge(sLD.h, ld0)
        sedge(s, s.activation(A_t[:], A_t[:], AF.Exp))
        sSc.inc(s.mul(A_t[:], A_t[:], -1.0))
    SEC("scalar", _)
    sc_A = sSc.n

    def _(t):
        t.wait_ge(sLD.h, ld0)
        sPE.inc(t.transpose(PBC[0:128, 0:48], tr48[:], eyef_t[0:48, 0:48]))
    SEC("tensor", _)
    pe_tr = sPE.n

    def _(v):
        v.wait_ge(sPE.h, pe_tr)
        v.tensor_copy(convb_t[:], PBC[:, 0:16])
        v.tensor_copy(dtpb_t[:], PBC[:, 16:32])
        sV.inc(v.tensor_copy(D_t[:], PBC[:, 32:48]))
    SEC("vector", _)
    vs_tr = sV.n

    def _(v):
        vedge(v, v.reduce_max(sm_s[:], tw_t[:], axis=AX))
        sV.inc(v.tensor_scalar(sm_t[:], tw_t[:], sm_s[0:1, 0:1], None,
                               ALU.subtract))
    SEC("vector", _)
    vs_sm = sV.n

    def _(s):
        s.wait_ge(sV.h, vs_sm)
        sSc.inc(s.activation(sm_t[:], sm_t[:], AF.Exp))
    SEC("scalar", _)
    sc_sm = sSc.n

    def _(v):
        v.wait_ge(sSc.h, sc_sm)
        vedge(v, v.reduce_sum(sm_s[:], sm_t[:], axis=AX))
        vedge(v, v.reciprocal(sm_s[:], sm_s[:]))
        vedge(v, v.tensor_scalar(sm_t[:], sm_t[:], sm_s[0:1, 0:1], None,
                                 ALU.mult))
        parv = v.to_reg(v.value_load(part_t[0:1, 0:1]))
        with v.If_eq(parv, 0):
            v.tensor_copy(wsel[:], sm_t[0:1, 0:1]).then_inc(sV.h, 1)
        with v.Else():
            v.tensor_copy(wsel[:], sm_t[0:1, 1:2]).then_inc(sV.h, 1)
        v.end_ifs()
        sV.n += 1
    SEC("vector", _)
    vs_wsel = sV.n

    def _(t):
        t.wait_ge(sV.h, vs_wsel)
        sPE.inc(t.matmul(P5[:, 0:1], onesf[:], wsel[:], start=True, stop=True))
    SEC("tensor", _)
    pe_wsc = sPE.n

    def _(v):
        v.wait_ge(sPE.h, pe_wsc)
        sV.inc(v.tensor_copy(wsc[:], P5[:, 0:1]))
    SEC("vector", _)
    vs_wsc = sV.n

    es1 = ExitStack()
    SB1 = lambda name, shape, dt: es1.enter_context(
        nc.sbuf_tensor(name, shape, dt))
    hidt = SB1("hidt", [128, 2, 1024], F32)
    lnt = SB1("lnt", [128, 1024], F32)
    hn_bf = SB1("hn_bf", [128, 1024], BF16)
    hnfm = SB1("hnfm", [128, 8, TC], BF16)
    XBF = SB1("XBF", [128, 16, TC + 3], BF16)
    ACC = SB1("ACC", [128, 8, TC], F32)
    UBF = SB1("UBF", [128, 16, TC], BF16)
    SZ = SB1("SZ", [128, 16, TC], BF16)
    Bbc = SB1("Bbc", [128, 16, TC], BF16)
    Cbc = SB1("Cbc", [128, 16, TC], BF16)
    a_t = SB1("a_t", [128, NS * SEG], F32)
    bh_t = SB1("bh_t", [128, NS * SEG], BF16)
    h_t = SB1("h_t", [128, 2, NS * SEG], BF16)
    gated = SB1("gated", [128, 2, TC], BF16)
    mo_fm = SB1("mo_fm", [128, 8, TC], F32)
    dtg3 = SB1("dtg3", [128, 16, TC], F32)
    sabs3 = SB1("sabs3", [128, 8, TC], F32)
    inwT = SB1("inwT", [128, 2, 8, 256], BF16)
    moutT = SB1("moutT", [128, 2, 1024], BF16)
    es1a = ExitStack()
    inwS = es1a.enter_context(nc.sbuf_tensor("inwS", [128, 4096], BF16))

    a3 = a_t[:].rearrange("p (n t) -> p n t", n=NS)
    b3 = bh_t[:].rearrange("p (n t) -> p n t", n=NS)
    hview = h_t[:].rearrange("p a b -> p (a b)")

    def h3s(sl):
        return h_t[:, sl, :].rearrange("p (n t) -> p n t", n=NS)

    def _(v):
        v.memset(a_t[:], 0.0)
        v.memset(dt_in[64:65, :], 1.0)
        sV.inc(v.memset(XBF[:, :, 0:3], 0.0))
    SEC("vector", _)

    def _(sync):
        sync.wait_ge(sLD.h, ld0)
        sync.dma_start(out=lnt[0:1, :],
                       in_=normw_in.rearrange("(o d) -> o d", o=1)
                       ).then_inc(sLD.h, 16)
        sLD.n += 16
    SEC("sync", _)
    ld_nw = sLD.n

    def _(t):
        t.wait_ge(sLD.h, ld_nw)
        t.wait_ge(sV.h, vs_tr)
        t.matmul(PBC[:, 0:512], onesf[:], lnt[0:1, 0:512], start=True, stop=True)
        sPE.inc(t.matmul(PBC[:, 512:1024], onesf[:], lnt[0:1, 512:1024],
                         start=True, stop=True))
    SEC("tensor", _)
    pe_nw = sPE.n

    def _(v):
        v.wait_ge(sPE.h, pe_nw)
        sV.inc(v.tensor_copy(nw_b[:], PBC[:, 0:1024]))
    SEC("vector", _)
    vs_nw = sV.n

    def _(sync):
        sync.wait_ge(sV.h, vs_nw)
        sync.dma_start(out=lnt[0:1, :],
                       in_=normb_in.rearrange("(o d) -> o d", o=1)
                       ).then_inc(sLD.h, 16)
        sLD.n += 16
    SEC("sync", _)
    ld_nb = sLD.n

    def _(t):
        t.wait_ge(sLD.h, ld_nb)
        t.matmul(PBC[:, 0:512], onesf[:], lnt[0:1, 0:512], start=True, stop=True)
        sPE.inc(t.matmul(PBC[:, 512:1024], onesf[:], lnt[0:1, 512:1024],
                         start=True, stop=True))
    SEC("tensor", _)
    pe_nb = sPE.n

    def _(v):
        v.wait_ge(sPE.h, pe_nb)
        sV.inc(v.tensor_copy(nb_b[:], PBC[:, 0:1024]))
    SEC("vector", _)
    vs_nb = sV.n

    inw_cast = []
    inw_store = []
    for i in range(32):
        sLP = sLa if i % 2 == 0 else sLb

        def _(sync, i=i, sLP=sLP):
            if i >= 2:
                sync.wait_ge(sV.h, inw_cast[i - 2])
            sLP.inc(sync.dma_start(out=hidt[:, i % 2, :],
                                   in_=inw_in[i * 128:(i + 1) * 128, :]), 16)
        SEC("sync", _)
        myld = sLP.n

        def _(v, i=i, myld=myld, sLP=sLP):
            v.wait_ge(sLP.h, myld)
            if i >= 1:
                v.wait_ge(sGI.h, inw_store[i - 1])
            inw_cast.append(
                sV.inc(v.tensor_copy(hn_bf[:], hidt[:, i % 2, :])))
        SEC("vector", _)

        def _(g_, i=i):
            g_.wait_ge(sV.h, inw_cast[i])
            inw_store.append(
                sGI.inc(g_.dma_start(out=inw_bf[i * 128:(i + 1) * 128, :],
                                     in_=hn_bf[:]), 16))
        SEC("gpsimd", _)

    inwT_st = []
    for kg in range(8):
        def _(sync, kg=kg):
            if kg == 0:
                sync.wait_ge(sGI.h, inw_store[-1])
            else:
                sync.wait_ge(sGT.h, inwT_st[kg - 1])
            sTR.inc(sync.dma_start(out=inwS[:],
                                   in_=inw_bf[:, kg * 128:(kg + 1) * 128],
                                   transpose=True), 16)
        SEC("sync", _)
        mytr = sTR.n

        def _(g_, kg=kg, mytr=mytr):
            g_.wait_ge(sTR.h, mytr)
            inwT_st.append(
                sGT.inc(g_.dma_start(out=inwT_d[kg * 128:(kg + 1) * 128, :],
                                     in_=inwS[:]), 16))
        SEC("gpsimd", _)
    es1a.close()
    pb3 = SB1("pb3", [128, 16, TC], BF16)

    hid_flat = hidt[:].rearrange("p a b -> p (a b)")
    mo_cast = []
    mo_store = []
    for i in range(8):
        def _(sync, i=i):
            if i == 0:
                sync.wait_ge(sV.h, inw_cast[-1])
            else:
                sync.wait_ge(sV.h, mo_cast[i - 1])
            sLD.inc(sync.dma_start(out=hid_flat,
                                   in_=moutw_in[i * 128:(i + 1) * 128, :]), 16)
        SEC("sync", _)
        myld = sLD.n

        def _(v, i=i, myld=myld):
            v.wait_ge(sLD.h, myld)
            v.wait_ge(sV.h, vs_wsc)
            if i == 0:
                v.wait_ge(sGI.h, inw_store[-1])
            else:
                v.wait_ge(sGM.h, mo_store[i - 1])
            mo_cast.append(
                sV.inc(v.tensor_scalar(hview[:, 0:2048], hid_flat, wsc[:, 0:1], None,
                                       ALU.mult)))
        SEC("vector", _)

        def _(g_, i=i):
            g_.wait_ge(sV.h, mo_cast[i])
            mo_store.append(
                sGM.inc(g_.dma_start(out=mout_bf[i * 128:(i + 1) * 128, :],
                                     in_=hview[:, 0:2048]), 16))
        SEC("gpsimd", _)

    def _(sync):
        sync.wait_ge(sV.h, mo_cast[-1])
        sync.dma_start(out=hidt[0:96, 0, :], in_=xw_in[:, 0:1024]
                       ).then_inc(sLD.h, 16)
        sync.dma_start(out=hidt[0:96, 1, :], in_=xw_in[:, 1024:2048]
                       ).then_inc(sLD.h, 16)
        sync.dma_start(out=lnt[:].rearrange("p (g r) -> p g r", g=16),
                       in_=dtpw_in.rearrange("(g p) r -> p g r", p=128)
                       ).then_inc(sLD.h, 16)
        sLD.n += 48
    SEC("sync", _)

    def _(g_):
        sDT.inc(g_.dma_start(out=dtpwT[64:65, :, :],
                             in_=dtpb_in.rearrange("(o g d) -> o g d",
                                                   o=1, g=16)), 16)
    SEC("gpsimd", _)
    ld_xw = sLD.n

    dtpw_bf = SZ[:, 0:4, :].rearrange("p a b -> p (a b)")
    def _(v):
        v.wait_ge(sLD.h, ld_xw)
        v.wait_ge(sDT.h, 16)
        v.wait_ge(sGM.h, mo_store[-1])
        v.tensor_copy(hview[0:96, 0:1024], hidt[0:96, 0, :])
        v.tensor_copy(hview[0:96, 1024:2048], hidt[0:96, 1, :])
        sV.inc(v.tensor_copy(dtpw_bf, lnt[:]))
    SEC("vector", _)
    vs_xw = sV.n

    prev_ev = vs_xw
    for g in range(16):
        def _(t, g=g, prev_ev=prev_ev):
            t.wait_ge(sV.h, prev_ev)
            tt, off = g // 8, (g % 8) * 128
            t.transpose(PTR[:, 0:96], hview[0:96, tt * 1024 + off:
                                            tt * 1024 + off + 128],
                        eyeb_t[0:96, 0:96])
            sPE.inc(t.transpose(PTR[0:64, 96:224],
                                dtpw_bf[:, g * 64:(g + 1) * 64], eyeb_t[:]))
        SEC("tensor", _)
        mype = sPE.n

        def _(v, g=g, mype=mype):
            v.wait_ge(sPE.h, mype)
            v.tensor_copy(xwT[:, g, :], PTR[:, 0:96])
            sV.inc(v.tensor_copy(dtpwT[0:64, g, :], PTR[0:64, 96:224]))
        SEC("vector", _)
        prev_ev = sV.n

    hid_free = sV.n
    inproj_done = sPE.n
    prev_mout_done = sPE.n
    prev_shx_store = sGX.n
    pool_prev15 = None
    prev_pool = None
    prev_mm = None
    _dbgmarks = []

    for c in range(NCH):
        def _(sync, c=c, hid_free=hid_free, inproj_done=inproj_done):
            sync.wait_ge(sV.h, hid_free)
            if c == 0:
                sync.wait_ge(sGI.h, inw_store[-1])
            sLD.inc(sync.dma_start(
                out=hidt[:],
                in_=hid_in[c * TC:(c + 1) * TC, :].rearrange(
                    "(a p) d -> p a d", p=128)), 16)
        SEC("sync", _)
        ld_hid = sLD.n

        if DBG and c == 0:
            def _(g_, ld_hid=ld_hid, tr_inw=tr_inw):
                g_.wait_ge(sLD.h, ld_hid)
                g_.wait_ge(sTR.h, tr_inw)
                g_.dma_start(out=dbg_hid, in_=hidt[:].rearrange("p a b -> p (a b)")).then_inc(sG.h, 16)
                g_.dma_start(out=dbg_inwt, in_=inwT[:].rearrange("p a b -> p (a b)")).then_inc(sG.h, 16)
                g_.dma_start(out=dbg_nwb, in_=nw_b[:]).then_inc(sG.h, 16)
                g_.dma_start(out=dbg_nbb, in_=nb_b[:]).then_inc(sG.h, 16)
                sG.n += 64
                _dbgmarks.append(sG.n)
            SEC("gpsimd", _)

        tp_evac = None
        for tt in range(2):
            def _(v, c=c, tt=tt, ld_hid=ld_hid):
                v.wait_ge(sLD.h, ld_hid)
                if DBG and c == 0 and tt == 0:
                    v.wait_ge(sG.h, _dbgmarks[-1])
                if tt == 0 and c > 0:
                    v.tensor_copy(XBF[:, :, 0:3], XBF[:, :, TC:TC + 3])
                vedge(v, v.reduce_sum(lnstat[:, 0:1], hidt[:, tt, :], axis=AX))
                vedge(v, v.tensor_scalar(lnstat[:, 0:1], lnstat[:, 0:1],
                                         1.0 / DM, None, ALU.mult))
                v.tensor_scalar(lnt[:], hidt[:, tt, :], lnstat[:, 0:1], None,
                                ALU.subtract)
                v.tensor_tensor(hidt[:, tt, :], lnt[:], lnt[:], ALU.mult)
                vedge(v, v.reduce_sum(lnstat[:, 1:2], hidt[:, tt, :], axis=AX))
                sV.inc(v.tensor_scalar(lnstat[:, 1:2], lnstat[:, 1:2],
                                       1.0 / DM, EPS, ALU.mult, ALU.add))
            SEC("vector", _)
            m1 = sV.n

            if DBG and c == 0 and tt == 0:
                def _(g_, m1=m1):
                    g_.wait_ge(sV.h, m1)
                    g_.dma_start(out=dbg_lns, in_=lnstat[:]).then_inc(sG.h, 16)
                    g_.dma_start(out=dbg_lnt, in_=lnt[:]).then_inc(sG.h, 16)
                    sG.n += 32
                    _dbgmarks.append(sG.n)
                SEC("gpsimd", _)

            def _(s, m1=m1, c=c, tt=tt):
                s.wait_ge(sV.h, m1)
                if DBG and c == 0 and tt == 0:
                    s.wait_ge(sG.h, _dbgmarks[-1])
                sedge(s, s.activation(lnstat[:, 1:2], lnstat[:, 1:2], AF.Ln))
                sSc.inc(s.activation(lnstat[:, 1:2], lnstat[:, 1:2], AF.Exp,
                                     scale=-0.5))
            SEC("scalar", _)
            m2 = sSc.n

            def _(v, tt=tt, m2=m2):
                v.wait_ge(sSc.h, m2)
                v.tensor_scalar(lnt[:], lnt[:], lnstat[:, 1:2], None, ALU.mult)
                vedge(v, v.reduce_sum(lnstat[:, 2:3], lnt[:], axis=AX))
                vedge(v, v.tensor_scalar(lnstat[:, 2:3], lnstat[:, 2:3],
                                         1.0 / DM, None, ALU.mult))
                v.tensor_scalar(lnt[:], lnt[:], lnstat[:, 2:3], None,
                                ALU.subtract)
                v.tensor_tensor(hidt[:, tt, :], lnt[:], lnt[:], ALU.mult)
                vedge(v, v.reduce_sum(lnstat[:, 3:4], hidt[:, tt, :], axis=AX))
                sV.inc(v.tensor_scalar(lnstat[:, 3:4], lnstat[:, 3:4],
                                       1.0 / DM, EPS, ALU.mult, ALU.add))
            SEC("vector", _)
            m3 = sV.n

            def _(s, m3=m3):
                s.wait_ge(sV.h, m3)
                sedge(s, s.activation(lnstat[:, 3:4], lnstat[:, 3:4], AF.Ln))
                sSc.inc(s.activation(lnstat[:, 3:4], lnstat[:, 3:4], AF.Exp,
                                     scale=-0.5))
            SEC("scalar", _)
            m4 = sSc.n

            def _(v, m4=m4):
                v.wait_ge(sSc.h, m4)
                v.scalar_tensor_tensor(lnt[:], lnt[:], lnstat[:, 3:4], nw_b[:],
                                       ALU.mult, ALU.mult)
                sV.inc(v.tensor_tensor(hn_bf[:], lnt[:], nb_b[:], ALU.add))
            SEC("vector", _)
            m5 = sV.n

            for i in range(4):
                def _(t, tt=tt, i=i, m5=m5, tp_evac=tp_evac):
                    t.wait_ge(sV.h, m5)
                    if tp_evac is not None:
                        t.wait_ge(sV.h, tp_evac)
                    kk0 = i * 2
                    t.transpose(PTR[:, 0:128],
                                hn_bf[:, kk0 * 128:kk0 * 128 + 128], eyeb_t[:])
                    sPE.inc(t.transpose(PTR[:, 128:256],
                                        hn_bf[:, kk0 * 128 + 128:kk0 * 128 + 256],
                                        eyeb_t[:]))
                SEC("tensor", _)
                mype = sPE.n

                def _(v, tt=tt, i=i, mype=mype):
                    v.wait_ge(sPE.h, mype)
                    kk0 = i * 2
                    v.tensor_copy(hnfm[:, kk0, tt * 128:(tt + 1) * 128],
                                  PTR[:, 0:128])
                    sV.inc(v.tensor_copy(
                        hnfm[:, kk0 + 1, tt * 128:(tt + 1) * 128],
                        PTR[:, 128:256]))
                SEC("vector", _)
                tp_evac = sV.n
        hid_free = sV.n
        vs_hnfm = sV.n

        evac_marks = []
        inw_rt_marks = []
        for rt2 in range(16):
            sTP = sTa if rt2 % 2 == 0 else sTb

            def _(sync, rt2=rt2, c=c, sTP=sTP):
                if c == 0 and rt2 == 0:
                    sync.wait_ge(sGT.h, inwT_st[-1])
                if len(inw_rt_marks) >= 2:
                    sync.wait_ge(sPE.h, inw_rt_marks[-2])
                sTP.inc(sync.dma_start(
                    out=inwT[:, rt2 % 2, :, :],
                    in_=inwT_d[:, rt2 * 256:(rt2 + 1) * 256].rearrange(
                        "(kg p) m -> p kg m", p=128)), 16)
            SEC("sync", _)
            inw_rt_marks.append(None)
            _my_tr = sTP.n

            def _(t, rt2=rt2, vs_hnfm=vs_hnfm, _my_tr=_my_tr, sTP=sTP):
                t.wait_ge(sV.h, vs_hnfm)
                t.wait_ge(sTP.h, _my_tr)
                if rt2 >= 2:
                    engh, cnt = evac_marks[rt2 - 2]
                    t.wait_ge(engh, cnt)
                ins = None
                for j in range(2):
                    rt = rt2 * 2 + j
                    slot = PBC[:, (rt % 4) * 256:(rt % 4) * 256 + 256]
                    for kg in range(8):
                        ins = t.matmul(slot,
                                       inwT[:, rt2 % 2, kg,
                                            j * 128:(j + 1) * 128],
                                       hnfm[:, kg, :], start=(kg == 0),
                                       stop=(kg == 7))
                sPE.inc(ins)
                inw_rt_marks[rt2] = sPE.n
            SEC("tensor", _)
            mype = sPE.n

            if rt2 < 8:
                def _(v, rt2=rt2, mype=mype):
                    v.wait_ge(sPE.h, mype)
                    ins = None
                    for j in range(2):
                        rt = rt2 * 2 + j
                        ins = v.tensor_copy(XBF[:, rt, 3:3 + TC],
                                            PBC[:, (rt % 4) * 256:(rt % 4) * 256 + 256])
                    sV.inc(ins)
                SEC("vector", _)
                evac_marks.append((sV.h, sV.n))
            else:
                def _(s, rt2=rt2, mype=mype, pool_prev15=pool_prev15):
                    s.wait_ge(sPE.h, mype)
                    if rt2 == 8 and pool_prev15 is not None:
                        s.wait_ge(sPL.h, pool_prev15)
                    ins = None
                    for j in range(2):
                        rt = rt2 * 2 + j
                        ins = s.activation(SZ[:, rt - 16, :],
                                           PBC[:, (rt % 4) * 256:(rt % 4) * 256 + 256],
                                           AF.Silu)
                    sSc.inc(ins)
                SEC("scalar", _)
                evac_marks.append((sSc.h, sSc.n))
        inproj_done = sPE.n
        x_ready = evac_marks[7][1]

        silu_marks = []
        for hh in range(2):
            def _(v, hh=hh, x_ready=x_ready, pool_prev15=pool_prev15):
                if hh == 0:
                    v.wait_ge(sV.h, x_ready)
                    if pool_prev15 is not None:
                        v.wait_ge(sPL.h, pool_prev15)
                else:
                    v.wait_ge(sSc.h, silu_marks[0])
                ins = None
                for g8 in range(8):
                    g = 8 * hh + g8
                    acc = ACC[:, g8, :]
                    v.tensor_scalar(acc, XBF[:, g, 0:TC], convw_t[:, g, 0:1],
                                    convb_t[:, g:g + 1], ALU.mult, ALU.add)
                    for j in range(1, 4):
                        ins = v.scalar_tensor_tensor(acc, XBF[:, g, j:j + TC],
                                                     convw_t[:, g, j:j + 1],
                                                     acc, ALU.mult, ALU.add)
                sV.inc(ins)
            SEC("vector", _)
            myv = sV.n

            def _(s, hh=hh, myv=myv):
                s.wait_ge(sV.h, myv)
                silu_marks.append(
                    sSc.inc(s.activation(
                        UBF[:, 8 * hh:8 * hh + 8, :].rearrange(
                            "p a b -> p (a b)"),
                        ACC[:].rearrange("p a b -> p (a b)"), AF.Silu)))
            SEC("scalar", _)
        u_ready = silu_marks[-1]

        def _(t, u_ready=u_ready, c=c):
            t.wait_ge(sSc.h, u_ready)
            ins = None
            for kg in range(16):
                ins = t.matmul(P5[0:96, 0:256], xwT[:, kg, :], UBF[:, kg, :],
                               start=(kg == 0), stop=(kg == 15))
            sPE.inc(ins)
        SEC("tensor", _)
        pe_xdbl = sPE.n

        def _(v, pe_xdbl=pe_xdbl):
            v.wait_ge(sPE.h, pe_xdbl)
            sV.inc(v.tensor_copy(dt_in[0:64, :], P5[0:64, 0:256]))
        SEC("vector", _)
        vs_dtin = sV.n

        def _(t, u_ready=u_ready):
            ins = None
            for ts_ in range(2):
                for kg in range(16):
                    ins = t.matmul(PBC[:, ts_ * 32:(ts_ + 1) * 32],
                                   UBF[:, kg, ts_ * 128:(ts_ + 1) * 128],
                                   xwT[:, kg, 64:96],
                                   start=(kg == 0), stop=(kg == 15))
            sPE.inc(ins)
        SEC("tensor", _)
        pe_xT = sPE.n

        def _(v, pe_xT=pe_xT):
            v.wait_ge(sPE.h, pe_xT)
            sV.inc(v.tensor_copy(bctok[:], PBC[:, 0:64]))
        SEC("vector", _)
        vs_bctok = sV.n

        def _(t, vs_bctok=vs_bctok):
            t.wait_ge(sV.h, vs_bctok)
            t.transpose(PTR[0:32, 0:128], bctok[:, 0:32], eyeb_t[:])
            sPE.inc(t.transpose(PTR[0:32, 128:256], bctok[:, 32:64], eyeb_t[:]))
        SEC("tensor", _)
        pe_bt = sPE.n

        def _(v, pe_bt=pe_bt):
            v.wait_ge(sPE.h, pe_bt)
            sV.inc(v.tensor_copy(btt[:], PTR[0:32, :]))
        SEC("vector", _)
        vs_bt = sV.n

        bc_evac = []
        for r in range(8):
            def _(t, r=r, vs_bt=vs_bt):
                t.wait_ge(sV.h, vs_bt)
                if r > 0:
                    t.wait_ge(sV.h, bc_evac[r - 1])
                ins = None
                for j in range(4):
                    n = (r % 4) * 4 + j + (0 if r < 4 else 16)
                    ins = t.matmul(pmslot(j), selm_t[:, n * 128:(n + 1) * 128],
                                   btt[:], start=True, stop=True)
                sPE.inc(ins)
            SEC("tensor", _)
            mype = sPE.n

            def _(v, r=r, mype=mype, c=c, pool_prev15=pool_prev15):
                v.wait_ge(sPE.h, mype)
                if r == 4 and pool_prev15 is not None:
                    v.wait_ge(sPL.h, pool_prev15)
                n0 = (r % 4) * 4
                tgt = Bbc if r < 4 else Cbc
                v.tensor_copy(
                    tgt[:, n0:n0 + 2, :].rearrange("p a b -> p (a b)"),
                    PM[0][:, 0:512])
                sV.inc(v.tensor_copy(
                    tgt[:, n0 + 2:n0 + 4, :].rearrange("p a b -> p (a b)"),
                    PM[1][:, 0:512]))
            SEC("vector", _)
            bc_evac.append(sV.n)
        bc_done_v = bc_evac[-1]
        bc_done_s = sSc.n

        if DBG and c == 0:
            def _(g_, x_ready=x_ready, m5=m5):
                g_.wait_ge(sV.h, x_ready)
                g_.dma_start(out=dbg_hnbf, in_=hn_bf[:]).then_inc(sG.h, 16)
                g_.dma_start(out=dbg_xbf, in_=XBF[:].rearrange("p a b -> p (a b)")).then_inc(sG.h, 16)
                sG.n += 32
            SEC("gpsimd", _)
            def _(g_, vs_dtin=vs_dtin, bc_done_v=bc_done_v, bc_done_s=bc_done_s):
                g_.wait_ge(sV.h, vs_dtin)
                g_.wait_ge(sV.h, bc_done_v)
                g_.wait_ge(sSc.h, bc_done_s)
                g_.dma_start(out=dbg_hnfm, in_=hnfm[:].rearrange("p a b -> p (a b)")).then_inc(sG.h, 16)
                g_.dma_start(out=dbg_u, in_=UBF[:].rearrange("p a b -> p (a b)")).then_inc(sG.h, 16)
                g_.dma_start(out=dbg_sz, in_=SZ[:].rearrange("p a b -> p (a b)")).then_inc(sG.h, 16)
                g_.dma_start(out=dbg_dtin, in_=dt_in[:]).then_inc(sG.h, 16)
                g_.dma_start(out=dbg_bbc, in_=Bbc[:].rearrange("p a b -> p (a b)")).then_inc(sG.h, 16)
                g_.dma_start(out=dbg_cbc, in_=Cbc[:].rearrange("p a b -> p (a b)")).then_inc(sG.h, 16)
                sG.n += 96
            SEC("gpsimd", _)

        def dtp_slot2(k):
            return (P5[:, 0:512] if k % 2 == 0 else PM[0][:, 0:512])

        dtp_evac = []
        dtg_half = [None, None]
        for k in range(8):
            def _(t, k=k, vs_dtin=vs_dtin, bc_done_v=bc_done_v):
                if k == 0:
                    t.wait_ge(sV.h, vs_dtin)
                if k == 1:
                    t.wait_ge(sV.h, bc_done_v)
                if k >= 2:
                    t.wait_ge(sSc.h, dtp_evac[k - 2][0])
                    t.wait_ge(sV.h, dtp_evac[k - 2][1])
                sl = dtp_slot2(k)
                t.matmul(sl[:, 0:256], dtpwT[:, 2 * k, :], dt_in[:],
                         start=True, stop=True)
                sPE.inc(t.matmul(sl[:, 256:512], dtpwT[:, 2 * k + 1, :],
                                 dt_in[:], start=True, stop=True))
            SEC("tensor", _)
            mype = sPE.n

            def _(s, k=k, mype=mype):
                s.wait_ge(sPE.h, mype)
                if k == 4:
                    s.wait_ge(sV.h, dtg_half[0])
                sSc.inc(s.activation(
                    sabs3[:, (2 * k) % 8:(2 * k) % 8 + 2, :].rearrange(
                        "p a b -> p (a b)"),
                    dtp_slot2(k), AF.Abs))
            SEC("scalar", _)
            sc_abs = sSc.n

            def _(v, k=k, mype=mype):
                v.wait_ge(sPE.h, mype)
                sV.inc(v.tensor_scalar(
                    dtg3[:, 2 * k:2 * k + 2, :].rearrange("p a b -> p (a b)"),
                    dtp_slot2(k), 0.0, None, ALU.max))
            SEC("vector", _)
            dtp_evac.append((sc_abs, sV.n))

            if k % 4 == 3:
                sf = sabs3[:].rearrange("p a b -> p (a b)")
                g = 2 * k + 1

                def _(s, sf=sf):
                    sedge(s, s.activation(sf, sf, AF.Exp, scale=-1.0))
                    sSc.inc(s.activation(sf, sf, AF.Ln, bias=cone[:, 0:1]))
                SEC("scalar", _)
                sc_sp = sSc.n

                def _(v, g=g, sc_sp=sc_sp, sf=sf):
                    v.wait_ge(sSc.h, sc_sp)
                    h8 = dtg3[:, g - 7:g + 1, :].rearrange("p a b -> p (a b)")
                    sV.inc(v.tensor_tensor(h8, h8, sf, ALU.add))
                SEC("vector", _)
                dtg_half[k // 4] = sV.n

        A_t3 = A_t[:].rearrange("p (g n) -> p g n", g=16)

        def a3_build(v, g):
            ins = v.tensor_tensor(
                a3[:, :, 1:SEG],
                dtg3[:, g, :].unsqueeze(1).broadcast_to([128, NS, TC]),
                A_t3[:, g, :].unsqueeze(2).broadcast_to([128, NS, TC]),
                ALU.mult)
            ins.then_inc(sV.h, 1)
            sV.n += 1
            return sV.n

        def b3_build(v, g):
            v.tensor_tensor(tmpdtu[:], dtg3[:, g, :], UBF[:, g, :], ALU.mult)
            v.tensor_tensor(
                b3[:, :, 1:SEG],
                tmpdtu[:].unsqueeze(1).broadcast_to([128, NS, TC]),
                Bbc[:, :, :], ALU.mult)
            ins = v.tensor_copy(
                b3[:, :, 0:1],
                states[:, g:g + 1, :].rearrange("p o n -> p n o"))
            ins.then_inc(sV.h, 1)
            sV.n += 1
            return sV.n

        exp_marks = []
        a3_marks = {}
        gated_marks = []
        mm_g = []
        moutT_mark = []

        def _(v, bc_done_v=bc_done_v):
            v.wait_ge(sV.h, dtg_half[0])
            v.wait_ge(sV.h, bc_done_v)
            a3_build(v, 0)
        SEC("vector", _)
        a3_mark = sV.n

        def _(s, a3_mark=a3_mark):
            s.wait_ge(sV.h, a3_mark)
            sSc.inc(s.activation(a3[:, :, 1:SEG], a3[:, :, 1:SEG], AF.Exp))
        SEC("scalar", _)
        exp_marks.append(sSc.n)

        def _(v, bc_done_v=bc_done_v):
            v.wait_ge(sV.h, bc_done_v)
            b3_build(v, 0)
        SEC("vector", _)

        pool_marks = []
        for g in range(16):
            sMP = sTc if g % 2 == 0 else sTd

            def _(sync, g=g, c=c, prev_mout_done=prev_mout_done, sMP=sMP):
                if g >= 2:
                    sync.wait_ge(sPE.h, mm_g[g - 2])
                elif c == 0:
                    sync.wait_ge(sGM.h, mo_store[-1])
                else:
                    sync.wait_ge(sPE.h, prev_mout_done)
                sMP.inc(sync.dma_start(
                    out=moutT[:, g % 2, :],
                    in_=mout_bf[:, g * 128:(g + 1) * 128],
                    transpose=True), 16)
                moutT_mark.append((sMP.h, sMP.n))
            SEC("sync", _)

            def _(v, g=g, prev_pool=prev_pool):
                v.wait_ge(sSc.h, exp_marks[g])
                if g >= 2:
                    v.wait_ge(sPL.h, pool_marks[g - 2])
                elif prev_pool is not None:
                    v.wait_ge(sPL.h, prev_pool[14 + g])
                ins = v.tensor_tensor_scan(h_t[:, g % 2, :], a_t[:], bh_t[:],
                                           0.0, ALU.mult, ALU.add)
                ins.then_inc(sV.h, 1)
                sV.n += 1
                v.wait_ge(sV.h, sV.n)
                if g < 15:
                    if g == 7:
                        v.wait_ge(sV.h, dtg_half[1])
                    a3_marks[g + 1] = a3_build(v, g + 1)
                    b3_build(v, g + 1)
            SEC("vector", _)
            scan_done = sV.n - (2 if g < 15 else 0)

            def _(p, g=g, scan_done=scan_done, prev_mm=prev_mm):
                p.wait_ge(sV.h, scan_done)
                if g >= 2:
                    p.wait_ge(sPE.h, mm_g[g - 2])
                elif prev_mm is not None:
                    p.wait_ge(sPE.h, prev_mm[14 + g])
                h3v = h3s(g % 2)
                p.tensor_copy(states[:, g:g + 1, :].rearrange("p o n -> p n o"),
                              h3v[:, :, SEG - 1:SEG])
                pedge(p, p.tensor_tensor(pb3[:], h3v[:, :, 1:SEG],
                                         Cbc[:, :, :], ALU.mult))
                pedge(p, p.tensor_tensor(pb3[:, 0:8, :], pb3[:, 0:8, :],
                                         pb3[:, 8:16, :], ALU.add))
                pedge(p, p.tensor_tensor(pb3[:, 0:4, :], pb3[:, 0:4, :],
                                         pb3[:, 4:8, :], ALU.add))
                pedge(p, p.tensor_tensor(pb3[:, 0:2, :], pb3[:, 0:2, :],
                                         pb3[:, 2:4, :], ALU.add))
                pedge(p, p.tensor_tensor(yg[:], pb3[:, 0, :], pb3[:, 1, :],
                                         ALU.add))
                p.tensor_scalar(udg[:], UBF[:, g, :], D_t[:, g:g + 1], None,
                                ALU.mult)
                pedge(p, p.tensor_tensor(yg[:], yg[:], udg[:], ALU.add))
                sPL.inc(p.tensor_tensor(gated[:, g % 2, :], yg[:], SZ[:, g, :],
                                        ALU.mult))
            SEC("gpsimd", _)
            pool_marks.append(sPL.n)

            if g < 15:
                def _(s, g=g):
                    s.wait_ge(sV.h, a3_marks[g + 1])
                    sSc.inc(s.activation(a3[:, :, 1:SEG], a3[:, :, 1:SEG],
                                         AF.Exp))
                SEC("scalar", _)
                exp_marks.append(sSc.n)

            def _(t, g=g):
                t.wait_ge(sPL.h, pool_marks[g])
                t.wait_ge(*moutT_mark[g])
                if g == 0:
                    for q in range(4):
                        t.matmul(PM[q][:, 0:512], onesb[:], zrow[:],
                                 start=True, stop=True)
                ins = None
                for og in range(8):
                    ins = t.matmul(pmslot(og),
                                   moutT[:, g % 2, og * 128:(og + 1) * 128],
                                   gated[:, g % 2, :],
                                   start=False, stop=(g == 15),
                                   skip_group_check=True)
                sPE.inc(ins)
                mm_g.append(sPE.n)
            SEC("tensor", _)
        prev_mout_done = sPE.n
        prev_pool = pool_marks
        prev_mm = mm_g
        pool_prev15 = pool_marks[15]

        def _(v, c=c, prev_shx_store=prev_shx_store):
            v.wait_ge(sPE.h, mm_g[-1])
            v.wait_ge(sGX.h, prev_shx_store)
            ins = None
            for q in range(4):
                ins = v.tensor_copy(
                    mo_fm[:, 2 * q:2 * q + 2, :].rearrange("p a b -> p (a b)"),
                    PM[q][:, 0:512])
            sV.inc(ins)
        SEC("vector", _)
        vs_mo = sV.n

        def _(g_, c=c, vs_mo=vs_mo):
            g_.wait_ge(sV.h, vs_mo)
            parv = g_.to_reg(g_.value_load(part_t[0:1, 0:1]))
            dst0 = shx[0].rearrange("(og p) t -> p og t", p=128)[
                :, :, c * TC:(c + 1) * TC]
            dst1 = shx[1].rearrange("(og p) t -> p og t", p=128)[
                :, :, c * TC:(c + 1) * TC]
            with g_.If_eq(parv, 0):
                g_.dma_start(out=dst0, in_=mo_fm[:]).then_inc(sGX.h, 16)
            with g_.Else():
                g_.dma_start(out=dst1, in_=mo_fm[:]).then_inc(sGX.h, 16)
            g_.end_ifs()
            sGX.n += 16
        SEC("gpsimd", _)
        prev_shx_store = sGX.n
        if DBG and c == 0:
            def _(g_, vs_mo=vs_mo):
                g_.wait_ge(sV.h, vs_mo)
                g_.dma_start(out=dbg_mo, in_=mo_fm[:].rearrange("p a b -> p (a b)")).then_inc(sG.h, 16)
                sG.n += 16
            SEC("gpsimd", _)

    def _(g_):
        sDT.inc(g_.dma_start(out=barin_d, in_=onesf[0:1, 0:2]), 16)
        g_.wait_ge(sDT.h, sDT.n)
        g_.wait_ge(sGX.h, sGX.n)
        g_.collective_compute(
            "AllReduce", ALU.add, ins=[barin_d[:]], outs=[barout_d[:]],
            replica_groups=[[0, 1], [2, 3], [4, 5], [6, 7]],
        ).then_inc(sPoll.h, 1)
        sPoll.n += 1
        g_.wait_ge(sPoll.h, sPoll.n)
    SEC("gpsimd", _)

    def _(v):
        ins = v.tensor_copy(cone[:], cone[:])
        ins.then_inc(sBAR.h, 1)
        sBAR.n += 1
    SEC("vector", _)
    def _(s):
        ins = s.copy(cone[:], cone[:])
        ins.then_inc(sBAR.h, 1)
        sBAR.n += 1
    SEC("scalar", _)
    def _(t):
        ins = t.matmul(P5[:, 0:1], onesf[:], wsel[:], start=True, stop=True)
        ins.then_inc(sBAR.h, 1)
        sBAR.n += 1
    SEC("tensor", _)
    def _(sync):
        sync.dma_start(out=dmy1[:], in_=part_in).then_inc(sBAR.h, 16)
        sBAR.n += 16
    SEC("sync", _)
    def _(g_):
        g_.dma_start(out=dmy2[:], in_=part_in).then_inc(sBG.h, 16)
        sBG.n += 16
    SEC("gpsimd", _)
    BARN = sBAR.n

    es1.close()

    es2 = ExitStack()
    SB2 = lambda name, shape, dt: es2.enter_context(
        nc.sbuf_tensor(name, shape, dt))
    W1T = SB2("W1T", [128, 8, 4096], BF16)
    projT = SB2("projT", [128, 8, 1024], BF16)
    W2Tp = SB2("W2Tp", [128, 2, 1024], BF16)
    comb = SB2("comb", [128, 8, T2C], F32)
    combbf = SB2("combbf", [128, 8, T2C], BF16)
    x1 = SB2("x1", [128, 8, T2C], F32)
    x2b = SB2("x2b", [128, 8, T2C], BF16)
    h1b = SB2("h1b", [128, 32, T2C], BF16)
    hid2t = SB2("hid2t", [128, 2, 1024], F32)
    hid2fm = SB2("hid2fm", [128, 8, T2C], F32)
    sqt = SB2("sqt", [128, 8, T2C], F32)
    wstage = SB2("wstage", [128, 1024], F32)
    wneg = SB2("wneg", [128, 1024], F32)
    wbf = SB2("wbf", [128, 1024], BF16)
    babs = SB2("babs", [128, 1], F32)
    thr = SB2("thr", [128, 1], F32)
    nthr = SB2("nthr", [128, 1], F32)
    b1t = SB2("b1t", [128, 32], F32)
    b2t = SB2("b2t", [128, 8], F32)
    pbt = SB2("pbt", [128, 8], F32)
    st48 = SB2("st48", [48, 128], F32)
    m_r = SB2("m_r", [1, T2C], F32)
    v_r = SB2("v_r", [1, T2C], F32)
    t_r = SB2("t_r", [1, T2C], F32)

    pj_cast = []
    pj_store = []
    for i in range(8):
        def _(sync, i=i):
            if i == 0:
                sync.wait_ge(sBAR.h, BARN)
                sync.wait_ge(sBG.h, 16)
            if i >= 1:
                sync.wait_ge(sV.h, pj_cast[i - 1])
            sLD.inc(sync.dma_start(out=wstage[:],
                                   in_=projw_in[i * 128:(i + 1) * 128, :]), 16)
        SEC("sync", _)
        myld = sLD.n

        def _(v, i=i, myld=myld):
            if i == 0:
                v.wait_ge(sBAR.h, BARN)
                v.wait_ge(sBG.h, 16)
            v.wait_ge(sLD.h, myld)
            if i >= 1:
                v.wait_ge(sGP.h, pj_store[i - 1])
            pj_cast.append(sV.inc(v.tensor_copy(wbf[:], wstage[:])))
        SEC("vector", _)

        def _(g_, i=i):
            if i == 0:
                g_.wait_ge(sBAR.h, BARN)
                g_.wait_ge(sBG.h, 16)
            g_.wait_ge(sV.h, pj_cast[i])
            pj_store.append(
                sGP.inc(g_.dma_start(out=proj_bf[i * 128:(i + 1) * 128, :],
                                     in_=wbf[:]), 16))
        SEC("gpsimd", _)

    def wtile_src(win, i):
        if win is w1_in:
            return win[i * 128:(i + 1) * 128, :]
        ri, ci = i // 4, i % 4
        return win[ri * 128:(ri + 1) * 128, ci * 1024:(ci + 1) * 1024]

    def wtile_dst(wout, i):
        if wout is w1_bf:
            return wout[i * 128:(i + 1) * 128, :]
        ri, ci = i // 4, i % 4
        return wout[ri * 128:(ri + 1) * 128, ci * 1024:(ci + 1) * 1024]

    tern_last = {}
    prev_wstage_user = ("v", None)
    for wname, win, wout in [("w1", w1_in, w1_bf), ("w2", w2_in, w2_bf)]:
        sGW = sGW1 if wname == "w1" else sGW2
        absred = []
        pv = pj_cast[-1] if wname == "w1" else _w1_done
        for i in range(32):
            def _(sync, i=i, win=win, pv=pv):
                if i >= 1:
                    sync.wait_ge(sSc.h, absred[i - 1][0])
                else:
                    sync.wait_ge(sV.h, pv)
                sLD.inc(sync.dma_start(out=wstage[:], in_=wtile_src(win, i)), 16)
            SEC("sync", _)
            myld = sLD.n

            def _(s, i=i, myld=myld):
                s.wait_ge(sLD.h, myld)
                if i >= 1:
                    s.wait_ge(sVE.h, absred[i - 1][1])
                sSc.inc(s.activation(wneg[:], wstage[:], AF.Abs))
            SEC("scalar", _)
            mysc = sSc.n

            def _(v, i=i, mysc=mysc):
                v.wait_ge(sSc.h, mysc)
                if i == 0:
                    ins = v.reduce_sum(babs[:], wneg[:], axis=AX)
                else:
                    vedge(v, v.reduce_sum(thr[:], wneg[:], axis=AX))
                    ins = v.tensor_tensor(babs[:], babs[:], thr[:], ALU.add)
                vedge(v, ins)
            SEC("vector", _)
            absred.append((mysc, sVE.n))
        vs_abs = sVE.n

        def _(t, vs_abs=vs_abs):
            t.wait_ge(sVE.h, vs_abs)
            sPE.inc(t.matmul(P5[0:1, 0:1], babs[:], onescol[:], start=True,
                             stop=True))
        SEC("tensor", _)
        pe_tot = sPE.n

        def _(v, pe_tot=pe_tot):
            v.wait_ge(sPE.h, pe_tot)
            sV.inc(v.tensor_copy(babs[0:1, 0:1], P5[0:1, 0:1]))
        SEC("vector", _)
        vs_tot = sV.n

        def _(t, vs_tot=vs_tot):
            t.wait_ge(sV.h, vs_tot)
            sPE.inc(t.matmul(P5[:, 0:1], onesf[:], babs[0:1, 0:1], start=True,
                             stop=True))
        SEC("tensor", _)
        pe_bc = sPE.n

        def _(v, pe_bc=pe_bc):
            v.wait_ge(sPE.h, pe_bc)
            vedge(v, v.tensor_scalar(thr[:], P5[:, 0:1], 1.0 / (FF * DM),
                                     1e-8, ALU.mult, ALU.max))
            vedge(v, v.tensor_scalar(thr[:], thr[:], 0.5, None, ALU.mult))
            sV.inc(v.tensor_scalar(nthr[:], thr[:], -1.0, None, ALU.mult))
        SEC("vector", _)
        vs_thr = sV.n

        tn_marks = []
        st_marks = []
        for i in range(32):
            def _(sync, i=i, win=win, vs_thr=vs_thr):
                if i == 0:
                    sync.wait_ge(sV.h, vs_thr)
                else:
                    sync.wait_ge(sV.h, tn_marks[i - 1])
                sLD.inc(sync.dma_start(out=wstage[:], in_=wtile_src(win, i)), 16)
            SEC("sync", _)
            myld = sLD.n

            def _(v, i=i, myld=myld, sGW=sGW):
                v.wait_ge(sLD.h, myld)
                if i >= 1:
                    v.wait_ge(sGW.h, st_marks[i - 1])
                v.tensor_scalar(wneg[:], wstage[:], nthr[:, 0:1], None,
                                ALU.is_le)
                ins = v.scalar_tensor_tensor(wbf[:], wstage[:], thr[:, 0:1],
                                             wneg[:], ALU.is_ge, ALU.subtract)
                ins.then_inc(sV.h, 1)
                sV.n += 1
                tn_marks.append(sV.n)
            SEC("vector", _)

            def _(g_, i=i, wout=wout, sGW=sGW):
                g_.wait_ge(sV.h, tn_marks[i])
                st_marks.append(
                    sGW.inc(g_.dma_start(out=wtile_dst(wout, i), in_=wbf[:]), 16))
            SEC("gpsimd", _)
        tern_last[wname] = st_marks[-1]
        _w1_done = tn_marks[-1]

    def _(sync):
        sync.wait_ge(sGW1.h, tern_last["w1"])
        for kg in range(8):
            sTW.inc(sync.dma_start(out=W1T[:, kg, :],
                                   in_=w1_bf[:, kg * 128:(kg + 1) * 128],
                                   transpose=True), 16)
        sync.wait_ge(sGP.h, pj_store[-1])
        for kg in range(8):
            sTW.inc(sync.dma_start(out=projT[:, kg, :],
                                   in_=proj_bf[:, kg * 128:(kg + 1) * 128],
                                   transpose=True), 16)
    SEC("sync", _)
    tr_wres = sTW.n

    def _(sync):
        sync.dma_start(out=st48[0:32, :],
                       in_=b1_in.rearrange("(g p) -> g p", p=128)
                       ).then_inc(sLB.h, 16)
        sync.dma_start(out=st48[32:40, :],
                       in_=b2_in.rearrange("(g p) -> g p", p=128)
                       ).then_inc(sLB.h, 16)
        sync.dma_start(out=st48[40:48, :],
                       in_=projb_in.rearrange("(g p) -> g p", p=128)
                       ).then_inc(sLB.h, 16)
        sLB.n += 48
    SEC("sync", _)
    ld_b = sLB.n

    def _(t, ld_b=ld_b):
        t.wait_ge(sLB.h, ld_b)
        sPE.inc(t.transpose(PBC[0:128, 0:48], st48[:], eyef_t[0:48, 0:48]))
    SEC("tensor", _)
    pe_b = sPE.n

    def _(v, pe_b=pe_b):
        v.wait_ge(sPE.h, pe_b)
        v.tensor_copy(b1t[:], PBC[:, 0:32])
        v.tensor_copy(b2t[:], PBC[:, 32:40])
        sV.inc(v.tensor_copy(pbt[:], PBC[:, 40:48]))
    SEC("vector", _)
    vs_b = sV.n

    prev_combfree = None
    store_marks = []
    ffn2_evac_marks = []
    w2_mm = []

    for c2 in range(N2CH):
        def _(g_, c2=c2, prev_combfree=prev_combfree):
            if prev_combfree is not None:
                g_.wait_ge(sV.h, prev_combfree)
            parv = g_.to_reg(g_.value_load(part_t[0:1, 0:1]))
            t0a = c2 * T2C
            t0b = HALF + c2 * T2C
            sh0 = shx[0].rearrange("(og p) t -> p og t", p=128)
            sh1 = shx[1].rearrange("(og p) t -> p og t", p=128)
            with g_.If_eq(parv, 0):
                g_.dma_start(out=comb[:], in_=sh0[:, :, t0a:t0a + T2C]
                             ).then_inc(sGC.h, 16)
                g_.dma_start(out=sqt[:], in_=sh1[:, :, t0a:t0a + T2C]
                             ).then_inc(sGC.h, 16)
            with g_.Else():
                g_.dma_start(out=comb[:], in_=sh0[:, :, t0b:t0b + T2C]
                             ).then_inc(sGC.h, 16)
                g_.dma_start(out=sqt[:], in_=sh1[:, :, t0b:t0b + T2C]
                             ).then_inc(sGC.h, 16)
            g_.end_ifs()
            sGC.n += 32
        SEC("gpsimd", _)
        g_comb = sGC.n

        def _(sync, c2=c2):
            if c2 >= 1:
                sync.wait_ge(sST.h, store_marks[-1])
            sH2.inc(sync.dma_start(
                out=hid2t[:],
                in_=hid2_in[c2 * T2C:(c2 + 1) * T2C, :].rearrange(
                    "(a p) d -> p a d", p=128)), 16)
        SEC("sync", _)
        ld_h2 = sH2.n

        def _(v, g_comb=g_comb, vs_b=vs_b):
            v.wait_ge(sGC.h, g_comb)
            v.wait_ge(sV.h, vs_b)
            v.tensor_tensor(comb[:], comb[:], sqt[:], ALU.add)
            sV.inc(v.tensor_copy(combbf[:], comb[:]))
        SEC("vector", _)
        vs_comb = sV.n
        if DBG and c2 == 0:
            def _(g_, vs_comb=vs_comb):
                g_.wait_ge(sV.h, vs_comb)
                g_.dma_start(out=dbg_comb, in_=comb[:].rearrange("p a b -> p (a b)")).then_inc(sG.h, 16)
                sG.n += 16
                _dbgmarks.append(sG.n)
            SEC("gpsimd", _)

        def _(t, vs_comb=vs_comb, c2=c2):
            t.wait_ge(sV.h, vs_comb)
            t.wait_ge(sTW.h, tr_wres)
            if c2 >= 1:
                t.wait_ge(sV.h, ffn2_evac_marks[-1])
            for q in range(4):
                t.matmul(PM[q][:, 0:512], onesb[:], zrow[:], start=True,
                         stop=True)
            ins = None
            for kg in range(8):
                for og in range(8):
                    ins = t.matmul(pmslot(og),
                                   projT[:, kg, og * 128:(og + 1) * 128],
                                   combbf[:, kg, :], start=False,
                                   stop=(kg == 7), skip_group_check=True)
            sPE.inc(ins)
            sPE.inc(t.matmul(P5[:, 0:2], onesb[:], zrow[0:1, 0:2], start=True,
                             stop=True))
        SEC("tensor", _)
        pe_proj = sPE.n

        if DBG and c2 == 0:
            def _(v, pe_proj=pe_proj):
                v.wait_ge(sPE.h, pe_proj)
                v.tensor_copy(sqt[:, 0:2, :].rearrange("p a b -> p (a b)"),
                              PM[3][:, 0:512])
                sV.inc(v.tensor_copy(sqt[:, 2:4, :].rearrange("p a b -> p (a b)"),
                                     PM[2][:, 0:512]))
            SEC("vector", _)
            def _(g_):
                g_.wait_ge(sV.h, sV.n)
                g_.dma_start(out=dbg_ps, in_=sqt[:, 0:4, :].rearrange(
                    "p a b -> p (a b)")).then_inc(sG.h, 16)
                g_.dma_start(out=dbg_pjt[:, 0:1024], in_=projT[:, 0, :]
                             ).then_inc(sG.h, 16)
                g_.dma_start(out=dbg_pjt[:, 1024:2048], in_=projT[:, 7, :]
                             ).then_inc(sG.h, 16)
                sG.n += 48
                _dbgmarks.append(sG.n)
            SEC("gpsimd", _)
        def _(v, pe_proj=pe_proj, c2=c2):
            v.wait_ge(sPE.h, pe_proj)
            if DBG and c2 == 0:
                v.wait_ge(sG.h, _dbgmarks[-1])
            ins = None
            for og in range(8):
                ins = v.tensor_scalar(comb[:, og, :], pmslot(og),
                                      pbt[:, og:og + 1], None, ALU.add)
            sV.inc(ins)
        SEC("vector", _)
        sc_attn = sV.n

        def fm_ln(buf, marker, gwait=None):
            def _(s, marker=marker, gwait=gwait):
                s.wait_ge(*marker)
                if gwait is not None:
                    s.wait_ge(sG.h, gwait)
                sSc.inc(s.activation(sqt[:].rearrange("p a b -> p (a b)"),
                                     buf[:].rearrange("p a b -> p (a b)"),
                                     AF.Square))
            SEC("scalar", _)
            sc_sq = sSc.n

            def _(t, sc_sq=sc_sq):
                t.wait_ge(sSc.h, sc_sq)
                for og in range(8):
                    t.matmul(PBC[0:1, 0:256], onescol[:], buf[:, og, :],
                             start=(og == 0), stop=(og == 7))
                ins = None
                for og in range(8):
                    ins = t.matmul(PBC[0:1, 256:512], onescol[:], sqt[:, og, :],
                                   start=(og == 0), stop=(og == 7))
                sPE.inc(ins)
            SEC("tensor", _)
            pe_st = sPE.n

            def _(v, pe_st=pe_st):
                v.wait_ge(sPE.h, pe_st)
                vedge(v, v.tensor_scalar(m_r[:], PBC[0:1, 0:256], 1.0 / DM,
                                         None, ALU.mult))
                vedge(v, v.tensor_scalar(v_r[:], PBC[0:1, 256:512], 1.0 / DM,
                                         None, ALU.mult))
                vedge(v, v.tensor_tensor(t_r[:], m_r[:], m_r[:], ALU.mult))
                vedge(v, v.tensor_tensor(v_r[:], v_r[:], t_r[:], ALU.subtract))
                sV.inc(v.tensor_scalar(v_r[:], v_r[:], 1.0, EPS, ALU.mult,
                                       ALU.add))
            SEC("vector", _)
            vs_st = sV.n

            def _(s, vs_st=vs_st):
                s.wait_ge(sV.h, vs_st)
                sedge(s, s.activation(v_r[:], v_r[:], AF.Ln))
                sSc.inc(s.activation(v_r[:], v_r[:], AF.Exp, scale=-0.5))
            SEC("scalar", _)
            vs_rs = sSc.n

            def _(t, vs_rs=vs_rs):
                t.wait_ge(sSc.h, vs_rs)
                t.matmul(PBC[:, 512:768], onesf[:], m_r[:], start=True,
                         stop=True)
                sPE.inc(t.matmul(PBC[:, 768:1024], onesf[:], v_r[:],
                                 start=True, stop=True))
            SEC("tensor", _)
            pe_bcst = sPE.n

            def _(v, pe_bcst=pe_bcst):
                v.wait_ge(sPE.h, pe_bcst)
                mb = PBC[:, 512:768].unsqueeze(1).broadcast_to([128, 8, 256])
                rb = PBC[:, 768:1024].unsqueeze(1).broadcast_to([128, 8, 256])
                v.tensor_tensor(buf[:], buf[:], mb, ALU.subtract)
                sV.inc(v.tensor_tensor(buf[:], buf[:], rb, ALU.mult))
            SEC("vector", _)
            return (sV.h, sV.n)

        ln1_done = fm_ln(comb, (sV.h, sc_attn))

        h2ev = None
        for i in range(8):
            def _(t, i=i, ld_h2=ld_h2, h2ev=h2ev):
                t.wait_ge(sH2.h, ld_h2)
                if h2ev is not None:
                    t.wait_ge(sV.h, h2ev)
                tt = i // 4
                kk0 = (i % 4) * 2
                t.transpose(P5[:, 0:128],
                            hid2t[:, tt, kk0 * 128:kk0 * 128 + 128], eyef_t[:])
                sPE.inc(t.transpose(P5[:, 128:256],
                                    hid2t[:, tt, kk0 * 128 + 128:kk0 * 128 + 256],
                                    eyef_t[:]))
            SEC("tensor", _)
            mype = sPE.n

            def _(v, i=i, mype=mype):
                v.wait_ge(sPE.h, mype)
                tt = i // 4
                kk0 = (i % 4) * 2
                v.tensor_copy(hid2fm[:, kk0, tt * 128:(tt + 1) * 128],
                              P5[:, 0:128])
                sV.inc(v.tensor_copy(hid2fm[:, kk0 + 1, tt * 128:(tt + 1) * 128],
                                     P5[:, 128:256]))
            SEC("vector", _)
            h2ev = sV.n
        vs_h2fm = sV.n

        if DBG and c2 == 0:
            def _(g_, ln1_done=ln1_done, vs_h2fm=vs_h2fm):
                g_.wait_ge(*ln1_done)
                g_.wait_ge(sV.h, vs_h2fm)
                g_.dma_start(out=dbg_attn, in_=comb[:].rearrange("p a b -> p (a b)")).then_inc(sG.h, 16)
                g_.dma_start(out=dbg_h2fm, in_=hid2fm[:].rearrange("p a b -> p (a b)")).then_inc(sG.h, 16)
                sG.n += 32
                _dbgmarks.append(sG.n)
            SEC("gpsimd", _)
        def _(v, ln1_done=ln1_done, vs_h2fm=vs_h2fm, c2=c2):
            v.wait_ge(*ln1_done)
            v.wait_ge(sV.h, vs_h2fm)
            if DBG and c2 == 0:
                v.wait_ge(sG.h, _dbgmarks[-1])
            v.tensor_tensor(x1[:], hid2fm[:], comb[:], ALU.add)
            sV.inc(v.tensor_copy(comb[:], x1[:]))
        SEC("vector", _)
        ln2_done = fm_ln(comb, (sV.h, sV.n))

        def _(v, ln2_done=ln2_done):
            v.wait_ge(*ln2_done)
            sV.inc(v.tensor_copy(x2b[:], comb[:]))
        SEC("vector", _)
        vs_x2b = sV.n
        if DBG and c2 == 0:
            def _(g_, vs_x2b=vs_x2b):
                g_.wait_ge(sV.h, vs_x2b)
                g_.dma_start(out=dbg_x1, in_=x1[:].rearrange("p a b -> p (a b)")).then_inc(sG.h, 16)
                g_.dma_start(out=dbg_x2b, in_=x2b[:].rearrange("p a b -> p (a b)")).then_inc(sG.h, 16)
                sG.n += 32
            SEC("gpsimd", _)

        ffn1_evac = []
        for og4 in range(8):
            def _(t, og4=og4, vs_x2b=vs_x2b):
                t.wait_ge(sV.h, vs_x2b)
                if og4 >= 2:
                    t.wait_ge(sSc.h, ffn1_evac[og4 - 2])
                ins = None
                for j in range(4):
                    og = og4 * 4 + j
                    sl = (og4 % 2) * 4 + j
                    for kg in range(8):
                        ins = t.matmul(pmslot(sl),
                                       W1T[:, kg, og * 128:(og + 1) * 128],
                                       x2b[:, kg, :], start=(kg == 0),
                                       stop=(kg == 7))
                sPE.inc(ins)
            SEC("tensor", _)
            mype = sPE.n

            def _(s, og4=og4, mype=mype):
                s.wait_ge(sPE.h, mype)
                ins = None
                for j in range(4):
                    og = og4 * 4 + j
                    sl = (og4 % 2) * 4 + j
                    ins = s.activation(h1b[:, og, :], pmslot(sl), AF.Gelu,
                                       bias=b1t[:, og:og + 1])
                sSc.inc(ins)
            SEC("scalar", _)
            ffn1_evac.append(sSc.n)
        h1_done = ffn1_evac[-1]

        for kg in range(32):
            def _(sync, kg=kg, c2=c2):
                if c2 == 0 and kg == 0:
                    sync.wait_ge(sGW2.h, tern_last["w2"])
                if w2_mm:
                    sync.wait_ge(sPE.h, w2_mm[-1])
                sTR.inc(sync.dma_start(out=W2Tp[:, kg % 2, :],
                                       in_=w2_bf[:, kg * 128:(kg + 1) * 128],
                                       transpose=True), 16)
            SEC("sync", _)
            mytr = sTR.n

            def _(t, kg=kg, mytr=mytr, h1_done=h1_done):
                t.wait_ge(sTR.h, mytr)
                if kg == 0:
                    t.wait_ge(sSc.h, h1_done)
                    for q in range(4):
                        t.matmul(PM[q][:, 0:512], onesb[:], zrow[:],
                                 start=True, stop=True)
                ins = None
                for og in range(8):
                    ins = t.matmul(pmslot(og),
                                   W2Tp[:, kg % 2, og * 128:(og + 1) * 128],
                                   h1b[:, kg, :], start=False,
                                   stop=(kg == 31), skip_group_check=True)
                sPE.inc(ins)
                w2_mm.append(sPE.n)
            SEC("tensor", _)
        pe_ffn2 = sPE.n

        def _(v, pe_ffn2=pe_ffn2):
            v.wait_ge(sPE.h, pe_ffn2)
            ins = None
            for og in range(8):
                ins = v.tensor_scalar(comb[:, og, :], pmslot(og),
                                      b2t[:, og:og + 1], None, ALU.add)
            sV.inc(ins)
        SEC("vector", _)
        sc_ffn = sV.n
        ffn2_evac_marks.append(sc_ffn)
        if DBG and c2 == 0:
            def _(g_, sc_ffn=sc_ffn):
                g_.wait_ge(sV.h, sc_ffn)
                g_.dma_start(out=dbg_h1, in_=h1b[:].rearrange("p a b -> p (a b)")).then_inc(sG.h, 16)
                g_.dma_start(out=dbg_ffn, in_=comb[:].rearrange("p a b -> p (a b)")).then_inc(sG.h, 16)
                sG.n += 32
                _dbgmarks.append(sG.n)
            SEC("gpsimd", _)
            _ffn_dump = sG.n
        else:
            _ffn_dump = None

        ln3_done = fm_ln(comb, (sV.h, sc_ffn), gwait=_ffn_dump)

        def _(v, ln3_done=ln3_done):
            v.wait_ge(*ln3_done)
            sV.inc(v.tensor_tensor(hid2fm[:], x1[:], comb[:], ALU.add))
        SEC("vector", _)
        vs_out = sV.n

        oev = None
        for i in range(8):
            def _(t, i=i, vs_out=vs_out, oev=oev):
                t.wait_ge(sV.h, vs_out)
                if oev is not None:
                    t.wait_ge(sV.h, oev)
                tt = i // 4
                kk0 = (i % 4) * 2
                t.transpose(P5[:, 0:128],
                            hid2fm[:, kk0, tt * 128:(tt + 1) * 128], eyef_t[:])
                sPE.inc(t.transpose(P5[:, 128:256],
                                    hid2fm[:, kk0 + 1, tt * 128:(tt + 1) * 128],
                                    eyef_t[:]))
            SEC("tensor", _)
            mype = sPE.n

            def _(v, i=i, mype=mype):
                v.wait_ge(sPE.h, mype)
                tt = i // 4
                kk0 = (i % 4) * 2
                v.tensor_copy(hid2t[:, tt, kk0 * 128:(kk0 + 1) * 128],
                              P5[:, 0:128])
                sV.inc(v.tensor_copy(hid2t[:, tt, (kk0 + 1) * 128:(kk0 + 2) * 128],
                                     P5[:, 128:256]))
            SEC("vector", _)
            oev = sV.n
        vs_otok = sV.n
        prev_combfree = vs_otok

        def _(sync, c2=c2, vs_otok=vs_otok):
            sync.wait_ge(sV.h, vs_otok)
            sST.inc(sync.dma_start(
                out=out_d[c2 * T2C:(c2 + 1) * T2C, :].rearrange(
                    "(a p) d -> p a d", p=128),
                in_=hid2t[:]), 16)
            store_marks.append(sST.n)
        SEC("sync", _)

    def _(sync):
        sync.wait_ge(sST.h, sST.n)
    SEC("sync", _)

    es2.close()
    es.close()
    return nc


_CACHED = {}


def kernel(**inputs):
    if "nc" not in _CACHED:
        _CACHED["nc"] = build()
    nc = _CACHED["nc"]

    hidden = inputs["L0_attn_hidden_states"]
    eyeb = np.eye(128, dtype=ml_dtypes.bfloat16)
    eyef = np.eye(128, dtype=np.float32)
    in_maps = []
    for c in range(8):
        b, tr = c // 2, c % 2
        p = "fast_" if tr == 0 else "slow_"
        m = dict(
            hid=np.ascontiguousarray(hidden[b]),
            hid2=np.ascontiguousarray(hidden[b, tr * HALF:(tr + 1) * HALF, :]),
            inw=inputs[f"L0_attn_{p}in_w"],
            convw=inputs[f"L0_attn_{p}conv_w"],
            convb=inputs[f"L0_attn_{p}conv_b"],
            xw=inputs[f"L0_attn_{p}x_w"],
            dtpw=inputs[f"L0_attn_{p}dtp_w"],
            dtpb=inputs[f"L0_attn_{p}dtp_b"],
            alog=inputs[f"L0_attn_{p}A_log"],
            dvec=inputs[f"L0_attn_{p}D"],
            moutw=inputs[f"L0_attn_{p}mout_w"],
            normw=inputs["L0_attn_norm_w"],
            normb=inputs["L0_attn_norm_b"],
            trackw=np.ascontiguousarray(
                inputs["L0_attn_track_weights"]).reshape(1, 2),
            projw=inputs["L0_attn_proj_w"],
            projb=inputs["L0_attn_proj_b"],
            w1=inputs["L0_ffn_W1"],
            b1=inputs["L0_ffn_b1"],
            w2=inputs["L0_ffn_W2"],
            b2=inputs["L0_ffn_b2"],
            eyeb=eyeb,
            eyef=eyef,
            part=np.array([[tr, 0]], dtype=np.uint32),
            selm=np.repeat(np.eye(32, dtype=ml_dtypes.bfloat16), 128, axis=1),
        )
        in_maps.append(m)

    res = run_bass_kernel_spmd(nc, in_maps, core_ids=list(range(8)))
    out = np.empty((B, L, DM), dtype=np.float32)
    for c in range(8):
        b, tr = c // 2, c % 2
        out[b, tr * HALF:(tr + 1) * HALF, :] = np.asarray(res.results[c]["out"])
    return out


# version 14
# speedup vs baseline: 1.1144x; 1.1144x over previous
import numpy as np
import ml_dtypes
import concourse.bass as bass
import concourse.mybir as mybir
from concourse.bass_utils import run_bass_kernel_spmd
from contextlib import ExitStack

F32 = mybir.dt.float32
BF16 = mybir.dt.bfloat16
U32 = mybir.dt.uint32
AF = mybir.ActivationFunctionType
ALU = mybir.AluOpType
AX = mybir.AxisListType.X

B, L, DM, DI, NS, RK = 4, 4096, 1024, 2048, 16, 64
TC = 256
NCH = L // TC
HALF = L // 2
T2C = 256
N2CH = HALF // T2C
SEG = TC + 1
EPS = 1e-5
FF = 4 * DM
DBG = False


class SemCtr:
    def __init__(self, nc, es, name):
        self.h = es.enter_context(nc.semaphore(name))
        self.n = 0

    def inc(self, instr, k=1):
        instr.then_inc(self.h, k)
        self.n += k
        return self.n


def build():
    nc = bass.Bass("TRN2", target_bir_lowering=False)
    ap = lambda t: t.ap()

    hid_in = ap(nc.dram_tensor("hid", [L, DM], F32, kind="ExternalInput"))
    hid2_in = ap(nc.dram_tensor("hid2", [HALF, DM], F32, kind="ExternalInput"))
    inw_in = ap(nc.dram_tensor("inw", [2 * DI, DM], F32, kind="ExternalInput"))
    convw_in = ap(nc.dram_tensor("convw", [DI, 4], F32, kind="ExternalInput"))
    convb_in = ap(nc.dram_tensor("convb", [DI], F32, kind="ExternalInput"))
    xw_in = ap(nc.dram_tensor("xw", [96, DI], F32, kind="ExternalInput"))
    dtpw_in = ap(nc.dram_tensor("dtpw", [DI, RK], F32, kind="ExternalInput"))
    dtpb_in = ap(nc.dram_tensor("dtpb", [DI], F32, kind="ExternalInput"))
    alog_in = ap(nc.dram_tensor("alog", [DI, NS], F32, kind="ExternalInput"))
    dvec_in = ap(nc.dram_tensor("dvec", [DI], F32, kind="ExternalInput"))
    moutw_in = ap(nc.dram_tensor("moutw", [DM, DI], F32, kind="ExternalInput"))
    normw_in = ap(nc.dram_tensor("normw", [DM], F32, kind="ExternalInput"))
    normb_in = ap(nc.dram_tensor("normb", [DM], F32, kind="ExternalInput"))
    trackw_in = ap(nc.dram_tensor("trackw", [1, 2], F32, kind="ExternalInput"))
    projw_in = ap(nc.dram_tensor("projw", [DM, DM], F32, kind="ExternalInput"))
    projb_in = ap(nc.dram_tensor("projb", [DM], F32, kind="ExternalInput"))
    w1_in = ap(nc.dram_tensor("w1", [FF, DM], F32, kind="ExternalInput"))
    b1_in = ap(nc.dram_tensor("b1", [FF], F32, kind="ExternalInput"))
    w2_in = ap(nc.dram_tensor("w2", [DM, FF], F32, kind="ExternalInput"))
    b2_in = ap(nc.dram_tensor("b2", [DM], F32, kind="ExternalInput"))
    eyeb_in = ap(nc.dram_tensor("eyeb", [128, 128], BF16, kind="ExternalInput"))
    eyef_in = ap(nc.dram_tensor("eyef", [128, 128], F32, kind="ExternalInput"))
    part_in = ap(nc.dram_tensor("part", [1, 2], U32, kind="ExternalInput"))
    selm_in = ap(nc.dram_tensor("selm", [32, 4096], BF16, kind="ExternalInput"))

    out_d = ap(nc.dram_tensor("out", [HALF, DM], F32, kind="ExternalOutput"))
    if DBG:
        dbg_shx = ap(nc.dram_tensor("dbg_shx", [DM, L], F32, kind="ExternalOutput"))
        dbg_hnfm = ap(nc.dram_tensor("dbg_hnfm", [128, 8 * TC], BF16, kind="ExternalOutput"))
        dbg_u = ap(nc.dram_tensor("dbg_u", [128, 16 * TC], BF16, kind="ExternalOutput"))
        dbg_sz = ap(nc.dram_tensor("dbg_sz", [128, 16 * TC], BF16, kind="ExternalOutput"))
        dbg_dtin = ap(nc.dram_tensor("dbg_dtin", [64, TC], BF16, kind="ExternalOutput"))
        dbg_bbc = ap(nc.dram_tensor("dbg_bbc", [128, 16 * TC], BF16, kind="ExternalOutput"))
        dbg_cbc = ap(nc.dram_tensor("dbg_cbc", [128, 16 * TC], BF16, kind="ExternalOutput"))
        dbg_a = ap(nc.dram_tensor("dbg_a", [128, NS * SEG], F32, kind="ExternalOutput"))
        dbg_h = ap(nc.dram_tensor("dbg_h", [128, NS * SEG], BF16, kind="ExternalOutput"))
        dbg_y = ap(nc.dram_tensor("dbg_y", [128, TC], F32, kind="ExternalOutput"))
        dbg_dtg = ap(nc.dram_tensor("dbg_dtg", [128, TC], F32, kind="ExternalOutput"))
        dbg_mo = ap(nc.dram_tensor("dbg_mo", [128, 8 * TC], F32, kind="ExternalOutput"))
        dbg_hid = ap(nc.dram_tensor("dbg_hid", [128, 2 * 1024], F32, kind="ExternalOutput"))
        dbg_hnbf = ap(nc.dram_tensor("dbg_hnbf", [128, 1024], BF16, kind="ExternalOutput"))
        dbg_nwb = ap(nc.dram_tensor("dbg_nwb", [128, 1024], F32, kind="ExternalOutput"))
        dbg_nbb = ap(nc.dram_tensor("dbg_nbb", [128, 1024], F32, kind="ExternalOutput"))
        dbg_inwt = ap(nc.dram_tensor("dbg_inwt", [128, 8 * 4096], BF16, kind="ExternalOutput"))
        dbg_xbf = ap(nc.dram_tensor("dbg_xbf", [128, 16 * (TC + 3)], BF16, kind="ExternalOutput"))
        dbg_lns = ap(nc.dram_tensor("dbg_lns", [128, 8], F32, kind="ExternalOutput"))
        dbg_lnt = ap(nc.dram_tensor("dbg_lnt", [128, 1024], F32, kind="ExternalOutput"))
        dbg_comb = ap(nc.dram_tensor("dbg_comb", [128, 8 * T2C], F32, kind="ExternalOutput"))
        dbg_attn = ap(nc.dram_tensor("dbg_attn", [128, 8 * T2C], F32, kind="ExternalOutput"))
        dbg_x1 = ap(nc.dram_tensor("dbg_x1", [128, 8 * T2C], F32, kind="ExternalOutput"))
        dbg_x2b = ap(nc.dram_tensor("dbg_x2b", [128, 8 * T2C], BF16, kind="ExternalOutput"))
        dbg_h1 = ap(nc.dram_tensor("dbg_h1", [128, 32 * T2C], BF16, kind="ExternalOutput"))
        dbg_ffn = ap(nc.dram_tensor("dbg_ffn", [128, 8 * T2C], F32, kind="ExternalOutput"))
        dbg_h2fm = ap(nc.dram_tensor("dbg_h2fm", [128, 8 * T2C], F32, kind="ExternalOutput"))
        dbg_ps = ap(nc.dram_tensor("dbg_ps", [128, 1024], F32, kind="ExternalOutput"))
        dbg_pjt = ap(nc.dram_tensor("dbg_pjt", [128, 2048], BF16, kind="ExternalOutput"))

    inw_bf = ap(nc.dram_tensor("inw_bf", [2 * DI, DM], BF16, kind="Internal"))
    inwT_d = ap(nc.dram_tensor("inwT_d", [DM, 2 * DI], BF16, kind="Internal"))
    mout_bf = ap(nc.dram_tensor("mout_bf", [DM, DI], BF16, kind="Internal"))
    proj_bf = ap(nc.dram_tensor("proj_bf", [DM, DM], BF16, kind="Internal"))
    w1_bf = ap(nc.dram_tensor("w1_bf", [FF, DM], BF16, kind="Internal"))
    w2_bf = ap(nc.dram_tensor("w2_bf", [DM, FF], BF16, kind="Internal"))

    shx = ap(nc.dram_tensor("shx", [2, DM, L], F32, kind="Internal",
                            addr_space="Shared"))
    barin_d = ap(nc.dram_tensor("barin_d", [1, 2], F32, kind="Internal"))
    barout_d = ap(nc.dram_tensor("barout_d", [1, 2], F32, kind="Internal"))

    es = ExitStack()
    SB = lambda name, shape, dt: es.enter_context(nc.sbuf_tensor(name, shape, dt))

    A_t = SB("A_t", [128, NS * NS], F32)
    D_t = SB("D_t", [128, 16], F32)
    convw_t = SB("convw_t", [128, 16, 4], F32)
    convb_t = SB("convb_t", [128, 16], F32)
    dtpb_t = SB("dtpb_t", [128, 16], F32)
    nw_b = SB("nw_b", [128, DM], F32)
    nb_b = SB("nb_b", [128, DM], F32)
    cone = SB("cone", [128, 1], F32)
    onesb = SB("onesb", [1, 128], BF16)
    zrow = SB("zrow", [1, 512], BF16)
    onesf = SB("onesf", [1, 128], F32)
    onescol = SB("onescol", [128, 1], F32)
    eyeb_t = SB("eyeb_t", [128, 128], BF16)
    eyef_t = SB("eyef_t", [128, 128], F32)
    xwT = SB("xwT", [128, 16, 96], BF16)
    dtpwT = SB("dtpwT", [65, 16, 128], BF16)
    states = SB("states", [128, 16, 16], F32)
    wsc = SB("wsc", [128, 1], F32)
    tr48 = SB("tr48", [48, 128], F32)
    selm_t = SB("selm_t", [32, 4096], BF16)
    tagt = SB("tagt", [1, 2], U32)
    ftile = SB("ftile", [1, 2], U32)
    part_t = SB("part_t", [1, 2], U32)
    dmy1 = SB("dmy1", [1, 2], U32)
    dmy2 = SB("dmy2", [1, 2], U32)
    tw_t = SB("tw_t", [1, 2], F32)
    sm_t = SB("sm_t", [1, 2], F32)
    sm_s = SB("sm_s", [1, 1], F32)
    wsel = SB("wsel", [1, 1], F32)
    tmpdtu = SB("tmpdtu", [128, TC], BF16)
    yg = SB("yg", [128, TC], F32)
    udg = SB("udg", [128, TC], F32)
    dt_in = SB("dt_in", [65, TC], BF16)
    bctok = SB("bctok", [128, 64], BF16)
    btt = SB("btt", [32, 256], BF16)
    lnstat = SB("lnstat", [128, 8], F32)

    PBC = es.enter_context(nc.psum_tensor("PBC", [128, 1024], F32))
    PM = [es.enter_context(nc.psum_tensor(f"PM{i}", [128, 512], F32))
          for i in range(4)]
    P5 = es.enter_context(nc.psum_tensor("P5", [128, 512], F32))
    PTR = es.enter_context(nc.psum_tensor("PTR", [128, 256], BF16))

    def pmslot(sl):
        return PM[sl // 2][:, (sl % 2) * 256:(sl % 2) * 256 + 256]

    sLD = SemCtr(nc, es, "sLD")
    sLa = SemCtr(nc, es, "sLa")
    sLb = SemCtr(nc, es, "sLb")
    sTa = SemCtr(nc, es, "sTa")
    sTb = SemCtr(nc, es, "sTb")
    sTc = SemCtr(nc, es, "sTc")
    sTd = SemCtr(nc, es, "sTd")
    sTW = SemCtr(nc, es, "sTW")
    sLB = SemCtr(nc, es, "sLB")
    sH2 = SemCtr(nc, es, "sH2")
    sGI = SemCtr(nc, es, "sGI")
    sGT = SemCtr(nc, es, "sGT")
    sGM = SemCtr(nc, es, "sGM")
    sGX = SemCtr(nc, es, "sGX")
    sGC = SemCtr(nc, es, "sGC")
    sGP = SemCtr(nc, es, "sGP")
    sGW1 = SemCtr(nc, es, "sGW1")
    sGW2 = SemCtr(nc, es, "sGW2")
    sDT = SemCtr(nc, es, "sDT")
    sBG = SemCtr(nc, es, "sBG")
    sTR = SemCtr(nc, es, "sTR")
    sG = SemCtr(nc, es, "sG")
    sPE = SemCtr(nc, es, "sPE")
    sV = SemCtr(nc, es, "sV")
    sSc = SemCtr(nc, es, "sSc")
    sPoll = SemCtr(nc, es, "sPoll")
    sST = SemCtr(nc, es, "sST")
    sLWa = SemCtr(nc, es, "sLWa")
    sLWb = SemCtr(nc, es, "sLWb")
    sBAR = SemCtr(nc, es, "sBAR")
    sVE = SemCtr(nc, es, "sVE")
    sSE = SemCtr(nc, es, "sSE")

    def vedge(v, instr):
        instr.then_inc(sVE.h, 1)
        sVE.n += 1
        v.wait_ge(sVE.h, sVE.n)

    def sedge(s, instr):
        instr.then_inc(sSE.h, 1)
        sSE.n += 1
        s.wait_ge(sSE.h, sSE.n)

    blk = es.enter_context(nc.Block())

    def SEC(name, body):
        getattr(blk, name)(body)

    def _(sync):
        for dst, src in [
            (part_t[:], part_in),
            (tagt[:], hid_in[0:1, 0:2].bitcast(U32)),
            (eyeb_t[:], eyeb_in),
            (eyef_t[:], eyef_in),
            (A_t[:].rearrange("p (g n) -> p g n", g=16),
             alog_in.rearrange("(g p) n -> p g n", p=128)),
            (convw_t[:], convw_in.rearrange("(g p) j -> p g j", p=128)),
            (tr48[0:16, :], convb_in.rearrange("(g p) -> g p", p=128)),
            (tr48[16:32, :], dtpb_in.rearrange("(g p) -> g p", p=128)),
            (tr48[32:48, :], dvec_in.rearrange("(g p) -> g p", p=128)),
            (tw_t[:], trackw_in),
            (selm_t[:], selm_in),
        ]:
            sLD.inc(sync.dma_start(out=dst, in_=src), 16)
    SEC("sync", _)
    ld0 = sLD.n

    def _(v):
        v.wait_ge(sLD.h, ld0)
        v.memset(cone[:], 1.0)
        v.memset(onesb[:], 1.0)
        v.memset(zrow[:], 0.0)
        v.memset(onesf[:], 1.0)
        v.memset(onescol[:], 1.0)
        v.memset(states[:], 0.0)
        sV.inc(v.memset(ftile[:], 0))
    SEC("vector", _)

    def _(s):
        s.wait_ge(sLD.h, ld0)
        sedge(s, s.activation(A_t[:], A_t[:], AF.Exp))
        sSc.inc(s.mul(A_t[:], A_t[:], -1.0))
    SEC("scalar", _)
    sc_A = sSc.n

    def _(t):
        t.wait_ge(sLD.h, ld0)
        sPE.inc(t.transpose(PBC[0:128, 0:48], tr48[:], eyef_t[0:48, 0:48]))
    SEC("tensor", _)
    pe_tr = sPE.n

    def _(v):
        v.wait_ge(sPE.h, pe_tr)
        v.tensor_copy(convb_t[:], PBC[:, 0:16])
        v.tensor_copy(dtpb_t[:], PBC[:, 16:32])
        sV.inc(v.tensor_copy(D_t[:], PBC[:, 32:48]))
    SEC("vector", _)
    vs_tr = sV.n

    def _(v):
        vedge(v, v.reduce_max(sm_s[:], tw_t[:], axis=AX))
        sV.inc(v.tensor_scalar(sm_t[:], tw_t[:], sm_s[0:1, 0:1], None,
                               ALU.subtract))
    SEC("vector", _)
    vs_sm = sV.n

    def _(s):
        s.wait_ge(sV.h, vs_sm)
        sSc.inc(s.activation(sm_t[:], sm_t[:], AF.Exp))
    SEC("scalar", _)
    sc_sm = sSc.n

    def _(v):
        v.wait_ge(sSc.h, sc_sm)
        vedge(v, v.reduce_sum(sm_s[:], sm_t[:], axis=AX))
        vedge(v, v.reciprocal(sm_s[:], sm_s[:]))
        vedge(v, v.tensor_scalar(sm_t[:], sm_t[:], sm_s[0:1, 0:1], None,
                                 ALU.mult))
        parv = v.to_reg(v.value_load(part_t[0:1, 0:1]))
        with v.If_eq(parv, 0):
            v.tensor_copy(wsel[:], sm_t[0:1, 0:1]).then_inc(sV.h, 1)
        with v.Else():
            v.tensor_copy(wsel[:], sm_t[0:1, 1:2]).then_inc(sV.h, 1)
        v.end_ifs()
        sV.n += 1
    SEC("vector", _)
    vs_wsel = sV.n

    def _(t):
        t.wait_ge(sV.h, vs_wsel)
        sPE.inc(t.matmul(P5[:, 0:1], onesf[:], wsel[:], start=True, stop=True))
    SEC("tensor", _)
    pe_wsc = sPE.n

    def _(v):
        v.wait_ge(sPE.h, pe_wsc)
        sV.inc(v.tensor_copy(wsc[:], P5[:, 0:1]))
    SEC("vector", _)
    vs_wsc = sV.n

    es1 = ExitStack()
    SB1 = lambda name, shape, dt: es1.enter_context(
        nc.sbuf_tensor(name, shape, dt))
    hidt = SB1("hidt", [128, 2, 1024], F32)
    lnt = SB1("lnt", [128, 1024], F32)
    hn_bf = SB1("hn_bf", [128, 1024], BF16)
    hnfm = SB1("hnfm", [128, 8, TC], BF16)
    XBF = SB1("XBF", [128, 16, TC + 3], BF16)
    ACC = SB1("ACC", [128, 8, TC], F32)
    UBF = SB1("UBF", [128, 16, TC], BF16)
    SZ = SB1("SZ", [128, 16, TC], BF16)
    Bbc = SB1("Bbc", [128, 16, TC], BF16)
    Cbc = SB1("Cbc", [128, 16, TC], BF16)
    a_t = SB1("a_t", [128, NS * SEG], F32)
    bh_t = SB1("bh_t", [128, NS * SEG], BF16)
    h_t = SB1("h_t", [128, NS * SEG], BF16)
    gated = SB1("gated", [128, 2, TC], BF16)
    mo_fm = SB1("mo_fm", [128, 8, TC], F32)
    dtg3 = SB1("dtg3", [128, 16, TC], F32)
    sabs3 = SB1("sabs3", [128, 8, TC], F32)
    inwT = SB1("inwT", [128, 2, 8, 256], BF16)
    moutT = SB1("moutT", [128, 2, 1024], BF16)
    es1a = ExitStack()
    inwS = es1a.enter_context(nc.sbuf_tensor("inwS", [128, 4096], BF16))

    a3 = a_t[:].rearrange("p (n t) -> p n t", n=NS)
    b3 = bh_t[:].rearrange("p (n t) -> p n t", n=NS)
    h3 = h_t[:].rearrange("p (n t) -> p n t", n=NS)

    def _(v):
        v.memset(a_t[:], 0.0)
        v.memset(dt_in[64:65, :], 1.0)
        sV.inc(v.memset(XBF[:, :, 0:3], 0.0))
    SEC("vector", _)

    def _(sync):
        sync.wait_ge(sLD.h, ld0)
        sync.dma_start(out=lnt[0:1, :],
                       in_=normw_in.rearrange("(o d) -> o d", o=1)
                       ).then_inc(sLD.h, 16)
        sLD.n += 16
    SEC("sync", _)
    ld_nw = sLD.n

    def _(t):
        t.wait_ge(sLD.h, ld_nw)
        t.wait_ge(sV.h, vs_tr)
        t.matmul(PBC[:, 0:512], onesf[:], lnt[0:1, 0:512], start=True, stop=True)
        sPE.inc(t.matmul(PBC[:, 512:1024], onesf[:], lnt[0:1, 512:1024],
                         start=True, stop=True))
    SEC("tensor", _)
    pe_nw = sPE.n

    def _(v):
        v.wait_ge(sPE.h, pe_nw)
        sV.inc(v.tensor_copy(nw_b[:], PBC[:, 0:1024]))
    SEC("vector", _)
    vs_nw = sV.n

    def _(sync):
        sync.wait_ge(sV.h, vs_nw)
        sync.dma_start(out=lnt[0:1, :],
                       in_=normb_in.rearrange("(o d) -> o d", o=1)
                       ).then_inc(sLD.h, 16)
        sLD.n += 16
    SEC("sync", _)
    ld_nb = sLD.n

    def _(t):
        t.wait_ge(sLD.h, ld_nb)
        t.matmul(PBC[:, 0:512], onesf[:], lnt[0:1, 0:512], start=True, stop=True)
        sPE.inc(t.matmul(PBC[:, 512:1024], onesf[:], lnt[0:1, 512:1024],
                         start=True, stop=True))
    SEC("tensor", _)
    pe_nb = sPE.n

    def _(v):
        v.wait_ge(sPE.h, pe_nb)
        sV.inc(v.tensor_copy(nb_b[:], PBC[:, 0:1024]))
    SEC("vector", _)
    vs_nb = sV.n

    inw_cast = []
    inw_store = []
    for i in range(32):
        sLP = sLa if i % 2 == 0 else sLb

        def _(sync, i=i, sLP=sLP):
            if i >= 2:
                sync.wait_ge(sV.h, inw_cast[i - 2])
            sLP.inc(sync.dma_start(out=hidt[:, i % 2, :],
                                   in_=inw_in[i * 128:(i + 1) * 128, :]), 16)
        SEC("sync", _)
        myld = sLP.n

        def _(v, i=i, myld=myld, sLP=sLP):
            v.wait_ge(sLP.h, myld)
            if i >= 1:
                v.wait_ge(sGI.h, inw_store[i - 1])
            inw_cast.append(
                sV.inc(v.tensor_copy(hn_bf[:], hidt[:, i % 2, :])))
        SEC("vector", _)

        def _(g_, i=i):
            g_.wait_ge(sV.h, inw_cast[i])
            inw_store.append(
                sGI.inc(g_.dma_start(out=inw_bf[i * 128:(i + 1) * 128, :],
                                     in_=hn_bf[:]), 16))
        SEC("gpsimd", _)

    inwT_st = []
    for kg in range(8):
        def _(sync, kg=kg):
            if kg == 0:
                sync.wait_ge(sGI.h, inw_store[-1])
            else:
                sync.wait_ge(sGT.h, inwT_st[kg - 1])
            sTR.inc(sync.dma_start(out=inwS[:],
                                   in_=inw_bf[:, kg * 128:(kg + 1) * 128],
                                   transpose=True), 16)
        SEC("sync", _)
        mytr = sTR.n

        def _(g_, kg=kg, mytr=mytr):
            g_.wait_ge(sTR.h, mytr)
            inwT_st.append(
                sGT.inc(g_.dma_start(out=inwT_d[kg * 128:(kg + 1) * 128, :],
                                     in_=inwS[:]), 16))
        SEC("gpsimd", _)

    es1a.close()
    wstage = SB1("wstage", [128, 1024], F32)
    wstag2 = SB1("wstag2", [128, 1024], F32)
    wneg = SB1("wneg", [128, 1024], F32)
    wbf = SB1("wbf", [128, 1024], BF16)
    wbf2 = SB1("wbf2", [128, 1024], BF16)
    babs = SB1("babs", [128, 1], F32)
    tred = SB1("tred", [128, 1], F32)
    thr1 = SB1("thr1", [128, 1], F32)
    nthr1 = SB1("nthr1", [128, 1], F32)
    thr2 = SB1("thr2", [128, 1], F32)
    nthr2 = SB1("nthr2", [128, 1], F32)

    pk_load_marks = []
    pk_cons_marks = []
    pk_store = {"w1": [], "w2": [], "pj": []}
    tern_last = {}
    pj_store = pk_store["pj"]

    def pk_buf(i):
        return wstage if i % 2 == 0 else wstag2

    def emit_load(src_ap):
        i = len(pk_load_marks)
        sLP = sLWa if i % 2 == 0 else sLWb

        def _(sync, i=i, src_ap=src_ap, sLP=sLP):
            if i == 0:
                sync.wait_ge(sGT.h, inwT_st[-1])
            if i >= 2:
                sync.wait_ge(sV.h, pk_cons_marks[i - 2])
            sLP.inc(sync.dma_start(out=pk_buf(i)[:], in_=src_ap), 16)
        SEC("sync", _)
        pk_load_marks.append((sLP.h, sLP.n))
        return i

    def task_projcast(src_ap, dst_ap, ti):
        def emit(src_ap=src_ap, dst_ap=dst_ap, ti=ti):
            i = emit_load(src_ap)
            wb = wbf if ti % 2 == 0 else wbf2

            def _(v, i=i, ti=ti, wb=wb):
                v.wait_ge(*pk_load_marks[i])
                if ti >= 2:
                    v.wait_ge(sGP.h, pk_store["pj"][ti - 2])
                sV.inc(v.tensor_copy(wb[:], pk_buf(i)[:]))
            SEC("vector", _)
            pk_cons_marks.append(sV.n)
            myc = sV.n

            def _(g_, myc=myc, dst_ap=dst_ap, wb=wb):
                g_.wait_ge(sV.h, myc)
                pk_store["pj"].append(
                    sGP.inc(g_.dma_start(out=dst_ap, in_=wb[:]), 16))
            SEC("gpsimd", _)
        return emit

    def task_absred(src_ap, ti):
        def emit(src_ap=src_ap, ti=ti):
            i = emit_load(src_ap)

            def _(s, i=i):
                s.wait_ge(*pk_load_marks[i])
                sSc.inc(s.activation(pk_buf(i)[:], pk_buf(i)[:], AF.Abs))
            SEC("scalar", _)
            myab = sSc.n

            def _(v, i=i, ti=ti, myab=myab):
                v.wait_ge(sSc.h, myab)
                if ti == 0:
                    ins = v.reduce_sum(babs[:], pk_buf(i)[:], axis=AX)
                else:
                    vedge(v, v.reduce_sum(tred[:], pk_buf(i)[:], axis=AX))
                    ins = v.tensor_tensor(babs[:], babs[:], tred[:], ALU.add)
                sV.inc(ins)
            SEC("vector", _)
            pk_cons_marks.append(sV.n)
        return emit

    def task_thr(wname):
        def emit(wname=wname):
            def _(t):
                t.wait_ge(sV.h, pk_cons_marks[-1])
                sPE.inc(t.matmul(P5[0:1, 0:1], babs[:], onescol[:],
                                 start=True, stop=True))
            SEC("tensor", _)
            pe_tot = sPE.n

            def _(v, pe_tot=pe_tot):
                v.wait_ge(sPE.h, pe_tot)
                sV.inc(v.tensor_copy(babs[0:1, 0:1], P5[0:1, 0:1]))
            SEC("vector", _)
            vs_tot = sV.n

            def _(t, vs_tot=vs_tot):
                t.wait_ge(sV.h, vs_tot)
                sPE.inc(t.matmul(P5[:, 0:1], onesf[:], babs[0:1, 0:1],
                                 start=True, stop=True))
            SEC("tensor", _)
            pe_bc = sPE.n

            def _(v, pe_bc=pe_bc, wname=wname):
                v.wait_ge(sPE.h, pe_bc)
                th = thr1 if wname == "w1" else thr2
                nth = nthr1 if wname == "w1" else nthr2
                vedge(v, v.tensor_scalar(th[:], P5[:, 0:1], 1.0 / (FF * DM),
                                         1e-8, ALU.mult, ALU.max))
                vedge(v, v.tensor_scalar(th[:], th[:], 0.5, None, ALU.mult))
                sV.inc(v.tensor_scalar(nth[:], th[:], -1.0, None, ALU.mult))
            SEC("vector", _)
        return emit

    def task_tern(wname, src_ap, dst_ap, ti):
        sGW = sGW1 if wname == "w1" else sGW2

        def emit(wname=wname, src_ap=src_ap, dst_ap=dst_ap, ti=ti, sGW=sGW):
            i = emit_load(src_ap)
            wb = wbf if ti % 2 == 0 else wbf2
            th = thr1 if wname == "w1" else thr2
            nth = nthr1 if wname == "w1" else nthr2

            def _(v, i=i, ti=ti, wb=wb, th=th, nth=nth, sGW=sGW,
                  wname=wname):
                v.wait_ge(*pk_load_marks[i])
                if ti >= 2:
                    v.wait_ge(sGW.h, pk_store[wname][ti - 2])
                v.tensor_scalar(wneg[:], pk_buf(i)[:], nth[:, 0:1], None,
                                ALU.is_le)
                ins = v.scalar_tensor_tensor(wb[:], pk_buf(i)[:], th[:, 0:1],
                                             wneg[:], ALU.is_ge, ALU.subtract)
                ins.then_inc(sV.h, 1)
                sV.n += 1
            SEC("vector", _)
            pk_cons_marks.append(sV.n)
            myc = sV.n

            def _(g_, myc=myc, dst_ap=dst_ap, wb=wb, sGW=sGW, wname=wname,
                  ti=ti):
                g_.wait_ge(sV.h, myc)
                pk_store[wname].append(
                    sGW.inc(g_.dma_start(out=dst_ap, in_=wb[:]), 16))
                if ti == 31:
                    tern_last[wname] = pk_store[wname][-1]
            SEC("gpsimd", _)
        return emit

    prep_q = []
    for ti in range(8):
        prep_q.append(task_projcast(projw_in[ti * 128:(ti + 1) * 128, :],
                                    proj_bf[ti * 128:(ti + 1) * 128, :], ti))
    for ti in range(32):
        prep_q.append(task_absred(w1_in[ti * 128:(ti + 1) * 128, :], ti))
    prep_q.append(task_thr("w1"))
    for ti in range(32):
        ri, ci = ti // 4, ti % 4
        prep_q.append(task_absred(
            w2_in[ri * 128:(ri + 1) * 128, ci * 1024:(ci + 1) * 1024], ti))
    prep_q.append(task_thr("w2"))
    for ti in range(32):
        prep_q.append(task_tern("w1", w1_in[ti * 128:(ti + 1) * 128, :],
                                w1_bf[ti * 128:(ti + 1) * 128, :], ti))
    for ti in range(32):
        ri, ci = ti // 4, ti % 4
        sl2 = (slice(ri * 128, (ri + 1) * 128),
               slice(ci * 1024, (ci + 1) * 1024))
        prep_q.append(task_tern("w2", w2_in[sl2[0], sl2[1]],
                                w2_bf[sl2[0], sl2[1]], ti))

    hid_flat = hidt[:].rearrange("p a b -> p (a b)")
    mo_cast = []
    mo_store = []
    for i in range(8):
        def _(sync, i=i):
            if i == 0:
                sync.wait_ge(sV.h, inw_cast[-1])
            else:
                sync.wait_ge(sV.h, mo_cast[i - 1])
            sLD.inc(sync.dma_start(out=hid_flat,
                                   in_=moutw_in[i * 128:(i + 1) * 128, :]), 16)
        SEC("sync", _)
        myld = sLD.n

        def _(v, i=i, myld=myld):
            v.wait_ge(sLD.h, myld)
            v.wait_ge(sV.h, vs_wsc)
            if i == 0:
                v.wait_ge(sGI.h, inw_store[-1])
            else:
                v.wait_ge(sGM.h, mo_store[i - 1])
            mo_cast.append(
                sV.inc(v.tensor_scalar(h_t[:, 0:2048], hid_flat, wsc[:, 0:1], None,
                                       ALU.mult)))
        SEC("vector", _)

        def _(g_, i=i):
            g_.wait_ge(sV.h, mo_cast[i])
            mo_store.append(
                sGM.inc(g_.dma_start(out=mout_bf[i * 128:(i + 1) * 128, :],
                                     in_=h_t[:, 0:2048]), 16))
        SEC("gpsimd", _)

    def _(sync):
        sync.wait_ge(sV.h, mo_cast[-1])
        sync.dma_start(out=hidt[0:96, 0, :], in_=xw_in[:, 0:1024]
                       ).then_inc(sLD.h, 16)
        sync.dma_start(out=hidt[0:96, 1, :], in_=xw_in[:, 1024:2048]
                       ).then_inc(sLD.h, 16)
        sync.dma_start(out=lnt[:].rearrange("p (g r) -> p g r", g=16),
                       in_=dtpw_in.rearrange("(g p) r -> p g r", p=128)
                       ).then_inc(sLD.h, 16)
        sLD.n += 48
    SEC("sync", _)

    def _(g_):
        sDT.inc(g_.dma_start(out=dtpwT[64:65, :, :],
                             in_=dtpb_in.rearrange("(o g d) -> o g d",
                                                   o=1, g=16)), 16)
    SEC("gpsimd", _)
    ld_xw = sLD.n

    dtpw_bf = SZ[:, 0:4, :].rearrange("p a b -> p (a b)")
    def _(v):
        v.wait_ge(sLD.h, ld_xw)
        v.wait_ge(sDT.h, 16)
        v.wait_ge(sGM.h, mo_store[-1])
        v.tensor_copy(h_t[0:96, 0:1024], hidt[0:96, 0, :])
        v.tensor_copy(h_t[0:96, 1024:2048], hidt[0:96, 1, :])
        sV.inc(v.tensor_copy(dtpw_bf, lnt[:]))
    SEC("vector", _)
    vs_xw = sV.n

    prev_ev = vs_xw
    for g in range(16):
        def _(t, g=g, prev_ev=prev_ev):
            t.wait_ge(sV.h, prev_ev)
            tt, off = g // 8, (g % 8) * 128
            t.transpose(PTR[:, 0:96], h_t[0:96, tt * 1024 + off:
                                            tt * 1024 + off + 128],
                        eyeb_t[0:96, 0:96])
            sPE.inc(t.transpose(PTR[0:64, 96:224],
                                dtpw_bf[:, g * 64:(g + 1) * 64], eyeb_t[:]))
        SEC("tensor", _)
        mype = sPE.n

        def _(v, g=g, mype=mype):
            v.wait_ge(sPE.h, mype)
            v.tensor_copy(xwT[:, g, :], PTR[:, 0:96])
            sV.inc(v.tensor_copy(dtpwT[0:64, g, :], PTR[0:64, 96:224]))
        SEC("vector", _)
        prev_ev = sV.n

    hid_free = sV.n
    inproj_done = sPE.n
    prev_mout_done = sPE.n
    prev_shx_store = sGX.n
    _dbgmarks = []

    for c in range(NCH):
        def _(sync, c=c, hid_free=hid_free, inproj_done=inproj_done):
            sync.wait_ge(sV.h, hid_free)
            if c == 0:
                sync.wait_ge(sGI.h, inw_store[-1])
            sLD.inc(sync.dma_start(
                out=hidt[:],
                in_=hid_in[c * TC:(c + 1) * TC, :].rearrange(
                    "(a p) d -> p a d", p=128)), 16)
        SEC("sync", _)
        ld_hid = sLD.n

        if DBG and c == 0:
            def _(g_, ld_hid=ld_hid, tr_inw=tr_inw):
                g_.wait_ge(sLD.h, ld_hid)
                g_.wait_ge(sTR.h, tr_inw)
                g_.dma_start(out=dbg_hid, in_=hidt[:].rearrange("p a b -> p (a b)")).then_inc(sG.h, 16)
                g_.dma_start(out=dbg_inwt, in_=inwT[:].rearrange("p a b -> p (a b)")).then_inc(sG.h, 16)
                g_.dma_start(out=dbg_nwb, in_=nw_b[:]).then_inc(sG.h, 16)
                g_.dma_start(out=dbg_nbb, in_=nb_b[:]).then_inc(sG.h, 16)
                sG.n += 64
                _dbgmarks.append(sG.n)
            SEC("gpsimd", _)

        tp_evac = None
        for tt in range(2):
            def _(v, c=c, tt=tt, ld_hid=ld_hid):
                v.wait_ge(sLD.h, ld_hid)
                if DBG and c == 0 and tt == 0:
                    v.wait_ge(sG.h, _dbgmarks[-1])
                if tt == 0 and c > 0:
                    v.tensor_copy(XBF[:, :, 0:3], XBF[:, :, TC:TC + 3])
                vedge(v, v.reduce_sum(lnstat[:, 0:1], hidt[:, tt, :], axis=AX))
                vedge(v, v.tensor_scalar(lnstat[:, 0:1], lnstat[:, 0:1],
                                         1.0 / DM, None, ALU.mult))
                v.tensor_scalar(lnt[:], hidt[:, tt, :], lnstat[:, 0:1], None,
                                ALU.subtract)
                v.tensor_tensor(hidt[:, tt, :], lnt[:], lnt[:], ALU.mult)
                vedge(v, v.reduce_sum(lnstat[:, 1:2], hidt[:, tt, :], axis=AX))
                sV.inc(v.tensor_scalar(lnstat[:, 1:2], lnstat[:, 1:2],
                                       1.0 / DM, EPS, ALU.mult, ALU.add))
            SEC("vector", _)
            m1 = sV.n

            if DBG and c == 0 and tt == 0:
                def _(g_, m1=m1):
                    g_.wait_ge(sV.h, m1)
                    g_.dma_start(out=dbg_lns, in_=lnstat[:]).then_inc(sG.h, 16)
                    g_.dma_start(out=dbg_lnt, in_=lnt[:]).then_inc(sG.h, 16)
                    sG.n += 32
                    _dbgmarks.append(sG.n)
                SEC("gpsimd", _)

            def _(s, m1=m1, c=c, tt=tt):
                s.wait_ge(sV.h, m1)
                if DBG and c == 0 and tt == 0:
                    s.wait_ge(sG.h, _dbgmarks[-1])
                sedge(s, s.activation(lnstat[:, 1:2], lnstat[:, 1:2], AF.Ln))
                sSc.inc(s.activation(lnstat[:, 1:2], lnstat[:, 1:2], AF.Exp,
                                     scale=-0.5))
            SEC("scalar", _)
            m2 = sSc.n

            def _(v, tt=tt, m2=m2):
                v.wait_ge(sSc.h, m2)
                v.tensor_scalar(lnt[:], lnt[:], lnstat[:, 1:2], None, ALU.mult)
                vedge(v, v.reduce_sum(lnstat[:, 2:3], lnt[:], axis=AX))
                vedge(v, v.tensor_scalar(lnstat[:, 2:3], lnstat[:, 2:3],
                                         1.0 / DM, None, ALU.mult))
                v.tensor_scalar(lnt[:], lnt[:], lnstat[:, 2:3], None,
                                ALU.subtract)
                v.tensor_tensor(hidt[:, tt, :], lnt[:], lnt[:], ALU.mult)
                vedge(v, v.reduce_sum(lnstat[:, 3:4], hidt[:, tt, :], axis=AX))
                sV.inc(v.tensor_scalar(lnstat[:, 3:4], lnstat[:, 3:4],
                                       1.0 / DM, EPS, ALU.mult, ALU.add))
            SEC("vector", _)
            m3 = sV.n

            def _(s, m3=m3):
                s.wait_ge(sV.h, m3)
                sedge(s, s.activation(lnstat[:, 3:4], lnstat[:, 3:4], AF.Ln))
                sSc.inc(s.activation(lnstat[:, 3:4], lnstat[:, 3:4], AF.Exp,
                                     scale=-0.5))
            SEC("scalar", _)
            m4 = sSc.n

            def _(v, m4=m4):
                v.wait_ge(sSc.h, m4)
                v.scalar_tensor_tensor(lnt[:], lnt[:], lnstat[:, 3:4], nw_b[:],
                                       ALU.mult, ALU.mult)
                sV.inc(v.tensor_tensor(hn_bf[:], lnt[:], nb_b[:], ALU.add))
            SEC("vector", _)
            m5 = sV.n

            for i in range(4):
                def _(t, tt=tt, i=i, m5=m5, tp_evac=tp_evac):
                    t.wait_ge(sV.h, m5)
                    if tp_evac is not None:
                        t.wait_ge(sV.h, tp_evac)
                    kk0 = i * 2
                    t.transpose(PTR[:, 0:128],
                                hn_bf[:, kk0 * 128:kk0 * 128 + 128], eyeb_t[:])
                    sPE.inc(t.transpose(PTR[:, 128:256],
                                        hn_bf[:, kk0 * 128 + 128:kk0 * 128 + 256],
                                        eyeb_t[:]))
                SEC("tensor", _)
                mype = sPE.n

                def _(v, tt=tt, i=i, mype=mype):
                    v.wait_ge(sPE.h, mype)
                    kk0 = i * 2
                    v.tensor_copy(hnfm[:, kk0, tt * 128:(tt + 1) * 128],
                                  PTR[:, 0:128])
                    sV.inc(v.tensor_copy(
                        hnfm[:, kk0 + 1, tt * 128:(tt + 1) * 128],
                        PTR[:, 128:256]))
                SEC("vector", _)
                tp_evac = sV.n
        hid_free = sV.n
        vs_hnfm = sV.n

        evac_marks = []
        inw_rt_marks = []
        for rt2 in range(16):
            sTP = sTa if rt2 % 2 == 0 else sTb

            def _(sync, rt2=rt2, c=c, sTP=sTP):
                if c == 0 and rt2 == 0:
                    sync.wait_ge(sGT.h, inwT_st[-1])
                if len(inw_rt_marks) >= 2:
                    sync.wait_ge(sPE.h, inw_rt_marks[-2])
                sTP.inc(sync.dma_start(
                    out=inwT[:, rt2 % 2, :, :],
                    in_=inwT_d[:, rt2 * 256:(rt2 + 1) * 256].rearrange(
                        "(kg p) m -> p kg m", p=128)), 16)
            SEC("sync", _)
            inw_rt_marks.append(None)
            _my_tr = sTP.n

            def _(t, rt2=rt2, vs_hnfm=vs_hnfm, _my_tr=_my_tr, sTP=sTP):
                t.wait_ge(sV.h, vs_hnfm)
                t.wait_ge(sTP.h, _my_tr)
                if rt2 >= 2:
                    engh, cnt = evac_marks[rt2 - 2]
                    t.wait_ge(engh, cnt)
                ins = None
                for j in range(2):
                    rt = rt2 * 2 + j
                    slot = PBC[:, (rt % 4) * 256:(rt % 4) * 256 + 256]
                    for kg in range(8):
                        ins = t.matmul(slot,
                                       inwT[:, rt2 % 2, kg,
                                            j * 128:(j + 1) * 128],
                                       hnfm[:, kg, :], start=(kg == 0),
                                       stop=(kg == 7))
                sPE.inc(ins)
                inw_rt_marks[rt2] = sPE.n
            SEC("tensor", _)
            mype = sPE.n

            if rt2 < 8:
                def _(v, rt2=rt2, mype=mype):
                    v.wait_ge(sPE.h, mype)
                    ins = None
                    for j in range(2):
                        rt = rt2 * 2 + j
                        ins = v.tensor_copy(XBF[:, rt, 3:3 + TC],
                                            PBC[:, (rt % 4) * 256:(rt % 4) * 256 + 256])
                    sV.inc(ins)
                SEC("vector", _)
                evac_marks.append((sV.h, sV.n))
            else:
                def _(s, rt2=rt2, mype=mype):
                    s.wait_ge(sPE.h, mype)
                    ins = None
                    for j in range(2):
                        rt = rt2 * 2 + j
                        ins = s.activation(SZ[:, rt - 16, :],
                                           PBC[:, (rt % 4) * 256:(rt % 4) * 256 + 256],
                                           AF.Silu)
                    sSc.inc(ins)
                SEC("scalar", _)
                evac_marks.append((sSc.h, sSc.n))
        inproj_done = sPE.n
        x_ready = evac_marks[7][1]

        silu_marks = []
        for hh in range(2):
            def _(v, hh=hh, x_ready=x_ready):
                if hh == 0:
                    v.wait_ge(sV.h, x_ready)
                else:
                    v.wait_ge(sSc.h, silu_marks[0])
                ins = None
                for g8 in range(8):
                    g = 8 * hh + g8
                    acc = ACC[:, g8, :]
                    v.tensor_scalar(acc, XBF[:, g, 0:TC], convw_t[:, g, 0:1],
                                    convb_t[:, g:g + 1], ALU.mult, ALU.add)
                    for j in range(1, 4):
                        ins = v.scalar_tensor_tensor(acc, XBF[:, g, j:j + TC],
                                                     convw_t[:, g, j:j + 1],
                                                     acc, ALU.mult, ALU.add)
                sV.inc(ins)
            SEC("vector", _)
            myv = sV.n

            def _(s, hh=hh, myv=myv):
                s.wait_ge(sV.h, myv)
                silu_marks.append(
                    sSc.inc(s.activation(
                        UBF[:, 8 * hh:8 * hh + 8, :].rearrange(
                            "p a b -> p (a b)"),
                        ACC[:].rearrange("p a b -> p (a b)"), AF.Silu)))
            SEC("scalar", _)
        u_ready = silu_marks[-1]

        def _(t, u_ready=u_ready, c=c):
            t.wait_ge(sSc.h, u_ready)
            ins = None
            for kg in range(16):
                ins = t.matmul(P5[0:96, 0:256], xwT[:, kg, :], UBF[:, kg, :],
                               start=(kg == 0), stop=(kg == 15))
            sPE.inc(ins)
        SEC("tensor", _)
        pe_xdbl = sPE.n

        def _(v, pe_xdbl=pe_xdbl):
            v.wait_ge(sPE.h, pe_xdbl)
            sV.inc(v.tensor_copy(dt_in[0:64, :], P5[0:64, 0:256]))
        SEC("vector", _)
        vs_dtin = sV.n

        def _(t, u_ready=u_ready):
            ins = None
            for ts_ in range(2):
                for kg in range(16):
                    ins = t.matmul(PBC[:, ts_ * 32:(ts_ + 1) * 32],
                                   UBF[:, kg, ts_ * 128:(ts_ + 1) * 128],
                                   xwT[:, kg, 64:96],
                                   start=(kg == 0), stop=(kg == 15))
            sPE.inc(ins)
        SEC("tensor", _)
        pe_xT = sPE.n

        def _(v, pe_xT=pe_xT):
            v.wait_ge(sPE.h, pe_xT)
            sV.inc(v.tensor_copy(bctok[:], PBC[:, 0:64]))
        SEC("vector", _)
        vs_bctok = sV.n

        def _(t, vs_bctok=vs_bctok):
            t.wait_ge(sV.h, vs_bctok)
            t.transpose(PTR[0:32, 0:128], bctok[:, 0:32], eyeb_t[:])
            sPE.inc(t.transpose(PTR[0:32, 128:256], bctok[:, 32:64], eyeb_t[:]))
        SEC("tensor", _)
        pe_bt = sPE.n

        def _(v, pe_bt=pe_bt):
            v.wait_ge(sPE.h, pe_bt)
            sV.inc(v.tensor_copy(btt[:], PTR[0:32, :]))
        SEC("vector", _)
        vs_bt = sV.n

        bc_evac = []
        for r in range(8):
            def _(t, r=r, vs_bt=vs_bt):
                t.wait_ge(sV.h, vs_bt)
                if r > 0:
                    t.wait_ge(sV.h, bc_evac[r - 1])
                ins = None
                for j in range(4):
                    n = (r % 4) * 4 + j + (0 if r < 4 else 16)
                    ins = t.matmul(pmslot(j), selm_t[:, n * 128:(n + 1) * 128],
                                   btt[:], start=True, stop=True)
                sPE.inc(ins)
            SEC("tensor", _)
            mype = sPE.n

            def _(v, r=r, mype=mype):
                v.wait_ge(sPE.h, mype)
                n0 = (r % 4) * 4
                tgt = Bbc if r < 4 else Cbc
                v.tensor_copy(
                    tgt[:, n0:n0 + 2, :].rearrange("p a b -> p (a b)"),
                    PM[0][:, 0:512])
                sV.inc(v.tensor_copy(
                    tgt[:, n0 + 2:n0 + 4, :].rearrange("p a b -> p (a b)"),
                    PM[1][:, 0:512]))
            SEC("vector", _)
            bc_evac.append(sV.n)
        bc_done_v = bc_evac[-1]
        bc_done_s = sSc.n

        if DBG and c == 0:
            def _(g_, x_ready=x_ready, m5=m5):
                g_.wait_ge(sV.h, x_ready)
                g_.dma_start(out=dbg_hnbf, in_=hn_bf[:]).then_inc(sG.h, 16)
                g_.dma_start(out=dbg_xbf, in_=XBF[:].rearrange("p a b -> p (a b)")).then_inc(sG.h, 16)
                sG.n += 32
            SEC("gpsimd", _)
            def _(g_, vs_dtin=vs_dtin, bc_done_v=bc_done_v, bc_done_s=bc_done_s):
                g_.wait_ge(sV.h, vs_dtin)
                g_.wait_ge(sV.h, bc_done_v)
                g_.wait_ge(sSc.h, bc_done_s)
                g_.dma_start(out=dbg_hnfm, in_=hnfm[:].rearrange("p a b -> p (a b)")).then_inc(sG.h, 16)
                g_.dma_start(out=dbg_u, in_=UBF[:].rearrange("p a b -> p (a b)")).then_inc(sG.h, 16)
                g_.dma_start(out=dbg_sz, in_=SZ[:].rearrange("p a b -> p (a b)")).then_inc(sG.h, 16)
                g_.dma_start(out=dbg_dtin, in_=dt_in[:]).then_inc(sG.h, 16)
                g_.dma_start(out=dbg_bbc, in_=Bbc[:].rearrange("p a b -> p (a b)")).then_inc(sG.h, 16)
                g_.dma_start(out=dbg_cbc, in_=Cbc[:].rearrange("p a b -> p (a b)")).then_inc(sG.h, 16)
                sG.n += 96
            SEC("gpsimd", _)

        def dtp_slot2(k):
            return (P5[:, 0:512] if k % 2 == 0 else PM[0][:, 0:512])

        dtp_evac = []
        dtg_half = [None, None]
        for k in range(8):
            def _(t, k=k, vs_dtin=vs_dtin, bc_done_v=bc_done_v):
                if k == 0:
                    t.wait_ge(sV.h, vs_dtin)
                if k == 1:
                    t.wait_ge(sV.h, bc_done_v)
                if k >= 2:
                    t.wait_ge(sSc.h, dtp_evac[k - 2][0])
                    t.wait_ge(sV.h, dtp_evac[k - 2][1])
                sl = dtp_slot2(k)
                t.matmul(sl[:, 0:256], dtpwT[:, 2 * k, :], dt_in[:],
                         start=True, stop=True)
                sPE.inc(t.matmul(sl[:, 256:512], dtpwT[:, 2 * k + 1, :],
                                 dt_in[:], start=True, stop=True))
            SEC("tensor", _)
            mype = sPE.n

            def _(s, k=k, mype=mype):
                s.wait_ge(sPE.h, mype)
                if k == 4:
                    s.wait_ge(sV.h, dtg_half[0])
                sSc.inc(s.activation(
                    sabs3[:, (2 * k) % 8:(2 * k) % 8 + 2, :].rearrange(
                        "p a b -> p (a b)"),
                    dtp_slot2(k), AF.Abs))
            SEC("scalar", _)
            sc_abs = sSc.n

            def _(v, k=k, mype=mype):
                v.wait_ge(sPE.h, mype)
                sV.inc(v.tensor_scalar(
                    dtg3[:, 2 * k:2 * k + 2, :].rearrange("p a b -> p (a b)"),
                    dtp_slot2(k), 0.0, None, ALU.max))
            SEC("vector", _)
            dtp_evac.append((sc_abs, sV.n))

            if k % 4 == 3:
                sf = sabs3[:].rearrange("p a b -> p (a b)")
                g = 2 * k + 1

                def _(s, sf=sf):
                    sedge(s, s.activation(sf, sf, AF.Exp, scale=-1.0))
                    sSc.inc(s.activation(sf, sf, AF.Ln, bias=cone[:, 0:1]))
                SEC("scalar", _)
                sc_sp = sSc.n

                def _(v, g=g, sc_sp=sc_sp, sf=sf):
                    v.wait_ge(sSc.h, sc_sp)
                    h8 = dtg3[:, g - 7:g + 1, :].rearrange("p a b -> p (a b)")
                    sV.inc(v.tensor_tensor(h8, h8, sf, ALU.add))
                SEC("vector", _)
                dtg_half[k // 4] = sV.n

        A_t3 = A_t[:].rearrange("p (g n) -> p g n", g=16)

        def a3_build(v, g):
            ins = v.tensor_tensor(
                a3[:, :, 1:SEG],
                dtg3[:, g, :].unsqueeze(1).broadcast_to([128, NS, TC]),
                A_t3[:, g, :].unsqueeze(2).broadcast_to([128, NS, TC]),
                ALU.mult)
            ins.then_inc(sV.h, 1)
            sV.n += 1
            return sV.n

        def b3_build(v, g):
            v.tensor_tensor(tmpdtu[:], dtg3[:, g, :], UBF[:, g, :], ALU.mult)
            v.tensor_tensor(
                b3[:, :, 1:SEG],
                tmpdtu[:].unsqueeze(1).broadcast_to([128, NS, TC]),
                Bbc[:, :, :], ALU.mult)
            ins = v.tensor_copy(
                b3[:, :, 0:1],
                states[:, g:g + 1, :].rearrange("p o n -> p n o"))
            ins.then_inc(sV.h, 1)
            sV.n += 1
            return sV.n

        exp_marks = []
        a3_marks = {}
        gated_marks = []
        mm_g = []
        moutT_mark = []

        def _(v, bc_done_v=bc_done_v):
            v.wait_ge(sV.h, dtg_half[0])
            v.wait_ge(sV.h, bc_done_v)
            a3_build(v, 0)
        SEC("vector", _)
        a3_mark = sV.n

        def _(s, a3_mark=a3_mark):
            s.wait_ge(sV.h, a3_mark)
            sSc.inc(s.activation(a3[:, :, 1:SEG], a3[:, :, 1:SEG], AF.Exp))
        SEC("scalar", _)
        exp_marks.append(sSc.n)

        def _(v, bc_done_v=bc_done_v):
            v.wait_ge(sV.h, bc_done_v)
            b3_build(v, 0)
        SEC("vector", _)

        for g in range(16):
            sMP = sTc if g % 2 == 0 else sTd

            def _(sync, g=g, c=c, prev_mout_done=prev_mout_done, sMP=sMP):
                if g >= 2:
                    sync.wait_ge(sPE.h, mm_g[g - 2])
                elif c == 0:
                    sync.wait_ge(sGM.h, mo_store[-1])
                else:
                    sync.wait_ge(sPE.h, prev_mout_done)
                sMP.inc(sync.dma_start(
                    out=moutT[:, g % 2, :],
                    in_=mout_bf[:, g * 128:(g + 1) * 128],
                    transpose=True), 16)
                moutT_mark.append((sMP.h, sMP.n))
            SEC("sync", _)

            def _(v, g=g):
                v.wait_ge(sSc.h, exp_marks[g])
                if g >= 2:
                    v.wait_ge(sPE.h, mm_g[g - 2])
                vedge(v, v.tensor_tensor_scan(h_t[:], a_t[:], bh_t[:], 0.0,
                                              ALU.mult, ALU.add))
                v.tensor_copy(states[:, g:g + 1, :].rearrange("p o n -> p n o"),
                              h3[:, :, SEG - 1:SEG])
                if g < 15:
                    if g == 7:
                        v.wait_ge(sV.h, dtg_half[1])
                    a3_marks[g + 1] = a3_build(v, g + 1)
                vedge(v, v.tensor_tensor(b3[:, :, 1:SEG], h3[:, :, 1:SEG],
                                         Cbc[:, :, :], ALU.mult))
                vedge(v, v.tensor_tensor(b3[:, 0:8, 1:SEG], b3[:, 0:8, 1:SEG],
                                         b3[:, 8:16, 1:SEG], ALU.add))
                vedge(v, v.tensor_tensor(b3[:, 0:4, 1:SEG], b3[:, 0:4, 1:SEG],
                                         b3[:, 4:8, 1:SEG], ALU.add))
                vedge(v, v.tensor_tensor(b3[:, 0:2, 1:SEG], b3[:, 0:2, 1:SEG],
                                         b3[:, 2:4, 1:SEG], ALU.add))
                vedge(v, v.tensor_tensor(yg[:], b3[:, 0, 1:SEG],
                                         b3[:, 1, 1:SEG], ALU.add))
                v.tensor_scalar(udg[:], UBF[:, g, :], D_t[:, g:g + 1], None,
                                ALU.mult)
                vedge(v, v.tensor_tensor(yg[:], yg[:], udg[:], ALU.add))
                sV.inc(v.tensor_tensor(gated[:, g % 2, :], yg[:], SZ[:, g, :],
                                       ALU.mult))
                gated_marks.append(sV.n)
                if g < 15:
                    b3_build(v, g + 1)
            SEC("vector", _)

            if g < 15:
                def _(s, g=g):
                    s.wait_ge(sV.h, a3_marks[g + 1])
                    sSc.inc(s.activation(a3[:, :, 1:SEG], a3[:, :, 1:SEG],
                                         AF.Exp))
                SEC("scalar", _)
                exp_marks.append(sSc.n)

            def _(t, g=g):
                t.wait_ge(sV.h, gated_marks[g])
                t.wait_ge(*moutT_mark[g])
                if g == 0:
                    for q in range(4):
                        t.matmul(PM[q][:, 0:512], onesb[:], zrow[:],
                                 start=True, stop=True)
                ins = None
                for og in range(8):
                    ins = t.matmul(pmslot(og),
                                   moutT[:, g % 2, og * 128:(og + 1) * 128],
                                   gated[:, g % 2, :],
                                   start=False, stop=(g == 15),
                                   skip_group_check=True)
                sPE.inc(ins)
                mm_g.append(sPE.n)
            SEC("tensor", _)
        prev_mout_done = sPE.n

        def _(v, c=c, prev_shx_store=prev_shx_store):
            v.wait_ge(sPE.h, mm_g[-1])
            v.wait_ge(sGX.h, prev_shx_store)
            ins = None
            for q in range(4):
                ins = v.tensor_copy(
                    mo_fm[:, 2 * q:2 * q + 2, :].rearrange("p a b -> p (a b)"),
                    PM[q][:, 0:512])
            sV.inc(ins)
        SEC("vector", _)
        vs_mo = sV.n

        def _(g_, c=c, vs_mo=vs_mo):
            g_.wait_ge(sV.h, vs_mo)
            parv = g_.to_reg(g_.value_load(part_t[0:1, 0:1]))
            dst0 = shx[0].rearrange("(og p) t -> p og t", p=128)[
                :, :, c * TC:(c + 1) * TC]
            dst1 = shx[1].rearrange("(og p) t -> p og t", p=128)[
                :, :, c * TC:(c + 1) * TC]
            with g_.If_eq(parv, 0):
                g_.dma_start(out=dst0, in_=mo_fm[:]).then_inc(sGX.h, 16)
            with g_.Else():
                g_.dma_start(out=dst1, in_=mo_fm[:]).then_inc(sGX.h, 16)
            g_.end_ifs()
            sGX.n += 16
        SEC("gpsimd", _)
        prev_shx_store = sGX.n

        for _k in range(9):
            if prep_q:
                prep_q.pop(0)()

        if DBG and c == 0:
            def _(g_, vs_mo=vs_mo):
                g_.wait_ge(sV.h, vs_mo)
                g_.dma_start(out=dbg_mo, in_=mo_fm[:].rearrange("p a b -> p (a b)")).then_inc(sG.h, 16)
                sG.n += 16
            SEC("gpsimd", _)

    while prep_q:
        prep_q.pop(0)()

    def _(g_):
        sDT.inc(g_.dma_start(out=barin_d, in_=onesf[0:1, 0:2]), 16)
        g_.wait_ge(sDT.h, sDT.n)
        g_.wait_ge(sGX.h, sGX.n)
        g_.collective_compute(
            "AllReduce", ALU.add, ins=[barin_d[:]], outs=[barout_d[:]],
            replica_groups=[[0, 1], [2, 3], [4, 5], [6, 7]],
        ).then_inc(sPoll.h, 1)
        sPoll.n += 1
        g_.wait_ge(sPoll.h, sPoll.n)
    SEC("gpsimd", _)

    def _(v):
        ins = v.tensor_copy(cone[:], cone[:])
        ins.then_inc(sBAR.h, 1)
        sBAR.n += 1
    SEC("vector", _)
    def _(s):
        ins = s.copy(cone[:], cone[:])
        ins.then_inc(sBAR.h, 1)
        sBAR.n += 1
    SEC("scalar", _)
    def _(t):
        ins = t.matmul(P5[:, 0:1], onesf[:], wsel[:], start=True, stop=True)
        ins.then_inc(sBAR.h, 1)
        sBAR.n += 1
    SEC("tensor", _)
    def _(sync):
        sync.dma_start(out=dmy1[:], in_=part_in).then_inc(sBAR.h, 16)
        sBAR.n += 16
    SEC("sync", _)
    def _(g_):
        g_.dma_start(out=dmy2[:], in_=part_in).then_inc(sBG.h, 16)
        sBG.n += 16
    SEC("gpsimd", _)
    BARN = sBAR.n

    es1.close()

    es2 = ExitStack()
    SB2 = lambda name, shape, dt: es2.enter_context(
        nc.sbuf_tensor(name, shape, dt))
    W1T = SB2("W1T", [128, 8, 4096], BF16)
    projT = SB2("projT", [128, 8, 1024], BF16)
    W2Tp = SB2("W2Tp", [128, 2, 1024], BF16)
    comb = SB2("comb", [128, 8, T2C], F32)
    combbf = SB2("combbf", [128, 8, T2C], BF16)
    x1 = SB2("x1", [128, 8, T2C], F32)
    x2b = SB2("x2b", [128, 8, T2C], BF16)
    h1b = SB2("h1b", [128, 32, T2C], BF16)
    hid2t = SB2("hid2t", [128, 2, 1024], F32)
    hid2fm = SB2("hid2fm", [128, 8, T2C], F32)
    sqt = SB2("sqt", [128, 8, T2C], F32)
    b1t = SB2("b1t", [128, 32], F32)
    b2t = SB2("b2t", [128, 8], F32)
    pbt = SB2("pbt", [128, 8], F32)
    st48 = SB2("st48", [48, 128], F32)
    m_r = SB2("m_r", [1, T2C], F32)
    v_r = SB2("v_r", [1, T2C], F32)
    t_r = SB2("t_r", [1, T2C], F32)

    def _(sync):
        sync.wait_ge(sBAR.h, BARN)
        sync.wait_ge(sBG.h, 16)
        sync.wait_ge(sGW1.h, tern_last["w1"])
        for kg in range(8):
            sTW.inc(sync.dma_start(out=W1T[:, kg, :],
                                   in_=w1_bf[:, kg * 128:(kg + 1) * 128],
                                   transpose=True), 16)
        sync.wait_ge(sGP.h, pj_store[-1])
        for kg in range(8):
            sTW.inc(sync.dma_start(out=projT[:, kg, :],
                                   in_=proj_bf[:, kg * 128:(kg + 1) * 128],
                                   transpose=True), 16)
    SEC("sync", _)
    tr_wres = sTW.n

    def _(sync):
        sync.dma_start(out=st48[0:32, :],
                       in_=b1_in.rearrange("(g p) -> g p", p=128)
                       ).then_inc(sLB.h, 16)
        sync.dma_start(out=st48[32:40, :],
                       in_=b2_in.rearrange("(g p) -> g p", p=128)
                       ).then_inc(sLB.h, 16)
        sync.dma_start(out=st48[40:48, :],
                       in_=projb_in.rearrange("(g p) -> g p", p=128)
                       ).then_inc(sLB.h, 16)
        sLB.n += 48
    SEC("sync", _)
    ld_b = sLB.n

    def _(t, ld_b=ld_b):
        t.wait_ge(sLB.h, ld_b)
        sPE.inc(t.transpose(PBC[0:128, 0:48], st48[:], eyef_t[0:48, 0:48]))
    SEC("tensor", _)
    pe_b = sPE.n

    def _(v, pe_b=pe_b):
        v.wait_ge(sPE.h, pe_b)
        v.tensor_copy(b1t[:], PBC[:, 0:32])
        v.tensor_copy(b2t[:], PBC[:, 32:40])
        sV.inc(v.tensor_copy(pbt[:], PBC[:, 40:48]))
    SEC("vector", _)
    vs_b = sV.n

    prev_combfree = None
    store_marks = []
    ffn2_evac_marks = []
    w2_mm = []

    for c2 in range(N2CH):
        def _(g_, c2=c2, prev_combfree=prev_combfree):
            if prev_combfree is not None:
                g_.wait_ge(sV.h, prev_combfree)
            parv = g_.to_reg(g_.value_load(part_t[0:1, 0:1]))
            t0a = c2 * T2C
            t0b = HALF + c2 * T2C
            sh0 = shx[0].rearrange("(og p) t -> p og t", p=128)
            sh1 = shx[1].rearrange("(og p) t -> p og t", p=128)
            with g_.If_eq(parv, 0):
                g_.dma_start(out=comb[:], in_=sh0[:, :, t0a:t0a + T2C]
                             ).then_inc(sGC.h, 16)
                g_.dma_start(out=sqt[:], in_=sh1[:, :, t0a:t0a + T2C]
                             ).then_inc(sGC.h, 16)
            with g_.Else():
                g_.dma_start(out=comb[:], in_=sh0[:, :, t0b:t0b + T2C]
                             ).then_inc(sGC.h, 16)
                g_.dma_start(out=sqt[:], in_=sh1[:, :, t0b:t0b + T2C]
                             ).then_inc(sGC.h, 16)
            g_.end_ifs()
            sGC.n += 32
        SEC("gpsimd", _)
        g_comb = sGC.n

        def _(sync, c2=c2):
            if c2 >= 1:
                sync.wait_ge(sST.h, store_marks[-1])
            sH2.inc(sync.dma_start(
                out=hid2t[:],
                in_=hid2_in[c2 * T2C:(c2 + 1) * T2C, :].rearrange(
                    "(a p) d -> p a d", p=128)), 16)
        SEC("sync", _)
        ld_h2 = sH2.n

        def _(v, g_comb=g_comb, vs_b=vs_b):
            v.wait_ge(sGC.h, g_comb)
            v.wait_ge(sV.h, vs_b)
            v.tensor_tensor(comb[:], comb[:], sqt[:], ALU.add)
            sV.inc(v.tensor_copy(combbf[:], comb[:]))
        SEC("vector", _)
        vs_comb = sV.n
        if DBG and c2 == 0:
            def _(g_, vs_comb=vs_comb):
                g_.wait_ge(sV.h, vs_comb)
                g_.dma_start(out=dbg_comb, in_=comb[:].rearrange("p a b -> p (a b)")).then_inc(sG.h, 16)
                sG.n += 16
                _dbgmarks.append(sG.n)
            SEC("gpsimd", _)

        def _(t, vs_comb=vs_comb, c2=c2):
            t.wait_ge(sV.h, vs_comb)
            t.wait_ge(sTW.h, tr_wres)
            if c2 >= 1:
                t.wait_ge(sV.h, ffn2_evac_marks[-1])
            for q in range(4):
                t.matmul(PM[q][:, 0:512], onesb[:], zrow[:], start=True,
                         stop=True)
            ins = None
            for kg in range(8):
                for og in range(8):
                    ins = t.matmul(pmslot(og),
                                   projT[:, kg, og * 128:(og + 1) * 128],
                                   combbf[:, kg, :], start=False,
                                   stop=(kg == 7), skip_group_check=True)
            sPE.inc(ins)
            sPE.inc(t.matmul(P5[:, 0:2], onesb[:], zrow[0:1, 0:2], start=True,
                             stop=True))
        SEC("tensor", _)
        pe_proj = sPE.n

        if DBG and c2 == 0:
            def _(v, pe_proj=pe_proj):
                v.wait_ge(sPE.h, pe_proj)
                v.tensor_copy(sqt[:, 0:2, :].rearrange("p a b -> p (a b)"),
                              PM[3][:, 0:512])
                sV.inc(v.tensor_copy(sqt[:, 2:4, :].rearrange("p a b -> p (a b)"),
                                     PM[2][:, 0:512]))
            SEC("vector", _)
            def _(g_):
                g_.wait_ge(sV.h, sV.n)
                g_.dma_start(out=dbg_ps, in_=sqt[:, 0:4, :].rearrange(
                    "p a b -> p (a b)")).then_inc(sG.h, 16)
                g_.dma_start(out=dbg_pjt[:, 0:1024], in_=projT[:, 0, :]
                             ).then_inc(sG.h, 16)
                g_.dma_start(out=dbg_pjt[:, 1024:2048], in_=projT[:, 7, :]
                             ).then_inc(sG.h, 16)
                sG.n += 48
                _dbgmarks.append(sG.n)
            SEC("gpsimd", _)
        def _(v, pe_proj=pe_proj, c2=c2):
            v.wait_ge(sPE.h, pe_proj)
            if DBG and c2 == 0:
                v.wait_ge(sG.h, _dbgmarks[-1])
            ins = None
            for og in range(8):
                ins = v.tensor_scalar(comb[:, og, :], pmslot(og),
                                      pbt[:, og:og + 1], None, ALU.add)
            sV.inc(ins)
        SEC("vector", _)
        sc_attn = sV.n

        def fm_ln(buf, marker, gwait=None):
            def _(s, marker=marker, gwait=gwait):
                s.wait_ge(*marker)
                if gwait is not None:
                    s.wait_ge(sG.h, gwait)
                sSc.inc(s.activation(sqt[:].rearrange("p a b -> p (a b)"),
                                     buf[:].rearrange("p a b -> p (a b)"),
                                     AF.Square))
            SEC("scalar", _)
            sc_sq = sSc.n

            def _(t, sc_sq=sc_sq):
                t.wait_ge(sSc.h, sc_sq)
                for og in range(8):
                    t.matmul(PBC[0:1, 0:256], onescol[:], buf[:, og, :],
                             start=(og == 0), stop=(og == 7))
                ins = None
                for og in range(8):
                    ins = t.matmul(PBC[0:1, 256:512], onescol[:], sqt[:, og, :],
                                   start=(og == 0), stop=(og == 7))
                sPE.inc(ins)
            SEC("tensor", _)
            pe_st = sPE.n

            def _(v, pe_st=pe_st):
                v.wait_ge(sPE.h, pe_st)
                vedge(v, v.tensor_scalar(m_r[:], PBC[0:1, 0:256], 1.0 / DM,
                                         None, ALU.mult))
                vedge(v, v.tensor_scalar(v_r[:], PBC[0:1, 256:512], 1.0 / DM,
                                         None, ALU.mult))
                vedge(v, v.tensor_tensor(t_r[:], m_r[:], m_r[:], ALU.mult))
                vedge(v, v.tensor_tensor(v_r[:], v_r[:], t_r[:], ALU.subtract))
                sV.inc(v.tensor_scalar(v_r[:], v_r[:], 1.0, EPS, ALU.mult,
                                       ALU.add))
            SEC("vector", _)
            vs_st = sV.n

            def _(s, vs_st=vs_st):
                s.wait_ge(sV.h, vs_st)
                sedge(s, s.activation(v_r[:], v_r[:], AF.Ln))
                sSc.inc(s.activation(v_r[:], v_r[:], AF.Exp, scale=-0.5))
            SEC("scalar", _)
            vs_rs = sSc.n

            def _(t, vs_rs=vs_rs):
                t.wait_ge(sSc.h, vs_rs)
                t.matmul(PBC[:, 512:768], onesf[:], m_r[:], start=True,
                         stop=True)
                sPE.inc(t.matmul(PBC[:, 768:1024], onesf[:], v_r[:],
                                 start=True, stop=True))
            SEC("tensor", _)
            pe_bcst = sPE.n

            def _(v, pe_bcst=pe_bcst):
                v.wait_ge(sPE.h, pe_bcst)
                mb = PBC[:, 512:768].unsqueeze(1).broadcast_to([128, 8, 256])
                rb = PBC[:, 768:1024].unsqueeze(1).broadcast_to([128, 8, 256])
                v.tensor_tensor(buf[:], buf[:], mb, ALU.subtract)
                sV.inc(v.tensor_tensor(buf[:], buf[:], rb, ALU.mult))
            SEC("vector", _)
            return (sV.h, sV.n)

        ln1_done = fm_ln(comb, (sV.h, sc_attn))

        h2ev = None
        for i in range(8):
            def _(t, i=i, ld_h2=ld_h2, h2ev=h2ev):
                t.wait_ge(sH2.h, ld_h2)
                if h2ev is not None:
                    t.wait_ge(sV.h, h2ev)
                tt = i // 4
                kk0 = (i % 4) * 2
                t.transpose(P5[:, 0:128],
                            hid2t[:, tt, kk0 * 128:kk0 * 128 + 128], eyef_t[:])
                sPE.inc(t.transpose(P5[:, 128:256],
                                    hid2t[:, tt, kk0 * 128 + 128:kk0 * 128 + 256],
                                    eyef_t[:]))
            SEC("tensor", _)
            mype = sPE.n

            def _(v, i=i, mype=mype):
                v.wait_ge(sPE.h, mype)
                tt = i // 4
                kk0 = (i % 4) * 2
                v.tensor_copy(hid2fm[:, kk0, tt * 128:(tt + 1) * 128],
                              P5[:, 0:128])
                sV.inc(v.tensor_copy(hid2fm[:, kk0 + 1, tt * 128:(tt + 1) * 128],
                                     P5[:, 128:256]))
            SEC("vector", _)
            h2ev = sV.n
        vs_h2fm = sV.n

        if DBG and c2 == 0:
            def _(g_, ln1_done=ln1_done, vs_h2fm=vs_h2fm):
                g_.wait_ge(*ln1_done)
                g_.wait_ge(sV.h, vs_h2fm)
                g_.dma_start(out=dbg_attn, in_=comb[:].rearrange("p a b -> p (a b)")).then_inc(sG.h, 16)
                g_.dma_start(out=dbg_h2fm, in_=hid2fm[:].rearrange("p a b -> p (a b)")).then_inc(sG.h, 16)
                sG.n += 32
                _dbgmarks.append(sG.n)
            SEC("gpsimd", _)
        def _(v, ln1_done=ln1_done, vs_h2fm=vs_h2fm, c2=c2):
            v.wait_ge(*ln1_done)
            v.wait_ge(sV.h, vs_h2fm)
            if DBG and c2 == 0:
                v.wait_ge(sG.h, _dbgmarks[-1])
            v.tensor_tensor(x1[:], hid2fm[:], comb[:], ALU.add)
            sV.inc(v.tensor_copy(comb[:], x1[:]))
        SEC("vector", _)
        ln2_done = fm_ln(comb, (sV.h, sV.n))

        def _(v, ln2_done=ln2_done):
            v.wait_ge(*ln2_done)
            sV.inc(v.tensor_copy(x2b[:], comb[:]))
        SEC("vector", _)
        vs_x2b = sV.n
        if DBG and c2 == 0:
            def _(g_, vs_x2b=vs_x2b):
                g_.wait_ge(sV.h, vs_x2b)
                g_.dma_start(out=dbg_x1, in_=x1[:].rearrange("p a b -> p (a b)")).then_inc(sG.h, 16)
                g_.dma_start(out=dbg_x2b, in_=x2b[:].rearrange("p a b -> p (a b)")).then_inc(sG.h, 16)
                sG.n += 32
            SEC("gpsimd", _)

        ffn1_evac = []
        for og4 in range(8):
            def _(t, og4=og4, vs_x2b=vs_x2b):
                t.wait_ge(sV.h, vs_x2b)
                if og4 >= 2:
                    t.wait_ge(sSc.h, ffn1_evac[og4 - 2])
                ins = None
                for j in range(4):
                    og = og4 * 4 + j
                    sl = (og4 % 2) * 4 + j
                    for kg in range(8):
                        ins = t.matmul(pmslot(sl),
                                       W1T[:, kg, og * 128:(og + 1) * 128],
                                       x2b[:, kg, :], start=(kg == 0),
                                       stop=(kg == 7))
                sPE.inc(ins)
            SEC("tensor", _)
            mype = sPE.n

            def _(s, og4=og4, mype=mype):
                s.wait_ge(sPE.h, mype)
                ins = None
                for j in range(4):
                    og = og4 * 4 + j
                    sl = (og4 % 2) * 4 + j
                    ins = s.activation(h1b[:, og, :], pmslot(sl), AF.Gelu,
                                       bias=b1t[:, og:og + 1])
                sSc.inc(ins)
            SEC("scalar", _)
            ffn1_evac.append(sSc.n)
        h1_done = ffn1_evac[-1]

        for kg in range(32):
            def _(sync, kg=kg, c2=c2):
                if c2 == 0 and kg == 0:
                    sync.wait_ge(sGW2.h, tern_last["w2"])
                if w2_mm:
                    sync.wait_ge(sPE.h, w2_mm[-1])
                sTR.inc(sync.dma_start(out=W2Tp[:, kg % 2, :],
                                       in_=w2_bf[:, kg * 128:(kg + 1) * 128],
                                       transpose=True), 16)
            SEC("sync", _)
            mytr = sTR.n

            def _(t, kg=kg, mytr=mytr, h1_done=h1_done):
                t.wait_ge(sTR.h, mytr)
                if kg == 0:
                    t.wait_ge(sSc.h, h1_done)
                    for q in range(4):
                        t.matmul(PM[q][:, 0:512], onesb[:], zrow[:],
                                 start=True, stop=True)
                ins = None
                for og in range(8):
                    ins = t.matmul(pmslot(og),
                                   W2Tp[:, kg % 2, og * 128:(og + 1) * 128],
                                   h1b[:, kg, :], start=False,
                                   stop=(kg == 31), skip_group_check=True)
                sPE.inc(ins)
                w2_mm.append(sPE.n)
            SEC("tensor", _)
        pe_ffn2 = sPE.n

        def _(v, pe_ffn2=pe_ffn2):
            v.wait_ge(sPE.h, pe_ffn2)
            ins = None
            for og in range(8):
                ins = v.tensor_scalar(comb[:, og, :], pmslot(og),
                                      b2t[:, og:og + 1], None, ALU.add)
            sV.inc(ins)
        SEC("vector", _)
        sc_ffn = sV.n
        ffn2_evac_marks.append(sc_ffn)
        if DBG and c2 == 0:
            def _(g_, sc_ffn=sc_ffn):
                g_.wait_ge(sV.h, sc_ffn)
                g_.dma_start(out=dbg_h1, in_=h1b[:].rearrange("p a b -> p (a b)")).then_inc(sG.h, 16)
                g_.dma_start(out=dbg_ffn, in_=comb[:].rearrange("p a b -> p (a b)")).then_inc(sG.h, 16)
                sG.n += 32
                _dbgmarks.append(sG.n)
            SEC("gpsimd", _)
            _ffn_dump = sG.n
        else:
            _ffn_dump = None

        ln3_done = fm_ln(comb, (sV.h, sc_ffn), gwait=_ffn_dump)

        def _(v, ln3_done=ln3_done):
            v.wait_ge(*ln3_done)
            sV.inc(v.tensor_tensor(hid2fm[:], x1[:], comb[:], ALU.add))
        SEC("vector", _)
        vs_out = sV.n

        oev = None
        for i in range(8):
            def _(t, i=i, vs_out=vs_out, oev=oev):
                t.wait_ge(sV.h, vs_out)
                if oev is not None:
                    t.wait_ge(sV.h, oev)
                tt = i // 4
                kk0 = (i % 4) * 2
                t.transpose(P5[:, 0:128],
                            hid2fm[:, kk0, tt * 128:(tt + 1) * 128], eyef_t[:])
                sPE.inc(t.transpose(P5[:, 128:256],
                                    hid2fm[:, kk0 + 1, tt * 128:(tt + 1) * 128],
                                    eyef_t[:]))
            SEC("tensor", _)
            mype = sPE.n

            def _(v, i=i, mype=mype):
                v.wait_ge(sPE.h, mype)
                tt = i // 4
                kk0 = (i % 4) * 2
                v.tensor_copy(hid2t[:, tt, kk0 * 128:(kk0 + 1) * 128],
                              P5[:, 0:128])
                sV.inc(v.tensor_copy(hid2t[:, tt, (kk0 + 1) * 128:(kk0 + 2) * 128],
                                     P5[:, 128:256]))
            SEC("vector", _)
            oev = sV.n
        vs_otok = sV.n
        prev_combfree = vs_otok

        def _(sync, c2=c2, vs_otok=vs_otok):
            sync.wait_ge(sV.h, vs_otok)
            sST.inc(sync.dma_start(
                out=out_d[c2 * T2C:(c2 + 1) * T2C, :].rearrange(
                    "(a p) d -> p a d", p=128),
                in_=hid2t[:]), 16)
            store_marks.append(sST.n)
        SEC("sync", _)

    def _(sync):
        sync.wait_ge(sST.h, sST.n)
    SEC("sync", _)

    es2.close()
    es.close()
    return nc


_CACHED = {}


def kernel(**inputs):
    if "nc" not in _CACHED:
        _CACHED["nc"] = build()
    nc = _CACHED["nc"]

    hidden = inputs["L0_attn_hidden_states"]
    eyeb = np.eye(128, dtype=ml_dtypes.bfloat16)
    eyef = np.eye(128, dtype=np.float32)
    in_maps = []
    for c in range(8):
        b, tr = c // 2, c % 2
        p = "fast_" if tr == 0 else "slow_"
        m = dict(
            hid=np.ascontiguousarray(hidden[b]),
            hid2=np.ascontiguousarray(hidden[b, tr * HALF:(tr + 1) * HALF, :]),
            inw=inputs[f"L0_attn_{p}in_w"],
            convw=inputs[f"L0_attn_{p}conv_w"],
            convb=inputs[f"L0_attn_{p}conv_b"],
            xw=inputs[f"L0_attn_{p}x_w"],
            dtpw=inputs[f"L0_attn_{p}dtp_w"],
            dtpb=inputs[f"L0_attn_{p}dtp_b"],
            alog=inputs[f"L0_attn_{p}A_log"],
            dvec=inputs[f"L0_attn_{p}D"],
            moutw=inputs[f"L0_attn_{p}mout_w"],
            normw=inputs["L0_attn_norm_w"],
            normb=inputs["L0_attn_norm_b"],
            trackw=np.ascontiguousarray(
                inputs["L0_attn_track_weights"]).reshape(1, 2),
            projw=inputs["L0_attn_proj_w"],
            projb=inputs["L0_attn_proj_b"],
            w1=inputs["L0_ffn_W1"],
            b1=inputs["L0_ffn_b1"],
            w2=inputs["L0_ffn_W2"],
            b2=inputs["L0_ffn_b2"],
            eyeb=eyeb,
            eyef=eyef,
            part=np.array([[tr, 0]], dtype=np.uint32),
            selm=np.repeat(np.eye(32, dtype=ml_dtypes.bfloat16), 128, axis=1),
        )
        in_maps.append(m)

    res = run_bass_kernel_spmd(nc, in_maps, core_ids=list(range(8)))
    out = np.empty((B, L, DM), dtype=np.float32)
    for c in range(8):
        b, tr = c // 2, c % 2
        out[b, tr * HALF:(tr + 1) * HALF, :] = np.asarray(res.results[c]["out"])
    return out


# version 15
# speedup vs baseline: 1.4177x; 1.2722x over previous
import numpy as np
import ml_dtypes
import concourse.bass as bass
import concourse.mybir as mybir
from concourse.bass_utils import run_bass_kernel_spmd
from contextlib import ExitStack

F32 = mybir.dt.float32
BF16 = mybir.dt.bfloat16
U32 = mybir.dt.uint32
AF = mybir.ActivationFunctionType
ALU = mybir.AluOpType
AX = mybir.AxisListType.X

B, L, DM, DI, NS, RK = 4, 4096, 1024, 2048, 16, 64
TC = 256
NCH = L // TC
HALF = L // 2
T2C = 256
N2CH = HALF // T2C
SEG = TC + 1
EPS = 1e-5
FF = 4 * DM
DBG = False


class SemCtr:
    def __init__(self, nc, es, name):
        self.h = es.enter_context(nc.semaphore(name))
        self.n = 0

    def inc(self, instr, k=1):
        instr.then_inc(self.h, k)
        self.n += k
        return self.n


def build():
    nc = bass.Bass("TRN2", target_bir_lowering=False)
    ap = lambda t: t.ap()

    hid_in = ap(nc.dram_tensor("hid", [L, DM], F32, kind="ExternalInput"))
    hid2_in = ap(nc.dram_tensor("hid2", [HALF, DM], F32, kind="ExternalInput"))
    inw_in = ap(nc.dram_tensor("inw", [2 * DI, DM], F32, kind="ExternalInput"))
    convw_in = ap(nc.dram_tensor("convw", [DI, 4], F32, kind="ExternalInput"))
    convb_in = ap(nc.dram_tensor("convb", [DI], F32, kind="ExternalInput"))
    xw_in = ap(nc.dram_tensor("xw", [96, DI], F32, kind="ExternalInput"))
    dtpw_in = ap(nc.dram_tensor("dtpw", [DI, RK], F32, kind="ExternalInput"))
    dtpb_in = ap(nc.dram_tensor("dtpb", [DI], F32, kind="ExternalInput"))
    alog_in = ap(nc.dram_tensor("alog", [DI, NS], F32, kind="ExternalInput"))
    dvec_in = ap(nc.dram_tensor("dvec", [DI], F32, kind="ExternalInput"))
    moutw_in = ap(nc.dram_tensor("moutw", [DM, DI], F32, kind="ExternalInput"))
    normw_in = ap(nc.dram_tensor("normw", [DM], F32, kind="ExternalInput"))
    normb_in = ap(nc.dram_tensor("normb", [DM], F32, kind="ExternalInput"))
    trackw_in = ap(nc.dram_tensor("trackw", [1, 2], F32, kind="ExternalInput"))
    projw_in = ap(nc.dram_tensor("projw", [DM, DM], F32, kind="ExternalInput"))
    projb_in = ap(nc.dram_tensor("projb", [DM], F32, kind="ExternalInput"))
    w1_in = ap(nc.dram_tensor("w1", [FF, DM], F32, kind="ExternalInput"))
    b1_in = ap(nc.dram_tensor("b1", [FF], F32, kind="ExternalInput"))
    w2_in = ap(nc.dram_tensor("w2", [DM, FF], F32, kind="ExternalInput"))
    b2_in = ap(nc.dram_tensor("b2", [DM], F32, kind="ExternalInput"))
    eyeb_in = ap(nc.dram_tensor("eyeb", [128, 128], BF16, kind="ExternalInput"))
    eyef_in = ap(nc.dram_tensor("eyef", [128, 128], F32, kind="ExternalInput"))
    part_in = ap(nc.dram_tensor("part", [1, 2], U32, kind="ExternalInput"))
    selm_in = ap(nc.dram_tensor("selm", [32, 4096], BF16, kind="ExternalInput"))

    out_d = ap(nc.dram_tensor("out", [HALF, DM], F32, kind="ExternalOutput"))
    if DBG:
        dbg_shx = ap(nc.dram_tensor("dbg_shx", [DM, L], F32, kind="ExternalOutput"))
        dbg_hnfm = ap(nc.dram_tensor("dbg_hnfm", [128, 8 * TC], BF16, kind="ExternalOutput"))
        dbg_u = ap(nc.dram_tensor("dbg_u", [128, 16 * TC], BF16, kind="ExternalOutput"))
        dbg_sz = ap(nc.dram_tensor("dbg_sz", [128, 16 * TC], BF16, kind="ExternalOutput"))
        dbg_dtin = ap(nc.dram_tensor("dbg_dtin", [64, TC], BF16, kind="ExternalOutput"))
        dbg_bbc = ap(nc.dram_tensor("dbg_bbc", [128, 16 * TC], BF16, kind="ExternalOutput"))
        dbg_cbc = ap(nc.dram_tensor("dbg_cbc", [128, 16 * TC], BF16, kind="ExternalOutput"))
        dbg_a = ap(nc.dram_tensor("dbg_a", [128, NS * SEG], F32, kind="ExternalOutput"))
        dbg_h = ap(nc.dram_tensor("dbg_h", [128, NS * SEG], BF16, kind="ExternalOutput"))
        dbg_y = ap(nc.dram_tensor("dbg_y", [128, TC], F32, kind="ExternalOutput"))
        dbg_dtg = ap(nc.dram_tensor("dbg_dtg", [128, TC], F32, kind="ExternalOutput"))
        dbg_mo = ap(nc.dram_tensor("dbg_mo", [128, 8 * TC], F32, kind="ExternalOutput"))
        dbg_hid = ap(nc.dram_tensor("dbg_hid", [128, 2 * 1024], F32, kind="ExternalOutput"))
        dbg_hnbf = ap(nc.dram_tensor("dbg_hnbf", [128, 1024], BF16, kind="ExternalOutput"))
        dbg_nwb = ap(nc.dram_tensor("dbg_nwb", [128, 1024], F32, kind="ExternalOutput"))
        dbg_nbb = ap(nc.dram_tensor("dbg_nbb", [128, 1024], F32, kind="ExternalOutput"))
        dbg_inwt = ap(nc.dram_tensor("dbg_inwt", [128, 8 * 4096], BF16, kind="ExternalOutput"))
        dbg_xbf = ap(nc.dram_tensor("dbg_xbf", [128, 16 * (TC + 3)], BF16, kind="ExternalOutput"))
        dbg_lns = ap(nc.dram_tensor("dbg_lns", [128, 8], F32, kind="ExternalOutput"))
        dbg_lnt = ap(nc.dram_tensor("dbg_lnt", [128, 1024], F32, kind="ExternalOutput"))
        dbg_comb = ap(nc.dram_tensor("dbg_comb", [128, 8 * T2C], F32, kind="ExternalOutput"))
        dbg_attn = ap(nc.dram_tensor("dbg_attn", [128, 8 * T2C], F32, kind="ExternalOutput"))
        dbg_x1 = ap(nc.dram_tensor("dbg_x1", [128, 8 * T2C], F32, kind="ExternalOutput"))
        dbg_x2b = ap(nc.dram_tensor("dbg_x2b", [128, 8 * T2C], BF16, kind="ExternalOutput"))
        dbg_h1 = ap(nc.dram_tensor("dbg_h1", [128, 32 * T2C], BF16, kind="ExternalOutput"))
        dbg_ffn = ap(nc.dram_tensor("dbg_ffn", [128, 8 * T2C], F32, kind="ExternalOutput"))
        dbg_h2fm = ap(nc.dram_tensor("dbg_h2fm", [128, 8 * T2C], F32, kind="ExternalOutput"))
        dbg_ps = ap(nc.dram_tensor("dbg_ps", [128, 1024], F32, kind="ExternalOutput"))
        dbg_pjt = ap(nc.dram_tensor("dbg_pjt", [128, 2048], BF16, kind="ExternalOutput"))

    inw_bf = ap(nc.dram_tensor("inw_bf", [2 * DI, DM], BF16, kind="Internal"))
    inwT_d = ap(nc.dram_tensor("inwT_d", [DM, 2 * DI], BF16, kind="Internal"))
    mout_bf = ap(nc.dram_tensor("mout_bf", [DM, DI], BF16, kind="Internal"))
    proj_bf = ap(nc.dram_tensor("proj_bf", [DM, DM], BF16, kind="Internal"))
    w1_bf = ap(nc.dram_tensor("w1_bf", [FF, DM], BF16, kind="Internal"))
    w2_bf = ap(nc.dram_tensor("w2_bf", [DM, FF], BF16, kind="Internal"))

    shx = ap(nc.dram_tensor("shx", [2, DM, L], F32, kind="Internal",
                            addr_space="Shared"))
    barin_d = ap(nc.dram_tensor("barin_d", [1, 2], F32, kind="Internal"))
    barout_d = ap(nc.dram_tensor("barout_d", [1, 2], F32, kind="Internal"))

    es = ExitStack()
    SB = lambda name, shape, dt: es.enter_context(nc.sbuf_tensor(name, shape, dt))

    A_t = SB("A_t", [128, NS * NS], F32)
    D_t = SB("D_t", [128, 16], F32)
    convw_t = SB("convw_t", [128, 16, 4], F32)
    convb_t = SB("convb_t", [128, 16], F32)
    dtpb_t = SB("dtpb_t", [128, 16], F32)
    nw_b = SB("nw_b", [128, DM], F32)
    nb_b = SB("nb_b", [128, DM], F32)
    cone = SB("cone", [128, 1], F32)
    onesb = SB("onesb", [1, 128], BF16)
    zrow = SB("zrow", [1, 512], BF16)
    onesf = SB("onesf", [1, 128], F32)
    onescol = SB("onescol", [128, 1], F32)
    eyeb_t = SB("eyeb_t", [128, 128], BF16)
    eyef_t = SB("eyef_t", [128, 128], F32)
    xwT = SB("xwT", [128, 16, 96], BF16)
    dtpwT = SB("dtpwT", [65, 16, 128], BF16)
    states = SB("states", [128, 16, 16], F32)
    wsc = SB("wsc", [128, 1], F32)
    tr48 = SB("tr48", [48, 128], F32)
    selm_t = SB("selm_t", [32, 4096], BF16)
    tagt = SB("tagt", [1, 2], U32)
    ftile = SB("ftile", [1, 2], U32)
    part_t = SB("part_t", [1, 2], U32)
    dmy1 = SB("dmy1", [1, 2], U32)
    dmy2 = SB("dmy2", [1, 2], U32)
    tw_t = SB("tw_t", [1, 2], F32)
    sm_t = SB("sm_t", [1, 2], F32)
    sm_s = SB("sm_s", [1, 1], F32)
    wsel = SB("wsel", [1, 1], F32)
    tmpdtu = SB("tmpdtu", [128, TC], BF16)
    yg = SB("yg", [128, TC], F32)
    udg = SB("udg", [128, TC], F32)
    dt_in = SB("dt_in", [65, TC], BF16)
    bctok = SB("bctok", [128, 64], BF16)
    btt = SB("btt", [32, 256], BF16)
    lnstat = SB("lnstat", [128, 8], F32)

    PBC = es.enter_context(nc.psum_tensor("PBC", [128, 1024], F32))
    PM = [es.enter_context(nc.psum_tensor(f"PM{i}", [128, 512], F32))
          for i in range(4)]
    P5 = es.enter_context(nc.psum_tensor("P5", [128, 512], F32))
    PTR = es.enter_context(nc.psum_tensor("PTR", [128, 256], BF16))

    def pmslot(sl):
        return PM[sl // 2][:, (sl % 2) * 256:(sl % 2) * 256 + 256]

    sLD = SemCtr(nc, es, "sLD")
    sLa = SemCtr(nc, es, "sLa")
    sLb = SemCtr(nc, es, "sLb")
    sTa = SemCtr(nc, es, "sTa")
    sTb = SemCtr(nc, es, "sTb")
    sTc = SemCtr(nc, es, "sTc")
    sTd = SemCtr(nc, es, "sTd")
    sTW = SemCtr(nc, es, "sTW")
    sLB = SemCtr(nc, es, "sLB")
    sH2 = SemCtr(nc, es, "sH2")
    sGI = SemCtr(nc, es, "sGI")
    sGT = SemCtr(nc, es, "sGT")
    sGM = SemCtr(nc, es, "sGM")
    sGX = SemCtr(nc, es, "sGX")
    sGC = SemCtr(nc, es, "sGC")
    sGP = SemCtr(nc, es, "sGP")
    sGW1 = SemCtr(nc, es, "sGW1")
    sGW2 = SemCtr(nc, es, "sGW2")
    sDT = SemCtr(nc, es, "sDT")
    sBG = SemCtr(nc, es, "sBG")
    sTR = SemCtr(nc, es, "sTR")
    sG = SemCtr(nc, es, "sG")
    sPE = SemCtr(nc, es, "sPE")
    sV = SemCtr(nc, es, "sV")
    sSc = SemCtr(nc, es, "sSc")
    sPoll = SemCtr(nc, es, "sPoll")
    sST = SemCtr(nc, es, "sST")
    sLWa = SemCtr(nc, es, "sLWa")
    sLWb = SemCtr(nc, es, "sLWb")
    sBAR = SemCtr(nc, es, "sBAR")
    sVE = SemCtr(nc, es, "sVE")
    sSE = SemCtr(nc, es, "sSE")

    def vedge(v, instr):
        instr.then_inc(sVE.h, 1)
        sVE.n += 1
        v.wait_ge(sVE.h, sVE.n)

    def sedge(s, instr):
        instr.then_inc(sSE.h, 1)
        sSE.n += 1
        s.wait_ge(sSE.h, sSE.n)

    blk = es.enter_context(nc.Block())

    def SEC(name, body):
        getattr(blk, name)(body)

    def _(sync):
        for dst, src in [
            (part_t[:], part_in),
            (tagt[:], hid_in[0:1, 0:2].bitcast(U32)),
            (eyeb_t[:], eyeb_in),
            (eyef_t[:], eyef_in),
            (A_t[:].rearrange("p (g n) -> p g n", g=16),
             alog_in.rearrange("(g p) n -> p g n", p=128)),
            (convw_t[:], convw_in.rearrange("(g p) j -> p g j", p=128)),
            (tr48[0:16, :], convb_in.rearrange("(g p) -> g p", p=128)),
            (tr48[16:32, :], dtpb_in.rearrange("(g p) -> g p", p=128)),
            (tr48[32:48, :], dvec_in.rearrange("(g p) -> g p", p=128)),
            (tw_t[:], trackw_in),
            (selm_t[:], selm_in),
        ]:
            sLD.inc(sync.dma_start(out=dst, in_=src), 16)
    SEC("sync", _)
    ld0 = sLD.n

    def _(v):
        v.wait_ge(sLD.h, ld0)
        v.memset(cone[:], 1.0)
        v.memset(onesb[:], 1.0)
        v.memset(zrow[:], 0.0)
        v.memset(onesf[:], 1.0)
        v.memset(onescol[:], 1.0)
        v.memset(states[:], 0.0)
        sV.inc(v.memset(ftile[:], 0))
    SEC("vector", _)

    def _(s):
        s.wait_ge(sLD.h, ld0)
        sedge(s, s.activation(A_t[:], A_t[:], AF.Exp))
        sSc.inc(s.mul(A_t[:], A_t[:], -1.0))
    SEC("scalar", _)
    sc_A = sSc.n

    def _(t):
        t.wait_ge(sLD.h, ld0)
        sPE.inc(t.transpose(PBC[0:128, 0:48], tr48[:], eyef_t[0:48, 0:48]))
    SEC("tensor", _)
    pe_tr = sPE.n

    def _(v):
        v.wait_ge(sPE.h, pe_tr)
        v.tensor_copy(convb_t[:], PBC[:, 0:16])
        v.tensor_copy(dtpb_t[:], PBC[:, 16:32])
        sV.inc(v.tensor_copy(D_t[:], PBC[:, 32:48]))
    SEC("vector", _)
    vs_tr = sV.n

    def _(v):
        vedge(v, v.reduce_max(sm_s[:], tw_t[:], axis=AX))
        sV.inc(v.tensor_scalar(sm_t[:], tw_t[:], sm_s[0:1, 0:1], None,
                               ALU.subtract))
    SEC("vector", _)
    vs_sm = sV.n

    def _(s):
        s.wait_ge(sV.h, vs_sm)
        sSc.inc(s.activation(sm_t[:], sm_t[:], AF.Exp))
    SEC("scalar", _)
    sc_sm = sSc.n

    def _(v):
        v.wait_ge(sSc.h, sc_sm)
        vedge(v, v.reduce_sum(sm_s[:], sm_t[:], axis=AX))
        vedge(v, v.reciprocal(sm_s[:], sm_s[:]))
        vedge(v, v.tensor_scalar(sm_t[:], sm_t[:], sm_s[0:1, 0:1], None,
                                 ALU.mult))
        parv = v.to_reg(v.value_load(part_t[0:1, 0:1]))
        with v.If_eq(parv, 0):
            v.tensor_copy(wsel[:], sm_t[0:1, 0:1]).then_inc(sV.h, 1)
        with v.Else():
            v.tensor_copy(wsel[:], sm_t[0:1, 1:2]).then_inc(sV.h, 1)
        v.end_ifs()
        sV.n += 1
    SEC("vector", _)
    vs_wsel = sV.n

    def _(t):
        t.wait_ge(sV.h, vs_wsel)
        sPE.inc(t.matmul(P5[:, 0:1], onesf[:], wsel[:], start=True, stop=True))
    SEC("tensor", _)
    pe_wsc = sPE.n

    def _(v):
        v.wait_ge(sPE.h, pe_wsc)
        sV.inc(v.tensor_copy(wsc[:], P5[:, 0:1]))
    SEC("vector", _)
    vs_wsc = sV.n

    es1 = ExitStack()
    SB1 = lambda name, shape, dt: es1.enter_context(
        nc.sbuf_tensor(name, shape, dt))
    hidt = SB1("hidt", [128, 2, 1024], F32)
    lnt = SB1("lnt", [128, 1024], F32)
    hn_bf = SB1("hn_bf", [128, 1024], BF16)
    hnfm = SB1("hnfm", [128, 8, TC], BF16)
    XBF = SB1("XBF", [128, 16, TC + 3], BF16)
    ACC = SB1("ACC", [128, 8, TC], F32)
    UBF = SB1("UBF", [128, 16, TC], BF16)
    SZ = SB1("SZ", [128, 16, TC], BF16)
    Bbc = SB1("Bbc", [128, 16, TC], BF16)
    Cbc = SB1("Cbc", [128, 16, TC], BF16)
    a_t = SB1("a_t", [128, NS * SEG], F32)
    bh_t = SB1("bh_t", [128, NS * SEG], BF16)
    h_t = SB1("h_t", [128, NS * SEG], BF16)
    gated = SB1("gated", [128, 2, TC], BF16)
    mo_fm = SB1("mo_fm", [128, 8, TC], F32)
    dtg3 = SB1("dtg3", [128, 16, TC], F32)
    sabs3 = SB1("sabs3", [128, 8, TC], F32)
    inwT = SB1("inwT", [128, 2, 8, 256], BF16)
    moutT = SB1("moutT", [128, 2, 1024], BF16)
    es1a = ExitStack()
    inwS = es1a.enter_context(nc.sbuf_tensor("inwS", [128, 4096], BF16))

    a3 = a_t[:].rearrange("p (n t) -> p n t", n=NS)
    b3 = bh_t[:].rearrange("p (n t) -> p n t", n=NS)
    h3 = h_t[:].rearrange("p (n t) -> p n t", n=NS)

    def _(v):
        v.memset(a_t[:], 0.0)
        v.memset(dt_in[64:65, :], 1.0)
        sV.inc(v.memset(XBF[:, :, 0:3], 0.0))
    SEC("vector", _)

    def _(sync):
        sync.wait_ge(sLD.h, ld0)
        sync.dma_start(out=lnt[0:1, :],
                       in_=normw_in.rearrange("(o d) -> o d", o=1)
                       ).then_inc(sLD.h, 16)
        sLD.n += 16
    SEC("sync", _)
    ld_nw = sLD.n

    def _(t):
        t.wait_ge(sLD.h, ld_nw)
        t.wait_ge(sV.h, vs_tr)
        t.matmul(PBC[:, 0:512], onesf[:], lnt[0:1, 0:512], start=True, stop=True)
        sPE.inc(t.matmul(PBC[:, 512:1024], onesf[:], lnt[0:1, 512:1024],
                         start=True, stop=True))
    SEC("tensor", _)
    pe_nw = sPE.n

    def _(v):
        v.wait_ge(sPE.h, pe_nw)
        sV.inc(v.tensor_copy(nw_b[:], PBC[:, 0:1024]))
    SEC("vector", _)
    vs_nw = sV.n

    def _(sync):
        sync.wait_ge(sV.h, vs_nw)
        sync.dma_start(out=lnt[0:1, :],
                       in_=normb_in.rearrange("(o d) -> o d", o=1)
                       ).then_inc(sLD.h, 16)
        sLD.n += 16
    SEC("sync", _)
    ld_nb = sLD.n

    def _(t):
        t.wait_ge(sLD.h, ld_nb)
        t.matmul(PBC[:, 0:512], onesf[:], lnt[0:1, 0:512], start=True, stop=True)
        sPE.inc(t.matmul(PBC[:, 512:1024], onesf[:], lnt[0:1, 512:1024],
                         start=True, stop=True))
    SEC("tensor", _)
    pe_nb = sPE.n

    def _(v):
        v.wait_ge(sPE.h, pe_nb)
        sV.inc(v.tensor_copy(nb_b[:], PBC[:, 0:1024]))
    SEC("vector", _)
    vs_nb = sV.n

    inw_cast = []
    inw_store = []
    for i in range(32):
        sLP = sLa if i % 2 == 0 else sLb

        def _(sync, i=i, sLP=sLP):
            if i >= 2:
                sync.wait_ge(sV.h, inw_cast[i - 2])
            sLP.inc(sync.dma_start(out=hidt[:, i % 2, :],
                                   in_=inw_in[i * 128:(i + 1) * 128, :]), 16)
        SEC("sync", _)
        myld = sLP.n

        def _(v, i=i, myld=myld, sLP=sLP):
            v.wait_ge(sLP.h, myld)
            if i >= 1:
                v.wait_ge(sGI.h, inw_store[i - 1])
            inw_cast.append(
                sV.inc(v.tensor_copy(hn_bf[:], hidt[:, i % 2, :])))
        SEC("vector", _)

        def _(g_, i=i):
            g_.wait_ge(sV.h, inw_cast[i])
            inw_store.append(
                sGI.inc(g_.dma_start(out=inw_bf[i * 128:(i + 1) * 128, :],
                                     in_=hn_bf[:]), 16))
        SEC("gpsimd", _)

    inwT_st = []
    for kg in range(8):
        def _(sync, kg=kg):
            if kg == 0:
                sync.wait_ge(sGI.h, inw_store[-1])
            else:
                sync.wait_ge(sGT.h, inwT_st[kg - 1])
            sTR.inc(sync.dma_start(out=inwS[:],
                                   in_=inw_bf[:, kg * 128:(kg + 1) * 128],
                                   transpose=True), 16)
        SEC("sync", _)
        mytr = sTR.n

        def _(g_, kg=kg, mytr=mytr):
            g_.wait_ge(sTR.h, mytr)
            inwT_st.append(
                sGT.inc(g_.dma_start(out=inwT_d[kg * 128:(kg + 1) * 128, :],
                                     in_=inwS[:]), 16))
        SEC("gpsimd", _)

    es1a.close()
    wstage = SB1("wstage", [128, 1024], F32)
    wstag2 = SB1("wstag2", [128, 1024], F32)
    wneg = SB1("wneg", [128, 1024], F32)
    wbf = SB1("wbf", [128, 1024], BF16)
    wbf2 = SB1("wbf2", [128, 1024], BF16)
    babs = SB1("babs", [128, 1], F32)
    tred = SB1("tred", [128, 1], F32)
    thr1 = SB1("thr1", [128, 1], F32)
    nthr1 = SB1("nthr1", [128, 1], F32)
    thr2 = SB1("thr2", [128, 1], F32)
    nthr2 = SB1("nthr2", [128, 1], F32)

    pk_load_marks = []
    pk_cons_marks = []
    pk_store = {"w1": [], "w2": [], "pj": []}
    tern_last = {}
    pj_store = pk_store["pj"]

    def pk_buf(i):
        return wstage if i % 2 == 0 else wstag2

    def emit_load(src_ap):
        i = len(pk_load_marks)
        sLP = sLWa if i % 2 == 0 else sLWb

        def _(sync, i=i, src_ap=src_ap, sLP=sLP):
            if i == 0:
                sync.wait_ge(sGT.h, inwT_st[-1])
            if i >= 2:
                sync.wait_ge(sV.h, pk_cons_marks[i - 2])
            sLP.inc(sync.dma_start(out=pk_buf(i)[:], in_=src_ap), 16)
        SEC("sync", _)
        pk_load_marks.append((sLP.h, sLP.n))
        return i

    def task_projcast(src_ap, dst_ap, ti):
        def emit(src_ap=src_ap, dst_ap=dst_ap, ti=ti):
            i = emit_load(src_ap)
            wb = wbf if ti % 2 == 0 else wbf2

            def _(v, i=i, ti=ti, wb=wb):
                v.wait_ge(*pk_load_marks[i])
                if ti >= 2:
                    v.wait_ge(sGP.h, pk_store["pj"][ti - 2])
                sV.inc(v.tensor_copy(wb[:], pk_buf(i)[:]))
            SEC("vector", _)
            pk_cons_marks.append(sV.n)
            myc = sV.n

            def _(g_, myc=myc, dst_ap=dst_ap, wb=wb):
                g_.wait_ge(sV.h, myc)
                pk_store["pj"].append(
                    sGP.inc(g_.dma_start(out=dst_ap, in_=wb[:]), 16))
            SEC("gpsimd", _)
        return emit

    def task_absred(src_ap, ti):
        def emit(src_ap=src_ap, ti=ti):
            i = emit_load(src_ap)

            def _(s, i=i):
                s.wait_ge(*pk_load_marks[i])
                sSc.inc(s.activation(pk_buf(i)[:], pk_buf(i)[:], AF.Abs))
            SEC("scalar", _)
            myab = sSc.n

            def _(v, i=i, ti=ti, myab=myab):
                v.wait_ge(sSc.h, myab)
                if ti == 0:
                    ins = v.reduce_sum(babs[:], pk_buf(i)[:], axis=AX)
                else:
                    vedge(v, v.reduce_sum(tred[:], pk_buf(i)[:], axis=AX))
                    ins = v.tensor_tensor(babs[:], babs[:], tred[:], ALU.add)
                sV.inc(ins)
            SEC("vector", _)
            pk_cons_marks.append(sV.n)
        return emit

    def task_thr(wname):
        def emit(wname=wname):
            def _(t):
                t.wait_ge(sV.h, pk_cons_marks[-1])
                sPE.inc(t.matmul(P5[0:1, 0:1], babs[:], onescol[:],
                                 start=True, stop=True))
            SEC("tensor", _)
            pe_tot = sPE.n

            def _(v, pe_tot=pe_tot):
                v.wait_ge(sPE.h, pe_tot)
                sV.inc(v.tensor_copy(babs[0:1, 0:1], P5[0:1, 0:1]))
            SEC("vector", _)
            vs_tot = sV.n

            def _(t, vs_tot=vs_tot):
                t.wait_ge(sV.h, vs_tot)
                sPE.inc(t.matmul(P5[:, 0:1], onesf[:], babs[0:1, 0:1],
                                 start=True, stop=True))
            SEC("tensor", _)
            pe_bc = sPE.n

            def _(v, pe_bc=pe_bc, wname=wname):
                v.wait_ge(sPE.h, pe_bc)
                th = thr1 if wname == "w1" else thr2
                nth = nthr1 if wname == "w1" else nthr2
                vedge(v, v.tensor_scalar(th[:], P5[:, 0:1], 1.0 / (FF * DM),
                                         1e-8, ALU.mult, ALU.max))
                vedge(v, v.tensor_scalar(th[:], th[:], 0.5, None, ALU.mult))
                sV.inc(v.tensor_scalar(nth[:], th[:], -1.0, None, ALU.mult))
            SEC("vector", _)
        return emit

    def task_tern(wname, src_ap, dst_ap, ti):
        sGW = sGW1 if wname == "w1" else sGW2

        def emit(wname=wname, src_ap=src_ap, dst_ap=dst_ap, ti=ti, sGW=sGW):
            i = emit_load(src_ap)
            wb = wbf if ti % 2 == 0 else wbf2
            th = thr1 if wname == "w1" else thr2
            nth = nthr1 if wname == "w1" else nthr2

            def _(v, i=i, ti=ti, wb=wb, th=th, nth=nth, sGW=sGW,
                  wname=wname):
                v.wait_ge(*pk_load_marks[i])
                if ti >= 2:
                    v.wait_ge(sGW.h, pk_store[wname][ti - 2])
                v.tensor_scalar(wneg[:], pk_buf(i)[:], nth[:, 0:1], None,
                                ALU.is_le)
                ins = v.scalar_tensor_tensor(wb[:], pk_buf(i)[:], th[:, 0:1],
                                             wneg[:], ALU.is_ge, ALU.subtract)
                ins.then_inc(sV.h, 1)
                sV.n += 1
            SEC("vector", _)
            pk_cons_marks.append(sV.n)
            myc = sV.n

            def _(g_, myc=myc, dst_ap=dst_ap, wb=wb, sGW=sGW, wname=wname,
                  ti=ti):
                g_.wait_ge(sV.h, myc)
                pk_store[wname].append(
                    sGW.inc(g_.dma_start(out=dst_ap, in_=wb[:]), 16))
                if ti == 31:
                    tern_last[wname] = pk_store[wname][-1]
            SEC("gpsimd", _)
        return emit

    prep_q = []
    for ti in range(8):
        prep_q.append(task_projcast(projw_in[ti * 128:(ti + 1) * 128, :],
                                    proj_bf[ti * 128:(ti + 1) * 128, :], ti))
    for ti in range(32):
        prep_q.append(task_absred(w1_in[ti * 128:(ti + 1) * 128, :], ti))
    prep_q.append(task_thr("w1"))
    for ti in range(32):
        ri, ci = ti // 4, ti % 4
        prep_q.append(task_absred(
            w2_in[ri * 128:(ri + 1) * 128, ci * 1024:(ci + 1) * 1024], ti))
    prep_q.append(task_thr("w2"))
    for ti in range(32):
        prep_q.append(task_tern("w1", w1_in[ti * 128:(ti + 1) * 128, :],
                                w1_bf[ti * 128:(ti + 1) * 128, :], ti))
    for ti in range(32):
        ri, ci = ti // 4, ti % 4
        sl2 = (slice(ri * 128, (ri + 1) * 128),
               slice(ci * 1024, (ci + 1) * 1024))
        prep_q.append(task_tern("w2", w2_in[sl2[0], sl2[1]],
                                w2_bf[sl2[0], sl2[1]], ti))

    hid_flat = hidt[:].rearrange("p a b -> p (a b)")
    mo_cast = []
    mo_store = []
    for i in range(8):
        def _(sync, i=i):
            if i == 0:
                sync.wait_ge(sV.h, inw_cast[-1])
            else:
                sync.wait_ge(sV.h, mo_cast[i - 1])
            sLD.inc(sync.dma_start(out=hid_flat,
                                   in_=moutw_in[i * 128:(i + 1) * 128, :]), 16)
        SEC("sync", _)
        myld = sLD.n

        def _(v, i=i, myld=myld):
            v.wait_ge(sLD.h, myld)
            v.wait_ge(sV.h, vs_wsc)
            if i == 0:
                v.wait_ge(sGI.h, inw_store[-1])
            else:
                v.wait_ge(sGM.h, mo_store[i - 1])
            mo_cast.append(
                sV.inc(v.tensor_scalar(h_t[:, 0:2048], hid_flat, wsc[:, 0:1], None,
                                       ALU.mult)))
        SEC("vector", _)

        def _(g_, i=i):
            g_.wait_ge(sV.h, mo_cast[i])
            mo_store.append(
                sGM.inc(g_.dma_start(out=mout_bf[i * 128:(i + 1) * 128, :],
                                     in_=h_t[:, 0:2048]), 16))
        SEC("gpsimd", _)

    def _(sync):
        sync.wait_ge(sV.h, mo_cast[-1])
        sync.dma_start(out=hidt[0:96, 0, :], in_=xw_in[:, 0:1024]
                       ).then_inc(sLD.h, 16)
        sync.dma_start(out=hidt[0:96, 1, :], in_=xw_in[:, 1024:2048]
                       ).then_inc(sLD.h, 16)
        sync.dma_start(out=lnt[:].rearrange("p (g r) -> p g r", g=16),
                       in_=dtpw_in.rearrange("(g p) r -> p g r", p=128)
                       ).then_inc(sLD.h, 16)
        sLD.n += 48
    SEC("sync", _)

    def _(g_):
        sDT.inc(g_.dma_start(out=dtpwT[64:65, :, :],
                             in_=dtpb_in.rearrange("(o g d) -> o g d",
                                                   o=1, g=16)), 16)
    SEC("gpsimd", _)
    ld_xw = sLD.n

    dtpw_bf = SZ[:, 0:4, :].rearrange("p a b -> p (a b)")
    def _(v):
        v.wait_ge(sLD.h, ld_xw)
        v.wait_ge(sDT.h, 16)
        v.wait_ge(sGM.h, mo_store[-1])
        v.tensor_copy(h_t[0:96, 0:1024], hidt[0:96, 0, :])
        v.tensor_copy(h_t[0:96, 1024:2048], hidt[0:96, 1, :])
        sV.inc(v.tensor_copy(dtpw_bf, lnt[:]))
    SEC("vector", _)
    vs_xw = sV.n

    prev_ev = vs_xw
    for g in range(16):
        def _(t, g=g, prev_ev=prev_ev):
            t.wait_ge(sV.h, prev_ev)
            tt, off = g // 8, (g % 8) * 128
            t.transpose(PTR[:, 0:96], h_t[0:96, tt * 1024 + off:
                                            tt * 1024 + off + 128],
                        eyeb_t[0:96, 0:96])
            sPE.inc(t.transpose(PTR[0:64, 96:224],
                                dtpw_bf[:, g * 64:(g + 1) * 64], eyeb_t[:]))
        SEC("tensor", _)
        mype = sPE.n

        def _(v, g=g, mype=mype):
            v.wait_ge(sPE.h, mype)
            v.tensor_copy(xwT[:, g, :], PTR[:, 0:96])
            sV.inc(v.tensor_copy(dtpwT[0:64, g, :], PTR[0:64, 96:224]))
        SEC("vector", _)
        prev_ev = sV.n

    hid_free = sV.n
    inproj_done = sPE.n
    prev_mout_done = sPE.n
    prev_shx_store = sGX.n
    _dbgmarks = []

    for c in range(NCH):
        def _(sync, c=c, hid_free=hid_free, inproj_done=inproj_done):
            sync.wait_ge(sV.h, hid_free)
            if c == 0:
                sync.wait_ge(sGI.h, inw_store[-1])
            sLD.inc(sync.dma_start(
                out=hidt[:],
                in_=hid_in[c * TC:(c + 1) * TC, :].rearrange(
                    "(a p) d -> p a d", p=128)), 16)
        SEC("sync", _)
        ld_hid = sLD.n

        if DBG and c == 0:
            def _(g_, ld_hid=ld_hid, tr_inw=tr_inw):
                g_.wait_ge(sLD.h, ld_hid)
                g_.wait_ge(sTR.h, tr_inw)
                g_.dma_start(out=dbg_hid, in_=hidt[:].rearrange("p a b -> p (a b)")).then_inc(sG.h, 16)
                g_.dma_start(out=dbg_inwt, in_=inwT[:].rearrange("p a b -> p (a b)")).then_inc(sG.h, 16)
                g_.dma_start(out=dbg_nwb, in_=nw_b[:]).then_inc(sG.h, 16)
                g_.dma_start(out=dbg_nbb, in_=nb_b[:]).then_inc(sG.h, 16)
                sG.n += 64
                _dbgmarks.append(sG.n)
            SEC("gpsimd", _)

        tp_evac = None
        for tt in range(2):
            def _(v, c=c, tt=tt, ld_hid=ld_hid):
                v.wait_ge(sLD.h, ld_hid)
                if DBG and c == 0 and tt == 0:
                    v.wait_ge(sG.h, _dbgmarks[-1])
                if tt == 0 and c > 0:
                    v.tensor_copy(XBF[:, :, 0:3], XBF[:, :, TC:TC + 3])
                vedge(v, v.reduce_sum(lnstat[:, 0:1], hidt[:, tt, :], axis=AX))
                vedge(v, v.tensor_scalar(lnstat[:, 0:1], lnstat[:, 0:1],
                                         1.0 / DM, None, ALU.mult))
                v.tensor_scalar(lnt[:], hidt[:, tt, :], lnstat[:, 0:1], None,
                                ALU.subtract)
                v.tensor_tensor(hidt[:, tt, :], lnt[:], lnt[:], ALU.mult)
                vedge(v, v.reduce_sum(lnstat[:, 1:2], hidt[:, tt, :], axis=AX))
                sV.inc(v.tensor_scalar(lnstat[:, 1:2], lnstat[:, 1:2],
                                       1.0 / DM, EPS, ALU.mult, ALU.add))
            SEC("vector", _)
            m1 = sV.n

            if DBG and c == 0 and tt == 0:
                def _(g_, m1=m1):
                    g_.wait_ge(sV.h, m1)
                    g_.dma_start(out=dbg_lns, in_=lnstat[:]).then_inc(sG.h, 16)
                    g_.dma_start(out=dbg_lnt, in_=lnt[:]).then_inc(sG.h, 16)
                    sG.n += 32
                    _dbgmarks.append(sG.n)
                SEC("gpsimd", _)

            def _(s, m1=m1, c=c, tt=tt):
                s.wait_ge(sV.h, m1)
                if DBG and c == 0 and tt == 0:
                    s.wait_ge(sG.h, _dbgmarks[-1])
                sedge(s, s.activation(lnstat[:, 1:2], lnstat[:, 1:2], AF.Ln))
                sSc.inc(s.activation(lnstat[:, 1:2], lnstat[:, 1:2], AF.Exp,
                                     scale=-0.5))
            SEC("scalar", _)
            m2 = sSc.n

            def _(v, tt=tt, m2=m2):
                v.wait_ge(sSc.h, m2)
                v.tensor_scalar(lnt[:], lnt[:], lnstat[:, 1:2], None, ALU.mult)
                vedge(v, v.reduce_sum(lnstat[:, 2:3], lnt[:], axis=AX))
                vedge(v, v.tensor_scalar(lnstat[:, 2:3], lnstat[:, 2:3],
                                         1.0 / DM, None, ALU.mult))
                v.tensor_scalar(lnt[:], lnt[:], lnstat[:, 2:3], None,
                                ALU.subtract)
                v.tensor_tensor(hidt[:, tt, :], lnt[:], lnt[:], ALU.mult)
                vedge(v, v.reduce_sum(lnstat[:, 3:4], hidt[:, tt, :], axis=AX))
                sV.inc(v.tensor_scalar(lnstat[:, 3:4], lnstat[:, 3:4],
                                       1.0 / DM, EPS, ALU.mult, ALU.add))
            SEC("vector", _)
            m3 = sV.n

            def _(s, m3=m3):
                s.wait_ge(sV.h, m3)
                sedge(s, s.activation(lnstat[:, 3:4], lnstat[:, 3:4], AF.Ln))
                sSc.inc(s.activation(lnstat[:, 3:4], lnstat[:, 3:4], AF.Exp,
                                     scale=-0.5))
            SEC("scalar", _)
            m4 = sSc.n

            def _(v, m4=m4):
                v.wait_ge(sSc.h, m4)
                v.scalar_tensor_tensor(lnt[:], lnt[:], lnstat[:, 3:4], nw_b[:],
                                       ALU.mult, ALU.mult)
                sV.inc(v.tensor_tensor(hn_bf[:], lnt[:], nb_b[:], ALU.add))
            SEC("vector", _)
            m5 = sV.n

            for i in range(4):
                def _(t, tt=tt, i=i, m5=m5, tp_evac=tp_evac):
                    t.wait_ge(sV.h, m5)
                    if tp_evac is not None:
                        t.wait_ge(sV.h, tp_evac)
                    kk0 = i * 2
                    t.transpose(PTR[:, 0:128],
                                hn_bf[:, kk0 * 128:kk0 * 128 + 128], eyeb_t[:])
                    sPE.inc(t.transpose(PTR[:, 128:256],
                                        hn_bf[:, kk0 * 128 + 128:kk0 * 128 + 256],
                                        eyeb_t[:]))
                SEC("tensor", _)
                mype = sPE.n

                def _(v, tt=tt, i=i, mype=mype):
                    v.wait_ge(sPE.h, mype)
                    kk0 = i * 2
                    v.tensor_copy(hnfm[:, kk0, tt * 128:(tt + 1) * 128],
                                  PTR[:, 0:128])
                    sV.inc(v.tensor_copy(
                        hnfm[:, kk0 + 1, tt * 128:(tt + 1) * 128],
                        PTR[:, 128:256]))
                SEC("vector", _)
                tp_evac = sV.n
        hid_free = sV.n
        vs_hnfm = sV.n

        evac_marks = []
        inw_rt_marks = []
        for rt2 in range(16):
            sTP = sTa if rt2 % 2 == 0 else sTb

            def _(sync, rt2=rt2, c=c, sTP=sTP):
                if c == 0 and rt2 == 0:
                    sync.wait_ge(sGT.h, inwT_st[-1])
                if len(inw_rt_marks) >= 2:
                    sync.wait_ge(sPE.h, inw_rt_marks[-2])
                sTP.inc(sync.dma_start(
                    out=inwT[:, rt2 % 2, :, :],
                    in_=inwT_d[:, rt2 * 256:(rt2 + 1) * 256].rearrange(
                        "(kg p) m -> p kg m", p=128)), 16)
            SEC("sync", _)
            inw_rt_marks.append(None)
            _my_tr = sTP.n

            def _(t, rt2=rt2, vs_hnfm=vs_hnfm, _my_tr=_my_tr, sTP=sTP):
                t.wait_ge(sV.h, vs_hnfm)
                t.wait_ge(sTP.h, _my_tr)
                if rt2 >= 2:
                    engh, cnt = evac_marks[rt2 - 2]
                    t.wait_ge(engh, cnt)
                ins = None
                for j in range(2):
                    rt = rt2 * 2 + j
                    slot = PBC[:, (rt % 4) * 256:(rt % 4) * 256 + 256]
                    for kg in range(8):
                        ins = t.matmul(slot,
                                       inwT[:, rt2 % 2, kg,
                                            j * 128:(j + 1) * 128],
                                       hnfm[:, kg, :], start=(kg == 0),
                                       stop=(kg == 7))
                sPE.inc(ins)
                inw_rt_marks[rt2] = sPE.n
            SEC("tensor", _)
            mype = sPE.n

            if rt2 < 8:
                def _(v, rt2=rt2, mype=mype):
                    v.wait_ge(sPE.h, mype)
                    ins = None
                    for j in range(2):
                        rt = rt2 * 2 + j
                        ins = v.tensor_copy(XBF[:, rt, 3:3 + TC],
                                            PBC[:, (rt % 4) * 256:(rt % 4) * 256 + 256])
                    sV.inc(ins)
                SEC("vector", _)
                evac_marks.append((sV.h, sV.n))
            else:
                def _(s, rt2=rt2, mype=mype):
                    s.wait_ge(sPE.h, mype)
                    ins = None
                    for j in range(2):
                        rt = rt2 * 2 + j
                        ins = s.activation(SZ[:, rt - 16, :],
                                           PBC[:, (rt % 4) * 256:(rt % 4) * 256 + 256],
                                           AF.Silu)
                    sSc.inc(ins)
                SEC("scalar", _)
                evac_marks.append((sSc.h, sSc.n))
        inproj_done = sPE.n
        x_ready = evac_marks[7][1]

        silu_marks = []
        for hh in range(2):
            def _(v, hh=hh, x_ready=x_ready):
                if hh == 0:
                    v.wait_ge(sV.h, x_ready)
                else:
                    v.wait_ge(sSc.h, silu_marks[0])
                ins = None
                for g8 in range(8):
                    g = 8 * hh + g8
                    acc = ACC[:, g8, :]
                    v.tensor_scalar(acc, XBF[:, g, 0:TC], convw_t[:, g, 0:1],
                                    convb_t[:, g:g + 1], ALU.mult, ALU.add)
                    for j in range(1, 4):
                        ins = v.scalar_tensor_tensor(acc, XBF[:, g, j:j + TC],
                                                     convw_t[:, g, j:j + 1],
                                                     acc, ALU.mult, ALU.add)
                sV.inc(ins)
            SEC("vector", _)
            myv = sV.n

            def _(s, hh=hh, myv=myv):
                s.wait_ge(sV.h, myv)
                silu_marks.append(
                    sSc.inc(s.activation(
                        UBF[:, 8 * hh:8 * hh + 8, :].rearrange(
                            "p a b -> p (a b)"),
                        ACC[:].rearrange("p a b -> p (a b)"), AF.Silu)))
            SEC("scalar", _)
        u_ready = silu_marks[-1]

        def _(t, u_ready=u_ready, c=c):
            t.wait_ge(sSc.h, u_ready)
            ins = None
            for kg in range(16):
                ins = t.matmul(P5[0:96, 0:256], xwT[:, kg, :], UBF[:, kg, :],
                               start=(kg == 0), stop=(kg == 15))
            sPE.inc(ins)
        SEC("tensor", _)
        pe_xdbl = sPE.n

        def _(v, pe_xdbl=pe_xdbl):
            v.wait_ge(sPE.h, pe_xdbl)
            sV.inc(v.tensor_copy(dt_in[0:64, :], P5[0:64, 0:256]))
        SEC("vector", _)
        vs_dtin = sV.n

        def _(t, u_ready=u_ready):
            ins = None
            for ts_ in range(2):
                for kg in range(16):
                    ins = t.matmul(PBC[:, ts_ * 32:(ts_ + 1) * 32],
                                   UBF[:, kg, ts_ * 128:(ts_ + 1) * 128],
                                   xwT[:, kg, 64:96],
                                   start=(kg == 0), stop=(kg == 15))
            sPE.inc(ins)
        SEC("tensor", _)
        pe_xT = sPE.n

        def _(v, pe_xT=pe_xT):
            v.wait_ge(sPE.h, pe_xT)
            sV.inc(v.tensor_copy(bctok[:], PBC[:, 0:64]))
        SEC("vector", _)
        vs_bctok = sV.n

        def _(t, vs_bctok=vs_bctok):
            t.wait_ge(sV.h, vs_bctok)
            t.transpose(PTR[0:32, 0:128], bctok[:, 0:32], eyeb_t[:])
            sPE.inc(t.transpose(PTR[0:32, 128:256], bctok[:, 32:64], eyeb_t[:]))
        SEC("tensor", _)
        pe_bt = sPE.n

        def _(v, pe_bt=pe_bt):
            v.wait_ge(sPE.h, pe_bt)
            sV.inc(v.tensor_copy(btt[:], PTR[0:32, :]))
        SEC("vector", _)
        vs_bt = sV.n

        bc_evac = []
        for r in range(8):
            def _(t, r=r, vs_bt=vs_bt):
                t.wait_ge(sV.h, vs_bt)
                if r > 0:
                    t.wait_ge(sV.h, bc_evac[r - 1])
                ins = None
                for j in range(4):
                    n = (r % 4) * 4 + j + (0 if r < 4 else 16)
                    ins = t.matmul(pmslot(j), selm_t[:, n * 128:(n + 1) * 128],
                                   btt[:], start=True, stop=True)
                sPE.inc(ins)
            SEC("tensor", _)
            mype = sPE.n

            def _(v, r=r, mype=mype):
                v.wait_ge(sPE.h, mype)
                n0 = (r % 4) * 4
                tgt = Bbc if r < 4 else Cbc
                v.tensor_copy(
                    tgt[:, n0:n0 + 2, :].rearrange("p a b -> p (a b)"),
                    PM[0][:, 0:512])
                sV.inc(v.tensor_copy(
                    tgt[:, n0 + 2:n0 + 4, :].rearrange("p a b -> p (a b)"),
                    PM[1][:, 0:512]))
            SEC("vector", _)
            bc_evac.append(sV.n)
        bc_done_v = bc_evac[-1]
        bc_done_s = sSc.n

        if DBG and c == 0:
            def _(g_, x_ready=x_ready, m5=m5):
                g_.wait_ge(sV.h, x_ready)
                g_.dma_start(out=dbg_hnbf, in_=hn_bf[:]).then_inc(sG.h, 16)
                g_.dma_start(out=dbg_xbf, in_=XBF[:].rearrange("p a b -> p (a b)")).then_inc(sG.h, 16)
                sG.n += 32
            SEC("gpsimd", _)
            def _(g_, vs_dtin=vs_dtin, bc_done_v=bc_done_v, bc_done_s=bc_done_s):
                g_.wait_ge(sV.h, vs_dtin)
                g_.wait_ge(sV.h, bc_done_v)
                g_.wait_ge(sSc.h, bc_done_s)
                g_.dma_start(out=dbg_hnfm, in_=hnfm[:].rearrange("p a b -> p (a b)")).then_inc(sG.h, 16)
                g_.dma_start(out=dbg_u, in_=UBF[:].rearrange("p a b -> p (a b)")).then_inc(sG.h, 16)
                g_.dma_start(out=dbg_sz, in_=SZ[:].rearrange("p a b -> p (a b)")).then_inc(sG.h, 16)
                g_.dma_start(out=dbg_dtin, in_=dt_in[:]).then_inc(sG.h, 16)
                g_.dma_start(out=dbg_bbc, in_=Bbc[:].rearrange("p a b -> p (a b)")).then_inc(sG.h, 16)
                g_.dma_start(out=dbg_cbc, in_=Cbc[:].rearrange("p a b -> p (a b)")).then_inc(sG.h, 16)
                sG.n += 96
            SEC("gpsimd", _)

        def dtp_slot2(k):
            return (P5[:, 0:512] if k % 2 == 0 else PM[0][:, 0:512])

        dtp_evac = []
        dtg_half = [None, None]
        for k in range(8):
            def _(t, k=k, vs_dtin=vs_dtin, bc_done_v=bc_done_v):
                if k == 0:
                    t.wait_ge(sV.h, vs_dtin)
                if k == 1:
                    t.wait_ge(sV.h, bc_done_v)
                if k >= 2:
                    t.wait_ge(sSc.h, dtp_evac[k - 2][0])
                    t.wait_ge(sV.h, dtp_evac[k - 2][1])
                sl = dtp_slot2(k)
                t.matmul(sl[:, 0:256], dtpwT[:, 2 * k, :], dt_in[:],
                         start=True, stop=True)
                sPE.inc(t.matmul(sl[:, 256:512], dtpwT[:, 2 * k + 1, :],
                                 dt_in[:], start=True, stop=True))
            SEC("tensor", _)
            mype = sPE.n

            def _(s, k=k, mype=mype):
                s.wait_ge(sPE.h, mype)
                if k == 4:
                    s.wait_ge(sV.h, dtg_half[0])
                sSc.inc(s.activation(
                    sabs3[:, (2 * k) % 8:(2 * k) % 8 + 2, :].rearrange(
                        "p a b -> p (a b)"),
                    dtp_slot2(k), AF.Abs))
            SEC("scalar", _)
            sc_abs = sSc.n

            def _(v, k=k, mype=mype):
                v.wait_ge(sPE.h, mype)
                sV.inc(v.tensor_scalar(
                    dtg3[:, 2 * k:2 * k + 2, :].rearrange("p a b -> p (a b)"),
                    dtp_slot2(k), 0.0, None, ALU.max))
            SEC("vector", _)
            dtp_evac.append((sc_abs, sV.n))

            if k % 4 == 3:
                sf = sabs3[:].rearrange("p a b -> p (a b)")
                g = 2 * k + 1

                def _(s, sf=sf):
                    sedge(s, s.activation(sf, sf, AF.Exp, scale=-1.0))
                    sSc.inc(s.activation(sf, sf, AF.Ln, bias=cone[:, 0:1]))
                SEC("scalar", _)
                sc_sp = sSc.n

                def _(v, g=g, sc_sp=sc_sp, sf=sf):
                    v.wait_ge(sSc.h, sc_sp)
                    h8 = dtg3[:, g - 7:g + 1, :].rearrange("p a b -> p (a b)")
                    sV.inc(v.tensor_tensor(h8, h8, sf, ALU.add))
                SEC("vector", _)
                dtg_half[k // 4] = sV.n

        def a3_exp(s, g):
            ins = None
            for n in range(NS):
                ins = s.activation(a3[:, n, 1:SEG], dtg3[:, g, :], AF.Exp,
                                   scale=A_t[:, g * 16 + n:g * 16 + n + 1])
            sSc.inc(ins)
            return sSc.n

        def b3_build(v, g):
            v.tensor_tensor(tmpdtu[:], dtg3[:, g, :], UBF[:, g, :], ALU.mult)
            v.tensor_tensor(
                b3[:, :, 1:SEG],
                tmpdtu[:].unsqueeze(1).broadcast_to([128, NS, TC]),
                Bbc[:, :, :], ALU.mult)
            ins = v.tensor_copy(
                b3[:, :, 0:1],
                states[:, g:g + 1, :].rearrange("p o n -> p n o"))
            ins.then_inc(sV.h, 1)
            sV.n += 1
            return sV.n

        exp_marks = []
        gated_marks = []
        scan_marks = []
        mm_g = []
        moutT_mark = []

        def _(s):
            s.wait_ge(sV.h, dtg_half[0])
            a3_exp(s, 0)
        SEC("scalar", _)
        exp_marks.append(sSc.n)

        def _(v, bc_done_v=bc_done_v):
            v.wait_ge(sV.h, bc_done_v)
            b3_build(v, 0)
        SEC("vector", _)

        for g in range(16):
            sMP = sTc if g % 2 == 0 else sTd

            def _(sync, g=g, c=c, prev_mout_done=prev_mout_done, sMP=sMP):
                if g >= 2:
                    sync.wait_ge(sPE.h, mm_g[g - 2])
                elif c == 0:
                    sync.wait_ge(sGM.h, mo_store[-1])
                else:
                    sync.wait_ge(sPE.h, prev_mout_done)
                sMP.inc(sync.dma_start(
                    out=moutT[:, g % 2, :],
                    in_=mout_bf[:, g * 128:(g + 1) * 128],
                    transpose=True), 16)
                moutT_mark.append((sMP.h, sMP.n))
            SEC("sync", _)

            def _(v, g=g):
                v.wait_ge(sSc.h, exp_marks[g])
                if g >= 2:
                    v.wait_ge(sPE.h, mm_g[g - 2])
                ins = v.tensor_tensor_scan(h_t[:], a_t[:], bh_t[:], 0.0,
                                           ALU.mult, ALU.add)
                ins.then_inc(sV.h, 1)
                sV.n += 1
                v.wait_ge(sV.h, sV.n)
                scan_marks.append(sV.n)
                v.tensor_copy(states[:, g:g + 1, :].rearrange("p o n -> p n o"),
                              h3[:, :, SEG - 1:SEG])
                vedge(v, v.tensor_tensor(b3[:, :, 1:SEG], h3[:, :, 1:SEG],
                                         Cbc[:, :, :], ALU.mult))
                vedge(v, v.tensor_tensor(b3[:, 0:8, 1:SEG], b3[:, 0:8, 1:SEG],
                                         b3[:, 8:16, 1:SEG], ALU.add))
                vedge(v, v.tensor_tensor(b3[:, 0:4, 1:SEG], b3[:, 0:4, 1:SEG],
                                         b3[:, 4:8, 1:SEG], ALU.add))
                vedge(v, v.tensor_tensor(b3[:, 0:2, 1:SEG], b3[:, 0:2, 1:SEG],
                                         b3[:, 2:4, 1:SEG], ALU.add))
                vedge(v, v.tensor_tensor(yg[:], b3[:, 0, 1:SEG],
                                         b3[:, 1, 1:SEG], ALU.add))
                v.tensor_scalar(udg[:], UBF[:, g, :], D_t[:, g:g + 1], None,
                                ALU.mult)
                vedge(v, v.tensor_tensor(yg[:], yg[:], udg[:], ALU.add))
                sV.inc(v.tensor_tensor(gated[:, g % 2, :], yg[:], SZ[:, g, :],
                                       ALU.mult))
                gated_marks.append(sV.n)
                if g < 15:
                    b3_build(v, g + 1)
            SEC("vector", _)

            if g < 15:
                def _(s, g=g):
                    s.wait_ge(sV.h, scan_marks[g])
                    if g == 7:
                        s.wait_ge(sV.h, dtg_half[1])
                    a3_exp(s, g + 1)
                SEC("scalar", _)
                exp_marks.append(sSc.n)

            def _(t, g=g):
                t.wait_ge(sV.h, gated_marks[g])
                t.wait_ge(*moutT_mark[g])
                if g == 0:
                    for q in range(4):
                        t.matmul(PM[q][:, 0:512], onesb[:], zrow[:],
                                 start=True, stop=True)
                ins = None
                for og in range(8):
                    ins = t.matmul(pmslot(og),
                                   moutT[:, g % 2, og * 128:(og + 1) * 128],
                                   gated[:, g % 2, :],
                                   start=False, stop=(g == 15),
                                   skip_group_check=True)
                sPE.inc(ins)
                mm_g.append(sPE.n)
            SEC("tensor", _)
        prev_mout_done = sPE.n

        def _(v, c=c, prev_shx_store=prev_shx_store):
            v.wait_ge(sPE.h, mm_g[-1])
            v.wait_ge(sGX.h, prev_shx_store)
            ins = None
            for q in range(4):
                ins = v.tensor_copy(
                    mo_fm[:, 2 * q:2 * q + 2, :].rearrange("p a b -> p (a b)"),
                    PM[q][:, 0:512])
            sV.inc(ins)
        SEC("vector", _)
        vs_mo = sV.n

        def _(g_, c=c, vs_mo=vs_mo):
            g_.wait_ge(sV.h, vs_mo)
            parv = g_.to_reg(g_.value_load(part_t[0:1, 0:1]))
            dst0 = shx[0].rearrange("(og p) t -> p og t", p=128)[
                :, :, c * TC:(c + 1) * TC]
            dst1 = shx[1].rearrange("(og p) t -> p og t", p=128)[
                :, :, c * TC:(c + 1) * TC]
            with g_.If_eq(parv, 0):
                g_.dma_start(out=dst0, in_=mo_fm[:]).then_inc(sGX.h, 16)
            with g_.Else():
                g_.dma_start(out=dst1, in_=mo_fm[:]).then_inc(sGX.h, 16)
            g_.end_ifs()
            sGX.n += 16
        SEC("gpsimd", _)
        prev_shx_store = sGX.n

        for _k in range(9):
            if prep_q:
                prep_q.pop(0)()

        if DBG and c == 0:
            def _(g_, vs_mo=vs_mo):
                g_.wait_ge(sV.h, vs_mo)
                g_.dma_start(out=dbg_mo, in_=mo_fm[:].rearrange("p a b -> p (a b)")).then_inc(sG.h, 16)
                sG.n += 16
            SEC("gpsimd", _)

    while prep_q:
        prep_q.pop(0)()

    def _(g_):
        sDT.inc(g_.dma_start(out=barin_d, in_=onesf[0:1, 0:2]), 16)
        g_.wait_ge(sDT.h, sDT.n)
        g_.wait_ge(sGX.h, sGX.n)
        g_.collective_compute(
            "AllReduce", ALU.add, ins=[barin_d[:]], outs=[barout_d[:]],
            replica_groups=[[0, 1], [2, 3], [4, 5], [6, 7]],
        ).then_inc(sPoll.h, 1)
        sPoll.n += 1
        g_.wait_ge(sPoll.h, sPoll.n)
    SEC("gpsimd", _)

    def _(v):
        ins = v.tensor_copy(cone[:], cone[:])
        ins.then_inc(sBAR.h, 1)
        sBAR.n += 1
    SEC("vector", _)
    def _(s):
        ins = s.copy(cone[:], cone[:])
        ins.then_inc(sBAR.h, 1)
        sBAR.n += 1
    SEC("scalar", _)
    def _(t):
        ins = t.matmul(P5[:, 0:1], onesf[:], wsel[:], start=True, stop=True)
        ins.then_inc(sBAR.h, 1)
        sBAR.n += 1
    SEC("tensor", _)
    def _(sync):
        sync.dma_start(out=dmy1[:], in_=part_in).then_inc(sBAR.h, 16)
        sBAR.n += 16
    SEC("sync", _)
    def _(g_):
        g_.dma_start(out=dmy2[:], in_=part_in).then_inc(sBG.h, 16)
        sBG.n += 16
    SEC("gpsimd", _)
    BARN = sBAR.n

    es1.close()

    es2 = ExitStack()
    SB2 = lambda name, shape, dt: es2.enter_context(
        nc.sbuf_tensor(name, shape, dt))
    W1T = SB2("W1T", [128, 8, 4096], BF16)
    projT = SB2("projT", [128, 8, 1024], BF16)
    W2Tp = SB2("W2Tp", [128, 2, 1024], BF16)
    comb = SB2("comb", [128, 8, T2C], F32)
    combbf = SB2("combbf", [128, 8, T2C], BF16)
    x1 = SB2("x1", [128, 8, T2C], F32)
    x2b = SB2("x2b", [128, 8, T2C], BF16)
    h1b = SB2("h1b", [128, 32, T2C], BF16)
    hid2t = SB2("hid2t", [128, 2, 1024], F32)
    hid2fm = SB2("hid2fm", [128, 8, T2C], F32)
    sqt = SB2("sqt", [128, 8, T2C], F32)
    b1t = SB2("b1t", [128, 32], F32)
    b2t = SB2("b2t", [128, 8], F32)
    pbt = SB2("pbt", [128, 8], F32)
    st48 = SB2("st48", [48, 128], F32)
    m_r = SB2("m_r", [1, T2C], F32)
    v_r = SB2("v_r", [1, T2C], F32)
    t_r = SB2("t_r", [1, T2C], F32)

    def _(sync):
        sync.wait_ge(sBAR.h, BARN)
        sync.wait_ge(sBG.h, 16)
        sync.wait_ge(sGW1.h, tern_last["w1"])
        for kg in range(8):
            sTW.inc(sync.dma_start(out=W1T[:, kg, :],
                                   in_=w1_bf[:, kg * 128:(kg + 1) * 128],
                                   transpose=True), 16)
        sync.wait_ge(sGP.h, pj_store[-1])
        for kg in range(8):
            sTW.inc(sync.dma_start(out=projT[:, kg, :],
                                   in_=proj_bf[:, kg * 128:(kg + 1) * 128],
                                   transpose=True), 16)
    SEC("sync", _)
    tr_wres = sTW.n

    def _(sync):
        sync.dma_start(out=st48[0:32, :],
                       in_=b1_in.rearrange("(g p) -> g p", p=128)
                       ).then_inc(sLB.h, 16)
        sync.dma_start(out=st48[32:40, :],
                       in_=b2_in.rearrange("(g p) -> g p", p=128)
                       ).then_inc(sLB.h, 16)
        sync.dma_start(out=st48[40:48, :],
                       in_=projb_in.rearrange("(g p) -> g p", p=128)
                       ).then_inc(sLB.h, 16)
        sLB.n += 48
    SEC("sync", _)
    ld_b = sLB.n

    def _(t, ld_b=ld_b):
        t.wait_ge(sLB.h, ld_b)
        sPE.inc(t.transpose(PBC[0:128, 0:48], st48[:], eyef_t[0:48, 0:48]))
    SEC("tensor", _)
    pe_b = sPE.n

    def _(v, pe_b=pe_b):
        v.wait_ge(sPE.h, pe_b)
        v.tensor_copy(b1t[:], PBC[:, 0:32])
        v.tensor_copy(b2t[:], PBC[:, 32:40])
        sV.inc(v.tensor_copy(pbt[:], PBC[:, 40:48]))
    SEC("vector", _)
    vs_b = sV.n

    prev_combfree = None
    store_marks = []
    ffn2_evac_marks = []
    w2_mm = []

    for c2 in range(N2CH):
        def _(g_, c2=c2, prev_combfree=prev_combfree):
            if prev_combfree is not None:
                g_.wait_ge(sV.h, prev_combfree)
            parv = g_.to_reg(g_.value_load(part_t[0:1, 0:1]))
            t0a = c2 * T2C
            t0b = HALF + c2 * T2C
            sh0 = shx[0].rearrange("(og p) t -> p og t", p=128)
            sh1 = shx[1].rearrange("(og p) t -> p og t", p=128)
            with g_.If_eq(parv, 0):
                g_.dma_start(out=comb[:], in_=sh0[:, :, t0a:t0a + T2C]
                             ).then_inc(sGC.h, 16)
                g_.dma_start(out=sqt[:], in_=sh1[:, :, t0a:t0a + T2C]
                             ).then_inc(sGC.h, 16)
            with g_.Else():
                g_.dma_start(out=comb[:], in_=sh0[:, :, t0b:t0b + T2C]
                             ).then_inc(sGC.h, 16)
                g_.dma_start(out=sqt[:], in_=sh1[:, :, t0b:t0b + T2C]
                             ).then_inc(sGC.h, 16)
            g_.end_ifs()
            sGC.n += 32
        SEC("gpsimd", _)
        g_comb = sGC.n

        def _(sync, c2=c2):
            if c2 >= 1:
                sync.wait_ge(sST.h, store_marks[-1])
            sH2.inc(sync.dma_start(
                out=hid2t[:],
                in_=hid2_in[c2 * T2C:(c2 + 1) * T2C, :].rearrange(
                    "(a p) d -> p a d", p=128)), 16)
        SEC("sync", _)
        ld_h2 = sH2.n

        def _(v, g_comb=g_comb, vs_b=vs_b):
            v.wait_ge(sGC.h, g_comb)
            v.wait_ge(sV.h, vs_b)
            v.tensor_tensor(comb[:], comb[:], sqt[:], ALU.add)
            sV.inc(v.tensor_copy(combbf[:], comb[:]))
        SEC("vector", _)
        vs_comb = sV.n
        if DBG and c2 == 0:
            def _(g_, vs_comb=vs_comb):
                g_.wait_ge(sV.h, vs_comb)
                g_.dma_start(out=dbg_comb, in_=comb[:].rearrange("p a b -> p (a b)")).then_inc(sG.h, 16)
                sG.n += 16
                _dbgmarks.append(sG.n)
            SEC("gpsimd", _)

        def _(t, vs_comb=vs_comb, c2=c2):
            t.wait_ge(sV.h, vs_comb)
            t.wait_ge(sTW.h, tr_wres)
            if c2 >= 1:
                t.wait_ge(sV.h, ffn2_evac_marks[-1])
            for q in range(4):
                t.matmul(PM[q][:, 0:512], onesb[:], zrow[:], start=True,
                         stop=True)
            ins = None
            for kg in range(8):
                for og in range(8):
                    ins = t.matmul(pmslot(og),
                                   projT[:, kg, og * 128:(og + 1) * 128],
                                   combbf[:, kg, :], start=False,
                                   stop=(kg == 7), skip_group_check=True)
            sPE.inc(ins)
            sPE.inc(t.matmul(P5[:, 0:2], onesb[:], zrow[0:1, 0:2], start=True,
                             stop=True))
        SEC("tensor", _)
        pe_proj = sPE.n

        if DBG and c2 == 0:
            def _(v, pe_proj=pe_proj):
                v.wait_ge(sPE.h, pe_proj)
                v.tensor_copy(sqt[:, 0:2, :].rearrange("p a b -> p (a b)"),
                              PM[3][:, 0:512])
                sV.inc(v.tensor_copy(sqt[:, 2:4, :].rearrange("p a b -> p (a b)"),
                                     PM[2][:, 0:512]))
            SEC("vector", _)
            def _(g_):
                g_.wait_ge(sV.h, sV.n)
                g_.dma_start(out=dbg_ps, in_=sqt[:, 0:4, :].rearrange(
                    "p a b -> p (a b)")).then_inc(sG.h, 16)
                g_.dma_start(out=dbg_pjt[:, 0:1024], in_=projT[:, 0, :]
                             ).then_inc(sG.h, 16)
                g_.dma_start(out=dbg_pjt[:, 1024:2048], in_=projT[:, 7, :]
                             ).then_inc(sG.h, 16)
                sG.n += 48
                _dbgmarks.append(sG.n)
            SEC("gpsimd", _)
        def _(v, pe_proj=pe_proj, c2=c2):
            v.wait_ge(sPE.h, pe_proj)
            if DBG and c2 == 0:
                v.wait_ge(sG.h, _dbgmarks[-1])
            ins = None
            for og in range(8):
                ins = v.tensor_scalar(comb[:, og, :], pmslot(og),
                                      pbt[:, og:og + 1], None, ALU.add)
            sV.inc(ins)
        SEC("vector", _)
        sc_attn = sV.n

        def fm_ln(buf, marker, gwait=None):
            def _(s, marker=marker, gwait=gwait):
                s.wait_ge(*marker)
                if gwait is not None:
                    s.wait_ge(sG.h, gwait)
                sSc.inc(s.activation(sqt[:].rearrange("p a b -> p (a b)"),
                                     buf[:].rearrange("p a b -> p (a b)"),
                                     AF.Square))
            SEC("scalar", _)
            sc_sq = sSc.n

            def _(t, sc_sq=sc_sq):
                t.wait_ge(sSc.h, sc_sq)
                for og in range(8):
                    t.matmul(PBC[0:1, 0:256], onescol[:], buf[:, og, :],
                             start=(og == 0), stop=(og == 7))
                ins = None
                for og in range(8):
                    ins = t.matmul(PBC[0:1, 256:512], onescol[:], sqt[:, og, :],
                                   start=(og == 0), stop=(og == 7))
                sPE.inc(ins)
            SEC("tensor", _)
            pe_st = sPE.n

            def _(v, pe_st=pe_st):
                v.wait_ge(sPE.h, pe_st)
                vedge(v, v.tensor_scalar(m_r[:], PBC[0:1, 0:256], 1.0 / DM,
                                         None, ALU.mult))
                vedge(v, v.tensor_scalar(v_r[:], PBC[0:1, 256:512], 1.0 / DM,
                                         None, ALU.mult))
                vedge(v, v.tensor_tensor(t_r[:], m_r[:], m_r[:], ALU.mult))
                vedge(v, v.tensor_tensor(v_r[:], v_r[:], t_r[:], ALU.subtract))
                sV.inc(v.tensor_scalar(v_r[:], v_r[:], 1.0, EPS, ALU.mult,
                                       ALU.add))
            SEC("vector", _)
            vs_st = sV.n

            def _(s, vs_st=vs_st):
                s.wait_ge(sV.h, vs_st)
                sedge(s, s.activation(v_r[:], v_r[:], AF.Ln))
                sSc.inc(s.activation(v_r[:], v_r[:], AF.Exp, scale=-0.5))
            SEC("scalar", _)
            vs_rs = sSc.n

            def _(t, vs_rs=vs_rs):
                t.wait_ge(sSc.h, vs_rs)
                t.matmul(PBC[:, 512:768], onesf[:], m_r[:], start=True,
                         stop=True)
                sPE.inc(t.matmul(PBC[:, 768:1024], onesf[:], v_r[:],
                                 start=True, stop=True))
            SEC("tensor", _)
            pe_bcst = sPE.n

            def _(v, pe_bcst=pe_bcst):
                v.wait_ge(sPE.h, pe_bcst)
                mb = PBC[:, 512:768].unsqueeze(1).broadcast_to([128, 8, 256])
                rb = PBC[:, 768:1024].unsqueeze(1).broadcast_to([128, 8, 256])
                v.tensor_tensor(buf[:], buf[:], mb, ALU.subtract)
                sV.inc(v.tensor_tensor(buf[:], buf[:], rb, ALU.mult))
            SEC("vector", _)
            return (sV.h, sV.n)

        ln1_done = fm_ln(comb, (sV.h, sc_attn))

        h2ev = None
        for i in range(8):
            def _(t, i=i, ld_h2=ld_h2, h2ev=h2ev):
                t.wait_ge(sH2.h, ld_h2)
                if h2ev is not None:
                    t.wait_ge(sV.h, h2ev)
                tt = i // 4
                kk0 = (i % 4) * 2
                t.transpose(P5[:, 0:128],
                            hid2t[:, tt, kk0 * 128:kk0 * 128 + 128], eyef_t[:])
                sPE.inc(t.transpose(P5[:, 128:256],
                                    hid2t[:, tt, kk0 * 128 + 128:kk0 * 128 + 256],
                                    eyef_t[:]))
            SEC("tensor", _)
            mype = sPE.n

            def _(v, i=i, mype=mype):
                v.wait_ge(sPE.h, mype)
                tt = i // 4
                kk0 = (i % 4) * 2
                v.tensor_copy(hid2fm[:, kk0, tt * 128:(tt + 1) * 128],
                              P5[:, 0:128])
                sV.inc(v.tensor_copy(hid2fm[:, kk0 + 1, tt * 128:(tt + 1) * 128],
                                     P5[:, 128:256]))
            SEC("vector", _)
            h2ev = sV.n
        vs_h2fm = sV.n

        if DBG and c2 == 0:
            def _(g_, ln1_done=ln1_done, vs_h2fm=vs_h2fm):
                g_.wait_ge(*ln1_done)
                g_.wait_ge(sV.h, vs_h2fm)
                g_.dma_start(out=dbg_attn, in_=comb[:].rearrange("p a b -> p (a b)")).then_inc(sG.h, 16)
                g_.dma_start(out=dbg_h2fm, in_=hid2fm[:].rearrange("p a b -> p (a b)")).then_inc(sG.h, 16)
                sG.n += 32
                _dbgmarks.append(sG.n)
            SEC("gpsimd", _)
        def _(v, ln1_done=ln1_done, vs_h2fm=vs_h2fm, c2=c2):
            v.wait_ge(*ln1_done)
            v.wait_ge(sV.h, vs_h2fm)
            if DBG and c2 == 0:
                v.wait_ge(sG.h, _dbgmarks[-1])
            v.tensor_tensor(x1[:], hid2fm[:], comb[:], ALU.add)
            sV.inc(v.tensor_copy(comb[:], x1[:]))
        SEC("vector", _)
        ln2_done = fm_ln(comb, (sV.h, sV.n))

        def _(v, ln2_done=ln2_done):
            v.wait_ge(*ln2_done)
            sV.inc(v.tensor_copy(x2b[:], comb[:]))
        SEC("vector", _)
        vs_x2b = sV.n
        if DBG and c2 == 0:
            def _(g_, vs_x2b=vs_x2b):
                g_.wait_ge(sV.h, vs_x2b)
                g_.dma_start(out=dbg_x1, in_=x1[:].rearrange("p a b -> p (a b)")).then_inc(sG.h, 16)
                g_.dma_start(out=dbg_x2b, in_=x2b[:].rearrange("p a b -> p (a b)")).then_inc(sG.h, 16)
                sG.n += 32
            SEC("gpsimd", _)

        ffn1_evac = []
        for og4 in range(8):
            def _(t, og4=og4, vs_x2b=vs_x2b):
                t.wait_ge(sV.h, vs_x2b)
                if og4 >= 2:
                    t.wait_ge(sSc.h, ffn1_evac[og4 - 2])
                ins = None
                for j in range(4):
                    og = og4 * 4 + j
                    sl = (og4 % 2) * 4 + j
                    for kg in range(8):
                        ins = t.matmul(pmslot(sl),
                                       W1T[:, kg, og * 128:(og + 1) * 128],
                                       x2b[:, kg, :], start=(kg == 0),
                                       stop=(kg == 7))
                sPE.inc(ins)
            SEC("tensor", _)
            mype = sPE.n

            def _(s, og4=og4, mype=mype):
                s.wait_ge(sPE.h, mype)
                ins = None
                for j in range(4):
                    og = og4 * 4 + j
                    sl = (og4 % 2) * 4 + j
                    ins = s.activation(h1b[:, og, :], pmslot(sl), AF.Gelu,
                                       bias=b1t[:, og:og + 1])
                sSc.inc(ins)
            SEC("scalar", _)
            ffn1_evac.append(sSc.n)
        h1_done = ffn1_evac[-1]

        for kg in range(32):
            def _(sync, kg=kg, c2=c2):
                if c2 == 0 and kg == 0:
                    sync.wait_ge(sGW2.h, tern_last["w2"])
                if w2_mm:
                    sync.wait_ge(sPE.h, w2_mm[-1])
                sTR.inc(sync.dma_start(out=W2Tp[:, kg % 2, :],
                                       in_=w2_bf[:, kg * 128:(kg + 1) * 128],
                                       transpose=True), 16)
            SEC("sync", _)
            mytr = sTR.n

            def _(t, kg=kg, mytr=mytr, h1_done=h1_done):
                t.wait_ge(sTR.h, mytr)
                if kg == 0:
                    t.wait_ge(sSc.h, h1_done)
                    for q in range(4):
                        t.matmul(PM[q][:, 0:512], onesb[:], zrow[:],
                                 start=True, stop=True)
                ins = None
                for og in range(8):
                    ins = t.matmul(pmslot(og),
                                   W2Tp[:, kg % 2, og * 128:(og + 1) * 128],
                                   h1b[:, kg, :], start=False,
                                   stop=(kg == 31), skip_group_check=True)
                sPE.inc(ins)
                w2_mm.append(sPE.n)
            SEC("tensor", _)
        pe_ffn2 = sPE.n

        def _(v, pe_ffn2=pe_ffn2):
            v.wait_ge(sPE.h, pe_ffn2)
            ins = None
            for og in range(8):
                ins = v.tensor_scalar(comb[:, og, :], pmslot(og),
                                      b2t[:, og:og + 1], None, ALU.add)
            sV.inc(ins)
        SEC("vector", _)
        sc_ffn = sV.n
        ffn2_evac_marks.append(sc_ffn)
        if DBG and c2 == 0:
            def _(g_, sc_ffn=sc_ffn):
                g_.wait_ge(sV.h, sc_ffn)
                g_.dma_start(out=dbg_h1, in_=h1b[:].rearrange("p a b -> p (a b)")).then_inc(sG.h, 16)
                g_.dma_start(out=dbg_ffn, in_=comb[:].rearrange("p a b -> p (a b)")).then_inc(sG.h, 16)
                sG.n += 32
                _dbgmarks.append(sG.n)
            SEC("gpsimd", _)
            _ffn_dump = sG.n
        else:
            _ffn_dump = None

        ln3_done = fm_ln(comb, (sV.h, sc_ffn), gwait=_ffn_dump)

        def _(v, ln3_done=ln3_done):
            v.wait_ge(*ln3_done)
            sV.inc(v.tensor_tensor(hid2fm[:], x1[:], comb[:], ALU.add))
        SEC("vector", _)
        vs_out = sV.n

        oev = None
        for i in range(8):
            def _(t, i=i, vs_out=vs_out, oev=oev):
                t.wait_ge(sV.h, vs_out)
                if oev is not None:
                    t.wait_ge(sV.h, oev)
                tt = i // 4
                kk0 = (i % 4) * 2
                t.transpose(P5[:, 0:128],
                            hid2fm[:, kk0, tt * 128:(tt + 1) * 128], eyef_t[:])
                sPE.inc(t.transpose(P5[:, 128:256],
                                    hid2fm[:, kk0 + 1, tt * 128:(tt + 1) * 128],
                                    eyef_t[:]))
            SEC("tensor", _)
            mype = sPE.n

            def _(v, i=i, mype=mype):
                v.wait_ge(sPE.h, mype)
                tt = i // 4
                kk0 = (i % 4) * 2
                v.tensor_copy(hid2t[:, tt, kk0 * 128:(kk0 + 1) * 128],
                              P5[:, 0:128])
                sV.inc(v.tensor_copy(hid2t[:, tt, (kk0 + 1) * 128:(kk0 + 2) * 128],
                                     P5[:, 128:256]))
            SEC("vector", _)
            oev = sV.n
        vs_otok = sV.n
        prev_combfree = vs_otok

        def _(sync, c2=c2, vs_otok=vs_otok):
            sync.wait_ge(sV.h, vs_otok)
            sST.inc(sync.dma_start(
                out=out_d[c2 * T2C:(c2 + 1) * T2C, :].rearrange(
                    "(a p) d -> p a d", p=128),
                in_=hid2t[:]), 16)
            store_marks.append(sST.n)
        SEC("sync", _)

    def _(sync):
        sync.wait_ge(sST.h, sST.n)
    SEC("sync", _)

    es2.close()
    es.close()
    return nc


_CACHED = {}


def kernel(**inputs):
    if "nc" not in _CACHED:
        _CACHED["nc"] = build()
    nc = _CACHED["nc"]

    hidden = inputs["L0_attn_hidden_states"]
    eyeb = np.eye(128, dtype=ml_dtypes.bfloat16)
    eyef = np.eye(128, dtype=np.float32)
    in_maps = []
    for c in range(8):
        b, tr = c // 2, c % 2
        p = "fast_" if tr == 0 else "slow_"
        m = dict(
            hid=np.ascontiguousarray(hidden[b]),
            hid2=np.ascontiguousarray(hidden[b, tr * HALF:(tr + 1) * HALF, :]),
            inw=inputs[f"L0_attn_{p}in_w"],
            convw=inputs[f"L0_attn_{p}conv_w"],
            convb=inputs[f"L0_attn_{p}conv_b"],
            xw=inputs[f"L0_attn_{p}x_w"],
            dtpw=inputs[f"L0_attn_{p}dtp_w"],
            dtpb=inputs[f"L0_attn_{p}dtp_b"],
            alog=inputs[f"L0_attn_{p}A_log"],
            dvec=inputs[f"L0_attn_{p}D"],
            moutw=inputs[f"L0_attn_{p}mout_w"],
            normw=inputs["L0_attn_norm_w"],
            normb=inputs["L0_attn_norm_b"],
            trackw=np.ascontiguousarray(
                inputs["L0_attn_track_weights"]).reshape(1, 2),
            projw=inputs["L0_attn_proj_w"],
            projb=inputs["L0_attn_proj_b"],
            w1=inputs["L0_ffn_W1"],
            b1=inputs["L0_ffn_b1"],
            w2=inputs["L0_ffn_W2"],
            b2=inputs["L0_ffn_b2"],
            eyeb=eyeb,
            eyef=eyef,
            part=np.array([[tr, 0]], dtype=np.uint32),
            selm=np.repeat(np.eye(32, dtype=ml_dtypes.bfloat16), 128, axis=1),
        )
        in_maps.append(m)

    res = run_bass_kernel_spmd(nc, in_maps, core_ids=list(range(8)))
    out = np.empty((B, L, DM), dtype=np.float32)
    for c in range(8):
        b, tr = c // 2, c % 2
        out[b, tr * HALF:(tr + 1) * HALF, :] = np.asarray(res.results[c]["out"])
    return out
